# version 05
# baseline (speedup 1.0000x reference)
import numpy as np
from contextlib import ExitStack

import concourse.bass as bass
import concourse.mybir as mybir
from concourse.bass_utils import run_bass_kernel_spmd
from concourse.ap import AP as _AP

F32 = mybir.dt.float32
BF16 = mybir.dt.bfloat16
I32 = mybir.dt.int32
AF = mybir.ActivationFunctionType
ALU = mybir.AluOpType
AX = mybir.AxisListType

COMPUTE_ENGINES = ("tensor", "vector", "scalar", "gpsimd")
ALL_ENGINES = COMPUTE_ENGINES + ("sync",)
DMA_QUEUES = ("sync", "gpsimd", "scalar")


class Op:
    __slots__ = ("eng", "build", "reads", "writes", "idx", "is_dma", "group",
                 "waits", "inc_needed", "sem_val")

    def __init__(self, eng, build, reads, writes, is_dma, group=None):
        self.eng = eng
        self.build = build
        self.reads = reads
        self.writes = writes
        self.is_dma = is_dma
        self.group = group
        self.idx = -1
        self.waits = []
        self.inc_needed = False
        self.sem_val = None


class Prog:

    def __init__(self, name="prog"):
        self.nc = bass.Bass("TRN2", target_bir_lowering=False)
        self.es = ExitStack()
        self.ops = {e: [] for e in ALL_ENGINES}
        self.last_write = {}
        self.readers = {}
        self.sb_bytes = 0
        self.psum_banks = 0
        self._finalized = False
        self._uniq = 0

    def dram(self, name, shape, dtype=F32, kind="ExternalInput"):
        key = ("dr", name)
        if key in getattr(self, "_cache", {}):
            return self._cache[key][0]
        ap = self.nc.dram_tensor(name, list(shape), dtype, kind=kind).ap()
        if not hasattr(self, "_cache"):
            self._cache = {}
        self._cache[key] = (ap, list(shape), dtype)
        return ap

    def sbuf(self, name, shape, dtype=F32):
        key = ("sb", name)
        if key in getattr(self, "_cache", {}):
            t, sh, dt = self._cache[key]
            assert list(sh) == list(shape) and dt == dtype, f"realloc {name} mismatch"
            return t
        t = self.es.enter_context(self.nc.sbuf_tensor(name, list(shape), dtype))
        itemsize = 4 if dtype in (F32, I32) else 2
        self.sb_bytes += int(np.prod(shape[1:])) * itemsize
        if not hasattr(self, "_cache"):
            self._cache = {}
        self._cache[key] = (t, list(shape), dtype)
        return t

    def psum(self, name, shape, dtype=F32):
        key = ("ps", name)
        if key in getattr(self, "_cache", {}):
            t, sh, dt = self._cache[key]
            assert list(sh) == list(shape) and dt == dtype, f"realloc {name} mismatch"
            return t
        t = self.es.enter_context(self.nc.psum_tensor(name, list(shape), dtype))
        itemsize = 4 if dtype == F32 else 2
        self.psum_banks += -(-int(np.prod(shape[1:])) * itemsize // 2048)
        if not hasattr(self, "_cache"):
            self._cache = {}
        self._cache[key] = (t, list(shape), dtype)
        return t

    def op(self, eng, build, reads=(), writes=(), group=None):
        o = Op(eng, build, tuple(reads), tuple(writes), is_dma=False, group=group)
        o.idx = len(self.ops[eng])
        self.ops[eng].append(o)
        self._track(o)
        return o

    def dma(self, queue, out_ap, in_ap, reads=(), writes=()):
        def build(e):
            return e.dma_start(out=out_ap, in_=in_ap)
        o = Op(queue, build, tuple(reads), tuple(writes), is_dma=True)
        o.idx = len(self.ops[queue])
        self.ops[queue].append(o)
        self._track(o)
        return o

    def _track(self, o):
        deps = set()
        for r in o.reads:
            w = self.last_write.get(r)
            if w is not None and w is not o:
                deps.add(w)
        for r in o.writes:
            w = self.last_write.get(r)
            if w is not None and w is not o:
                deps.add(w)
            for rd in self.readers.get(r, ()):
                if rd is not o:
                    deps.add(rd)
        for d in deps:
            if d.eng == o.eng and not d.is_dma and not o.is_dma:
                if o.group is not None and d.group == o.group:
                    continue
            o.waits.append(d)
            d.inc_needed = True
        for r in o.reads:
            self.readers.setdefault(r, []).append(o)
        for r in o.writes:
            self.last_write[r] = o
            self.readers[r] = []

    def finalize(self):
        assert not self._finalized
        self._finalized = True
        nc = self.nc
        self.sem = {}
        for e in ALL_ENGINES:
            self.sem[e] = self.es.enter_context(nc.semaphore(f"sem_{e}"))
            self.sem[e + "_dma"] = self.es.enter_context(nc.semaphore(f"dma_{e}")) \
                if e in DMA_QUEUES else None
        for e in ALL_ENGINES:
            rank = 0
            dma_rank = 0
            for o in self.ops[e]:
                if o.is_dma:
                    dma_rank += 1
                    o.sem_val = dma_rank * 16
                elif o.inc_needed:
                    rank += 1
                    o.sem_val = rank
        engmap = {}
        for e in ALL_ENGINES:
            have = {}
            for o in self.ops[e]:
                wl = []
                for d in o.waits:
                    key = (d.eng, d.is_dma)
                    assert d.sem_val is not None
                    if have.get(key, 0) >= d.sem_val:
                        continue
                    have[key] = d.sem_val
                    sem = self.sem[d.eng + "_dma"] if d.is_dma else self.sem[d.eng]
                    wl.append((sem, d.sem_val))
                o.waits = wl
        block = self.es.enter_context(nc.Block())

        def mk(e_name):
            ops = self.ops[e_name]
            sem_c = self.sem[e_name]
            sem_d = self.sem.get(e_name + "_dma")

            def run(eng):
                for o in ops:
                    if o.is_dma and o.sem_val > 16:
                        eng.wait_ge(sem_d, o.sem_val - 16)
                    for (sem, val) in o.waits:
                        eng.wait_ge(sem, val)
                    ins = o.build(eng)
                    if o.is_dma:
                        ins.then_inc(sem_d, 16)
                    elif o.inc_needed:
                        ins.then_inc(sem_c, 1)
            return run

        for e_name, deco in (("sync", block.sync), ("vector", block.vector),
                             ("scalar", block.scalar), ("tensor", block.tensor),
                             ("gpsimd", block.gpsimd)):
            if self.ops[e_name]:
                deco(mk(e_name))
        self.es.close()
        return nc

    def finish(self):
        tails = []
        for q in DMA_QUEUES:
            dmas = [o for o in self.ops[q] if o.is_dma]
            if dmas:
                tails.append(dmas[-1])
        def build(eng):
            return eng.nop()
        o = Op("gpsimd", build, (), (), is_dma=False)
        o.idx = len(self.ops["gpsimd"])
        self.ops["gpsimd"].append(o)
        for d in tails:
            o.waits.append(d)
            d.inc_needed = True

    def t_op(self, eng, method, out, *ins, reads=(), writes=(), **kw):
        if eng == "gpsimd" and method in ("tensor_scalar", "tensor_single_scalar",
                                          "scalar_tensor_tensor", "tensor_reduce",
                                          "reduce_max", "reduce_sum",
                                          "tensor_tensor_scan"):
            eng = "vector"
        def build(e):
            return getattr(e, method)(out, *ins, **kw)
        return self.op(eng, build, reads=reads, writes=writes)

    def mm(self, out, lhsT, rhs, start, stop, reads=(), writes=(), group=None, **kw):
        def build(e):
            return e.matmul(out, lhsT, rhs, start=start, stop=stop, **kw)
        return self.op("tensor", build, reads=reads, writes=writes, group=group)

    def transp(self, out, in_, ident, reads=(), writes=(), **kw):
        def build(e):
            return e.transpose(out, in_, ident, **kw)
        return self.op("tensor", build, reads=reads, writes=writes)

    def uniq(self):
        self._uniq += 1
        return f"_u{self._uniq}"


D = 1024
TFULL = 4096
TH = 2048
KT = D // 128
NB = 4

import ml_dtypes
BF = ml_dtypes.bfloat16


def tkey(t):
    try:
        return t.tensor.name
    except AttributeError:
        return t.name


def keys(tiles):
    return [tkey(t) for t in tiles]


class PsumPool:
    def __init__(self, p, name, n=4, shape=(128, 512), dtype=F32):
        self.tiles = [p.psum(f"{name}{i}", list(shape), dtype) for i in range(n)]
        self.i = 0

    def get(self):
        t = self.tiles[self.i % len(self.tiles)]
        self.i += 1
        return t


class Common:
    def __init__(self, p, npool=4):
        self.p = p
        self.ones_bf = p.sbuf("c_ones_bf", [128, 1], BF16)
        self.ones_f = p.sbuf("c_ones_f", [1, 128], F32)
        p.t_op("vector", "memset", self.ones_bf[:], 1.0, writes=("c_ones_bf",))
        p.t_op("vector", "memset", self.ones_f[:], 1.0, writes=("c_ones_f",))
        self.pool = PsumPool(p, "c_ps", n=npool)
        self._consts = {}

    def const(self, val):
        key = float(val)
        if key not in self._consts:
            i = len(self._consts)
            t = self.p.sbuf(f"c_const{i}", [128, 1], F32)
            self.p.t_op("vector", "memset", t[:], key, writes=(f"c_const{i}",))
            self._consts[key] = (t, f"c_const{i}")
        return self._consts[key]


def load_dmajor(p, name, T, dtype=F32, rows=D):
    d = p.dram(name, [rows, T], dtype)
    tiles = []
    for k in range(rows // 128):
        t = p.sbuf(f"{name}_t{k}", [128, T], dtype)
        p.dma("sync", t[:], d[k * 128:(k + 1) * 128, :], writes=(f"{name}_t{k}",))
        tiles.append(t)
    return tiles


def store_dmajor(p, name, tiles, T, dtype=F32, queue="gpsimd"):
    rows = 128 * len(tiles)
    d = p.dram(name, [rows, T], dtype, kind="ExternalOutput")
    for k, t in enumerate(tiles):
        p.dma(queue, d[k * 128:(k + 1) * 128, :], t[:, 0:T], reads=(tkey(t),))
    return d


def load_w(p, name, din, dout, dtype=BF16):
    d = p.dram(name, [din, dout], dtype)
    tiles = []
    for k in range((din + 127) // 128):
        r = min(128, din - k * 128)
        t = p.sbuf(f"{name}_w{k}", [r, dout], dtype)
        p.dma("sync", t[:], d[k * 128:k * 128 + r, :], writes=(f"{name}_w{k}",))
        tiles.append(t)
    return tiles


def load_rows(p, name, rows, cols=1, dtype=F32):
    d = p.dram(name, [rows, cols], dtype)
    tiles = []
    for k in range((rows + 127) // 128):
        r = min(128, rows - k * 128)
        t = p.sbuf(f"{name}_r{k}", [128, cols], dtype)
        p.dma("sync", t[0:r, :], d[k * 128:k * 128 + r, :], writes=(name,))
        tiles.append(t)
    return tiles


def proj(p, tag, x_tiles, w_tiles, dout, T, com,
         out_dtype=BF16, evac_func=None, bias=None,
         evac_eng="vector", chunk=512, out_aps=None, out_keys=None,
         x_keys=None, w_key=None, bias_key=None, scale=None,
         terms=None, accum_into=None, accum_keys=None, x_off=0):
    M = (dout + 127) // 128
    if terms is None:
        terms = [(w_tiles, x_tiles, w_key or keys(w_tiles),
                  x_keys or keys(x_tiles))]
    else:
        terms = [(w, x, wk or keys(w), xk or keys(x))
                 for (w, x, wk, xk) in terms]
    terms = [(w, x, wk if isinstance(wk, (list, tuple)) else [wk] * len(w), xk)
             for (w, x, wk, xk) in terms]
    made = None
    if out_aps is None:
        made = [p.sbuf(f"{tag}_o{m}", [128, T], out_dtype) for m in range(M)]
        out_aps = [t[:] for t in made]
        out_keys = keys(made)
    nterm = len(terms)
    for m in range(M):
        for c0 in range(0, T, chunk):
            cw = min(chunk, T - c0)
            ps = com.pool.get()
            pk = tkey(ps)
            gid = f"{tag}_{m}_{c0}"
            mrows = min(128, dout - m * 128)
            for ti, (wt, xt, wk, xk) in enumerate(terms):
                nk = len(xt)
                for k in range(nk):
                    kr = wt[k].shape[0]
                    p.mm(ps[0:mrows, 0:cw],
                         wt[k][:, m * 128:m * 128 + mrows],
                         xt[k][0:kr, x_off + c0:x_off + c0 + cw],
                         start=(ti == 0 and k == 0),
                         stop=(ti == nterm - 1 and k == nk - 1),
                         reads=(wk[k], xk[k]), writes=(pk,), group=gid)
            oap = out_aps[m]
            ok = out_keys[m]
            osl = oap[0:mrows, c0:c0 + cw]
            if accum_into is not None:
                at, ak = accum_into[m], accum_keys[m]
                if bias is not None:
                    p.t_op("vector", "scalar_tensor_tensor", osl,
                           ps[0:mrows, 0:cw], bias[m], at[0:mrows, c0:c0 + cw],
                           ALU.add, ALU.add,
                           reads=(pk, ak, bias_key), writes=(ok,))
                else:
                    p.t_op("vector", "tensor_tensor", osl,
                           at[0:mrows, c0:c0 + cw], ps[0:mrows, 0:cw], ALU.add,
                           reads=(pk, ak), writes=(ok,))
            elif evac_func == "bias_add":
                p.t_op("vector", "tensor_scalar", osl,
                       ps[0:mrows, 0:cw], bias[m], None, ALU.add,
                       reads=(pk, bias_key), writes=(ok,))
            elif evac_func is not None:
                kw = {}
                rd = [pk]
                if bias is not None:
                    kw["bias"] = bias[m]
                    rd.append(bias_key)
                if scale is not None:
                    kw["scale"] = scale
                p.t_op("scalar", "activation", osl, ps[0:mrows, 0:cw],
                       evac_func, reads=tuple(rd), writes=(ok,), **kw)
            else:
                p.t_op(evac_eng, "tensor_copy", osl, ps[0:mrows, 0:cw],
                       reads=(pk,), writes=(ok,))
    return made


def bcast_row(p, tag, row_ap, row_key, T, com, out=None, dtype=F32):
    if out is None:
        out = p.sbuf(f"{tag}_bc", [128, T], dtype)
    ok = tkey(out)
    for c0 in range(0, T, 512):
        cw = min(512, T - c0)
        ps = com.pool.get()
        pk = tkey(ps)
        p.mm(ps[:, 0:cw], com.ones_f[:], row_ap[:, c0:c0 + cw], start=True,
             stop=True, reads=("c_ones_f", row_key), writes=(pk,))
        p.t_op("vector", "tensor_copy", out[:, c0:c0 + cw], ps[:, 0:cw],
               reads=(pk,), writes=(ok,))
    return out


def ln_d(p, tag, x_tiles, T, com, g=None, b=None, gb_key=None,
         out_dtype=BF16, eps=1e-5, x_keys=None, scratch="lnws", in_bf=None,
         precise=False):
    nk = len(x_tiles)
    Dn = 128 * nk
    x_keys = x_keys or keys(x_tiles)
    if precise:
        if not hasattr(com, "ones_f32c"):
            com.ones_f32c = p.sbuf("c_ones_f32c", [128, 1], F32)
            p.t_op("vector", "memset", com.ones_f32c[:], 1.0, writes=("c_ones_f32c",))
        xb_aps = [t[:, 0:T] for t in x_tiles]
        xb_keys = x_keys
        ones_stat = com.ones_f32c
        ones_key = "c_ones_f32c"
        sqd = F32
    elif in_bf is None:
        xb = [p.sbuf(f"{scratch}_xb{k}", [128, T], BF16) for k in range(nk)]
        for k in range(nk):
            p.t_op("vector", "tensor_copy", xb[k][:], x_tiles[k][:, 0:T],
                   reads=(x_keys[k],), writes=(tkey(xb[k]),))
        xb_keys = [tkey(t) for t in xb]
        xb_aps = [t[:] for t in xb]
    else:
        xb_aps = [t[:, 0:T] for t in x_tiles]
        xb_keys = x_keys
    if not precise:
        ones_stat = com.ones_bf
        ones_key = "c_ones_bf"
        sqd = BF16
    com.ps_s = p.psum("c_pss", [1, 512], F32)
    com.ps_q = p.psum("c_psq", [1, 512], F32)
    sq = p.sbuf(f"{scratch}_sq{'p' if precise else ''}", [128, 512], sqd)
    sums = p.sbuf(f"{scratch}_sums", [1, T], F32)
    sumq = p.sbuf(f"{scratch}_sumq", [1, T], F32)
    k_sums, k_sumq = tkey(sums), tkey(sumq)
    for c0 in range(0, T, 512):
        cw = min(512, T - c0)
        for k in range(nk):
            p.mm(com.ps_s[:, 0:cw], ones_stat[:], xb_aps[k][:, c0:c0 + cw],
                 start=(k == 0), stop=(k == nk - 1),
                 reads=(ones_key, xb_keys[k]), writes=(tkey(com.ps_s),),
                 group=f"{tag}s{c0}")
        for k in range(nk):
            p.t_op("scalar", "activation", sq[:, 0:cw], xb_aps[k][:, c0:c0 + cw],
                   AF.Square, reads=(xb_keys[k],), writes=(tkey(sq),))
            p.mm(com.ps_q[:, 0:cw], ones_stat[:], sq[:, 0:cw],
                 start=(k == 0), stop=(k == nk - 1),
                 reads=(ones_key, tkey(sq)), writes=(tkey(com.ps_q),),
                 group=f"{tag}q{c0}")
        p.t_op("vector", "tensor_copy", sums[:, c0:c0 + cw], com.ps_s[:, 0:cw],
               reads=(tkey(com.ps_s),), writes=(k_sums,))
        p.t_op("vector", "tensor_copy", sumq[:, c0:c0 + cw], com.ps_q[:, 0:cw],
               reads=(tkey(com.ps_q),), writes=(k_sumq,))
    mean = p.sbuf(f"{scratch}_mean", [1, T], F32)
    rstd = p.sbuf(f"{scratch}_rstd", [1, T], F32)
    tmp = p.sbuf(f"{scratch}_tmpr", [1, T], F32)
    km, kr, kt2 = tkey(mean), tkey(rstd), tkey(tmp)
    p.t_op("vector", "tensor_scalar", mean[:], sums[:], 1.0 / Dn, None,
           ALU.mult, reads=(k_sums,), writes=(km,))
    p.t_op("scalar", "activation", tmp[:], mean[:], AF.Square,
           reads=(km,), writes=(kt2,))
    p.t_op("vector", "scalar_tensor_tensor", rstd[:], sumq[:], 1.0 / Dn, tmp[:],
           ALU.mult, ALU.subtract, reads=(k_sumq, kt2), writes=(kr,))
    epst, epsk = com.const(float(eps))
    p.t_op("scalar", "activation", tmp[:], rstd[:], AF.Ln, bias=epst[0:1, :],
           reads=(kr, epsk), writes=(kt2,))
    p.t_op("scalar", "activation", rstd[:], tmp[:], AF.Exp, scale=-0.5,
           reads=(kt2,), writes=(kr,))
    mb = p.sbuf(f"{scratch}_mb", [128, T], F32)
    rb = p.sbuf(f"{scratch}_rb", [128, T], F32)
    kmb, krb = tkey(mb), tkey(rb)
    for (row, rk, dst, dk) in ((mean, km, mb, kmb), (rstd, kr, rb, krb)):
        for c0 in range(0, T, 512):
            cw = min(512, T - c0)
            ps = com.pool.get()
            pk = tkey(ps)
            p.mm(ps[:, 0:cw], com.ones_f[:], row[:, c0:c0 + cw], start=True,
                 stop=True, reads=("c_ones_f", rk), writes=(pk,))
            p.t_op("vector", "tensor_copy", dst[:, c0:c0 + cw], ps[:, 0:cw],
                   reads=(pk,), writes=(dk,))
    out_tiles = [p.sbuf(f"{tag}_n{k}", [128, T], out_dtype) for k in range(nk)]
    engs = ["vector", "gpsimd"]
    for k in range(nk):
        e = engs[k % 2]
        ok = tkey(out_tiles[k])
        p.t_op(e, "tensor_sub", out_tiles[k][:], x_tiles[k][:, 0:T], mb[:],
               reads=(x_keys[k], kmb), writes=(ok,))
        p.t_op(e, "tensor_mul", out_tiles[k][:], out_tiles[k][:], rb[:],
               reads=(ok, krb), writes=(ok,))
        if g is not None:
            p.t_op(e, "tensor_scalar", out_tiles[k][:], out_tiles[k][:],
                   g[k], b[k], ALU.mult, ALU.add,
                   reads=(ok, gb_key), writes=(ok,))
    return out_tiles


LAUNCHES = []


def as_bf(a):
    return np.ascontiguousarray(a.astype(BF))


def as_f32(a):
    return np.ascontiguousarray(a.astype(np.float32))


def xshard(x, c, halo=0):
    b, h = c // 2, c % 2
    t0 = h * TH - halo
    lo = max(t0, 0)
    sl = x[b, lo:(h + 1) * TH, :].T
    if t0 < 0:
        pad = np.zeros((D, -t0), np.float32)
        sl = np.concatenate([pad, sl], axis=1)
    return np.ascontiguousarray(sl.astype(np.float32))


def unshard_x(results, key="xo"):
    x = np.zeros((NB, TFULL, D), np.float32)
    for c in range(8):
        b, h = c // 2, c % 2
        x[b, h * TH:(h + 1) * TH, :] = results[c][key].T
    return x


def build_l1f_m():
    p = Prog()
    com = Common(p)
    wrt = load_w(p, "wrt", D, D)
    wit = load_w(p, "wit", D, D)
    gr = load_dmajor(p, "gr", D, dtype=BF16)
    gi = load_dmajor(p, "gi", D, dtype=BF16)
    ngi = load_dmajor(p, "ngi", D, dtype=BF16)
    fr = load_w(p, "fr", D, 128)
    nfi = load_w(p, "nfi", D, 128)
    t1 = proj(p, "t1", None, None, D, D, com,
              terms=[(wrt, gr, None, None), (wit, ngi, None, None)])
    t2 = proj(p, "t2", None, None, D, D, com,
              terms=[(wrt, gi, None, None), (wit, gr, None, None)])
    mcol = proj(p, "mcol", None, None, 128, D, com,
                terms=[(fr, t1, None, None), (nfi, t2, None, None)])
    store_dmajor(p, "mo", mcol, D, dtype=BF16)
    p.finish()
    return p.finalize()


def host_l1f_m(state):
    inp = state["inp"]
    Fmat = state["consts"]["F"]
    G = state["consts"]["G"]
    wr = inp["L1_ffn_w_freq_real"]
    wi = inp["L1_ffn_w_freq_imag"]
    maps = []
    for c in range(8):
        maps.append({
            "wrt": as_bf(wr.T), "wit": as_bf(wi.T),
            "gr": as_bf(G.real), "gi": as_bf(G.imag), "ngi": as_bf(-G.imag),
            "fr": as_bf(Fmat.real[:, c * 128:(c + 1) * 128]),
            "nfi": as_bf(-Fmat.imag[:, c * 128:(c + 1) * 128]),
        })
    return maps


def post_l1f_m(state, results):
    M = np.zeros((D, D), BF)
    for c in range(8):
        M[c * 128:(c + 1) * 128, :] = results[c]["mo"].astype(BF)
    state["l1f_M"] = M


def build_l1f():
    p = Prog()
    com = Common(p)
    xt = load_dmajor(p, "x", TH)
    mt = load_w(p, "m", D, D)
    bias = load_rows(p, "bias", D)
    xn = ln_d(p, "ln", xt, TH, com)
    xkeys = [f"x_t{k}" for k in range(KT)]
    proj(p, "y", xn, mt, D, TH, com,
         out_aps=[t[:] for t in xt], out_keys=xkeys,
         bias=[t[:] for t in bias], bias_key="bias",
         accum_into=xt, accum_keys=xkeys)
    d_out = p.dram("xo", [D, TH], F32, kind="ExternalOutput")
    for k in range(KT):
        p.dma("gpsimd", d_out[k * 128:(k + 1) * 128, :], xt[k][:], reads=(xkeys[k],))
    p.finish()
    return p.finalize()


def host_l1f(state):
    inp = state["inp"]
    M = state["l1f_M"]
    b = inp["L1_ffn_bias"].reshape(D, 1)
    return [{"x": xshard(state["x"], c), "m": np.ascontiguousarray(M),
             "bias": as_f32(b)} for c in range(8)]


def post_l1f(state, results):
    state["x"] = unshard_x(results)


def transpose_128(p, tag, src_ap, src_key, dst_ap, dst_key, ident, com):
    ps = p.psum("trps", [128, 128], BF16)
    p.transp(ps[:], src_ap, ident[:], reads=(src_key, "ident"), writes=(tkey(ps),))
    p.t_op("vector", "tensor_copy", dst_ap, ps[:], reads=(tkey(ps),), writes=(dst_key,))


def load_ident(p):
    d = p.dram("ident", [128, 128], BF16)
    t = p.sbuf("ident_t", [128, 128], BF16)
    p.dma("sync", t[:], d[:, :], writes=("ident",))
    return t


def round_ints(p, tag, src_ap, src_key, dst_ap, dst_key, itile, mtile,
               eng="vector"):
    ik, mk = tkey(itile), tkey(mtile)
    sh = list(src_ap.shape)
    ct = p.sbuf(f"ri_{tag}_ct", sh, F32)
    ck = tkey(ct)
    p.t_op(eng, "tensor_copy", itile, src_ap, reads=(src_key,), writes=(ik,))
    p.t_op(eng, "tensor_copy", mtile, itile, reads=(ik,), writes=(mk,))
    p.t_op(eng, "tensor_sub", ct[:], src_ap, mtile, reads=(src_key, mk),
           writes=(ck,))
    p.t_op(eng, "tensor_scalar", dst_ap, ct[:], -0.5, None, ALU.is_le,
           reads=(ck,), writes=(dst_key,))
    p.t_op(eng, "tensor_scalar", ct[:], ct[:], 0.5, None, ALU.is_ge,
           reads=(ck,), writes=(ck,))
    p.t_op(eng, "tensor_sub", ct[:], ct[:], dst_ap, reads=(ck, dst_key),
           writes=(ck,))
    p.t_op(eng, "tensor_add", dst_ap, mtile, ct[:], reads=(mk, ck),
           writes=(dst_key,))


L3I = 2816


def build_l3f_w():
    p = Prog()
    com = Common(p)
    ident = load_ident(p)
    outs = {}
    for (nm, nt, cols) in (("wg", 3, 1024), ("wu", 3, 1024), ("wd", 1, 2816)):
        w = load_dmajor(p, f"{nm}_in", cols, rows=128 * nt)
        scales = p.sbuf(f"{nm}_scales", [128, nt], F32)
        it = p.sbuf(f"{nm}_i32", [128, cols], I32)
        qt = [p.sbuf(f"{nm}_q{j}", [128, cols], BF16) for j in range(nt)]
        rtmp = p.sbuf(f"{nm}_rt", [128, cols], F32)
        for j in range(nt):
            wk = f"{nm}_in_t{j}"
            amax = p.sbuf(f"{nm}_am{j}", [128, 1], F32)
            p.t_op("vector", "reduce_max", amax[:], w[j][:], AX.X,
                   apply_absolute_value=True, reads=(wk,), writes=(tkey(amax),))
            p.t_op("vector", "tensor_scalar", scales[:, j:j + 1], amax[:],
                   1.0 / 127.0, None, ALU.mult, reads=(tkey(amax),),
                   writes=(tkey(scales),))
            rsc = p.sbuf(f"{nm}_rs{j}", [128, 1], F32)
            p.t_op("vector", "reciprocal", rsc[:], amax[:], reads=(tkey(amax),),
                   writes=(tkey(rsc),))
            p.t_op("vector", "tensor_scalar", rtmp[:], w[j][:], rsc[:], 127.0,
                   ALU.mult, ALU.mult, reads=(wk, tkey(rsc)), writes=(tkey(rtmp),))
            mt_ = p.sbuf(f"{nm}_mt", [128, cols], F32)
            round_ints(p, f"{nm}{j}", rtmp[:], tkey(rtmp), rtmp[:], tkey(rtmp),
                       it[:], mt_[:])
            p.t_op("vector", "tensor_copy", qt[j][:], rtmp[:],
                   reads=(tkey(rtmp),), writes=(tkey(qt[j]),))
        od = p.dram(f"{nm}_qT", [cols, 128 * nt], BF16, kind="ExternalOutput")
        for j in range(nt):
            for ct in range(cols // 128):
                tt = p.sbuf(f"{nm}_tt", [128, 128], BF16)
                transpose_128(p, f"{nm}tr", qt[j][:, ct * 128:(ct + 1) * 128],
                              tkey(qt[j]), tt[:], tkey(tt), ident, com)
                p.dma("gpsimd", od[ct * 128:(ct + 1) * 128, j * 128:(j + 1) * 128],
                      tt[:], reads=(tkey(tt),))
        sd = p.dram(f"{nm}_sc_o", [128, nt], F32, kind="ExternalOutput")
        p.dma("gpsimd", sd[:, :], scales[:], reads=(tkey(scales),))
    p.finish()
    return p.finalize()


IDENT = None


def get_ident():
    global IDENT
    if IDENT is None:
        IDENT = np.eye(128).astype(BF)
    return IDENT


def host_l3f_w(state):
    inp = state["inp"]
    maps = []
    wg, wu, wd = (inp["L3_ffn_w_gate"], inp["L3_ffn_w_up"], inp["L3_ffn_w_down"])
    for c in range(8):
        m = {}
        for nm, w, nt in (("wg", wg, 3), ("wu", wu, 3), ("wd", wd, 1)):
            r0 = c * nt * 128
            rows = np.zeros((nt * 128, w.shape[1]), np.float32)
            r1 = min(r0 + nt * 128, w.shape[0])
            if r0 < w.shape[0]:
                rows[0:r1 - r0] = w[r0:r1]
            else:
                rows[:, 0] = 1.0
            if r1 - r0 < nt * 128 and r0 < w.shape[0]:
                rows[r1 - r0:, 0] = 1.0
            m[f"{nm}_in"] = as_f32(rows)
        m["ident"] = get_ident()
        maps.append(m)
    return maps


def post_l3f_w(state, results):
    out = {}
    for nm, rows, cols, nt in (("wg", L3I, 1024, 3), ("wu", L3I, 1024, 3),
                               ("wd", 1024, 2816, 1)):
        qT = np.zeros((cols, rows), BF)
        sc = np.zeros((rows,), np.float32)
        for c in range(8):
            r0 = c * nt * 128
            r1 = min(r0 + nt * 128, rows)
            if r0 >= rows:
                continue
            qT[:, r0:r1] = results[c][f"{nm}_qT"][:, 0:r1 - r0]
            sc[r0:r1] = results[c][f"{nm}_sc_o"].T.reshape(-1)[0:r1 - r0]
        out[nm + "_qT"] = qT
        out[nm + "_sc"] = sc
    state["l3f_w"] = out


def global_max_rows(p, tag, tiles, tile_keys, T, com):
    am = p.sbuf(f"{tag}_am", [128, 1], F32)
    amk = tkey(am)
    for i, t in enumerate(tiles):
        tmp = p.sbuf(f"{tag}_amt", [128, 1], F32)
        p.t_op("vector", "reduce_max", tmp[:], t[:, 0:T], AX.X,
               apply_absolute_value=True, reads=(tile_keys[i],), writes=(tkey(tmp),))
        if i == 0:
            p.t_op("vector", "tensor_copy", am[:], tmp[:], reads=(tkey(tmp),),
                   writes=(amk,))
        else:
            p.t_op("vector", "tensor_max", am[:], am[:], tmp[:],
                   reads=(amk, tkey(tmp)), writes=(amk,))
    return finish_max128(p, tag, am, amk)


def finish_max128(p, tag, am, amk):
    dscr = p.dram(f"{tag}_scr", [128, 1], F32, kind="Internal")
    p.dma("gpsimd", dscr[:, :], am[:], reads=(amk,))
    row = p.sbuf(f"{tag}_row", [1, 128], F32)
    p.dma("gpsimd", row[:], dscr[:, :].rearrange("p one -> one p"), writes=(tkey(row),))
    out = p.sbuf(f"{tag}_gm", [1, 1], F32)
    p.t_op("vector", "reduce_max", out[:], row[:], AX.X,
           reads=(tkey(row),), writes=(tkey(out),))
    return out


def bcast_scalar(p, tag, val_ap, val_key, com):
    ps = com.pool.get()
    out = p.sbuf(f"{tag}_bs", [128, 1], F32)
    p.mm(ps[:, 0:1], com.ones_f[:], val_ap, start=True, stop=True,
         reads=("c_ones_f", val_key), writes=(tkey(ps),))
    p.t_op("vector", "tensor_copy", out[:], ps[:, 0:1], reads=(tkey(ps),),
           writes=(tkey(out),))
    return out


def build_l3f_a():
    p = Prog()
    com = Common(p)
    TQ = 1024
    x_d = p.dram("x", [D, TH], F32)
    xt = [p.sbuf(f"xh{k}", [128, TQ], F32) for k in range(KT)]
    am = p.sbuf("am_all", [128, 1], F32)
    p.t_op("vector", "memset", am[:], 0.0, writes=("am_all",))
    for h0 in (0, TQ):
        for k in range(KT):
            p.dma("sync", xt[k][:], x_d[k * 128:(k + 1) * 128, h0:h0 + TQ],
                  writes=(f"xh{k}",))
        xn = ln_d(p, "ln", xt, TQ, com, out_dtype=F32, precise=True,
                  x_keys=[f"xh{k}" for k in range(KT)])
        for i, t in enumerate(xn):
            tmp = p.sbuf("am_t", [128, 1], F32)
            p.t_op("vector", "reduce_max", tmp[:], t[:], AX.X,
                   apply_absolute_value=True, reads=(tkey(t),), writes=("am_t",))
            p.t_op("vector", "tensor_max", am[:], am[:], tmp[:],
                   reads=("am_all", "am_t"), writes=("am_all",))
    gm = finish_max128(p, "gm", am, "am_all")
    d = p.dram("gmo", [1, 1], F32, kind="ExternalOutput")
    p.dma("gpsimd", d[:, :], gm[:], reads=(tkey(gm),))
    p.finish()
    return p.finalize()


def host_l3f_a(state):
    return [{"x": xshard(state["x"], c)} for c in range(8)]


def post_l3f_a(state, results):
    state["l3f_gmax8"] = np.concatenate(
        [results[c]["gmo"].reshape(1, 1) for c in range(8)], axis=1)


def quant_acts(p, tag, xn, T, com, gmax8_name, clip_name, out_dtype=BF16, out_tiles=None):
    g8 = p.sbuf(f"{tag}_g8", [1, 8], F32)
    d = p.dram(gmax8_name, [1, 8], F32)
    p.dma("sync", g8[:], d[:, :], writes=(tkey(g8),))
    cl = p.sbuf(f"{tag}_cl", [1, 1], F32)
    dc = p.dram(clip_name, [1, 1], F32)
    p.dma("sync", cl[:], dc[:, :], writes=(tkey(cl),))
    gm = p.sbuf(f"{tag}_gm", [1, 1], F32)
    p.t_op("vector", "reduce_max", gm[:], g8[:], AX.X, reads=(tkey(g8),),
           writes=(tkey(gm),))
    sig = p.sbuf(f"{tag}_sig", [1, 1], F32)
    p.t_op("scalar", "activation", sig[:], cl[:], AF.Sigmoid, reads=(tkey(cl),),
           writes=(tkey(sig),))
    sc = p.sbuf(f"{tag}_sc", [1, 1], F32)
    p.t_op("vector", "tensor_tensor", sc[:], sig[:], gm[:], ALU.mult,
           reads=(tkey(sig), tkey(gm)), writes=(tkey(sc),))
    p.t_op("vector", "tensor_scalar", sc[:], sc[:], 1.0 / 127.0, None, ALU.mult,
           reads=(tkey(sc),), writes=(tkey(sc),))
    rsc = p.sbuf(f"{tag}_rsc", [1, 1], F32)
    p.t_op("vector", "reciprocal", rsc[:], sc[:], reads=(tkey(sc),),
           writes=(tkey(rsc),))
    rb = bcast_scalar(p, f"{tag}r", rsc[:], tkey(rsc), com)
    if out_tiles is None:
        qt = [p.sbuf(f"{tag}_q{k}", [128, T], out_dtype) for k in range(len(xn))]
    else:
        qt = out_tiles
    it = p.sbuf(f"{tag}_i32", [128, 512], I32)
    mt = p.sbuf(f"{tag}_mt", [128, 512], F32)
    rt = p.sbuf(f"{tag}_rt", [128, 512], F32)
    for k in range(len(xn)):
        for c0 in range(0, T, 512):
            s = slice(c0, c0 + 512)
            p.t_op("vector", "tensor_scalar", rt[:], xn[k][:, s], rb[:], None,
                   ALU.mult, reads=(tkey(xn[k]), tkey(rb)), writes=(tkey(rt),))
            round_ints(p, f"{tag}{k}", rt[:], tkey(rt), rt[:], tkey(rt), it[:], mt[:])
            p.t_op("vector", "tensor_scalar", rt[:], rt[:], 127.0, -128.0,
                   ALU.min, ALU.max, reads=(tkey(rt),), writes=(tkey(rt),))
            p.t_op("vector", "tensor_copy", qt[k][:, s], rt[:], reads=(tkey(rt),),
                   writes=(tkey(qt[k]),))
    return qt, sc


def build_l3f_b():
    p = Prog()
    com = Common(p)
    TQ = 512
    x_d = p.dram("x", [D, TH], F32)
    wg = load_w(p, "wg", D, L3I)
    wu = load_w(p, "wu", D, L3I)
    gsc = load_rows(p, "gsc", L3I)
    usc = load_rows(p, "usc", L3I)
    hd = p.dram("h", [L3I, TH], F32, kind="ExternalOutput")
    hmax = p.sbuf("hmax", [128, 1], F32)
    p.t_op("vector", "memset", hmax[:], 0.0, writes=("hmax",))
    gbuf = p.sbuf("gbuf", [128, 512], F32)
    ubuf = p.sbuf("ubuf", [128, 512], F32)
    hbuf = p.sbuf("hbuf", [128, 512], F32)
    hm_t = p.sbuf("hm_t", [128, 1], F32)
    MT = L3I // 128
    xt = [p.sbuf(f"xh{k}", [128, TQ], F32) for k in range(KT)]
    smg = [p.sbuf(f"sg{m}", [128, 1], F32) for m in range(MT)]
    smu = [p.sbuf(f"su{m}", [128, 1], F32) for m in range(MT)]
    first = True
    for h0 in (0, 512, 1024, 1536):
        for k in range(KT):
            p.dma("sync", xt[k][:], x_d[k * 128:(k + 1) * 128, h0:h0 + TQ],
                  writes=(f"xh{k}",))
        xn = ln_d(p, "ln", xt, TQ, com, out_dtype=F32, precise=True,
                  x_keys=[f"xh{k}" for k in range(KT)])
        qt, scx = quant_acts(p, "qa", xn, TQ, com, "gmax8", "clipx")
        if first:
            sxb = bcast_scalar(p, "sx", scx[:], tkey(scx), com)
            for m in range(MT):
                p.t_op("vector", "tensor_scalar", smg[m][:], gsc[m][:], sxb[:],
                       None, ALU.mult, reads=("gsc", tkey(sxb)),
                       writes=(tkey(smg[m]),))
                p.t_op("vector", "tensor_scalar", smu[m][:], usc[m][:], sxb[:],
                       None, ALU.mult, reads=("usc", tkey(sxb)),
                       writes=(tkey(smu[m]),))
            first = False
        for m in range(MT):
            for c0 in range(0, TQ, 512):
                psg = com.pool.get()
                psu = com.pool.get()
                for k in range(KT):
                    p.mm(psg[:, :], wg[k][:, m * 128:(m + 1) * 128],
                         qt[k][:, c0:c0 + 512], start=(k == 0),
                         stop=(k == KT - 1), reads=(f"wg_w{k}", tkey(qt[k])),
                         writes=(tkey(psg),), group=f"g{h0}_{m}_{c0}")
                for k in range(KT):
                    p.mm(psu[:, :], wu[k][:, m * 128:(m + 1) * 128],
                         qt[k][:, c0:c0 + 512], start=(k == 0),
                         stop=(k == KT - 1), reads=(f"wu_w{k}", tkey(qt[k])),
                         writes=(tkey(psu),), group=f"u{h0}_{m}_{c0}")
                p.t_op("scalar", "activation", gbuf[:], psg[:, :], AF.Silu,
                       scale=smg[m][:], reads=(tkey(psg), tkey(smg[m])),
                       writes=("gbuf",))
                p.t_op("scalar", "activation", ubuf[:], psu[:, :], AF.Copy,
                       scale=smu[m][:], reads=(tkey(psu), tkey(smu[m])),
                       writes=("ubuf",))
                p.t_op("vector", "tensor_mul", hbuf[:], gbuf[:], ubuf[:],
                       reads=("gbuf", "ubuf"), writes=("hbuf",))
                p.t_op("vector", "reduce_max", hm_t[:], hbuf[:], AX.X,
                       apply_absolute_value=True, reads=("hbuf",),
                       writes=("hm_t",))
                p.t_op("vector", "tensor_max", hmax[:], hmax[:], hm_t[:],
                       reads=("hmax", "hm_t"), writes=("hmax",))
                p.dma("gpsimd", hd[m * 128:(m + 1) * 128, h0 + c0:h0 + c0 + 512],
                      hbuf[:], reads=("hbuf",))
    gm = finish_max128(p, "hgm", hmax, "hmax")
    d = p.dram("hmaxo", [1, 1], F32, kind="ExternalOutput")
    p.dma("gpsimd", d[:, :], gm[:], reads=(tkey(gm),))
    p.finish()
    return p.finalize()


def host_l3f_b(state, app):
    inp = state["inp"]
    w = state["l3f_w"]
    maps = []
    for c in range(8):
        maps.append({
            "x": xshard(state["x"], c),
            "wg": as_bf(w["wg_qT"]), "wu": as_bf(w["wu_qT"]),
            "gsc": as_f32(w["wg_sc"].reshape(-1, 1)),
            "usc": as_f32(w["wu_sc"].reshape(-1, 1)),
            "gmax8": as_f32(state["l3f_gmax8"]),
            "clipx": as_f32(inp["L3_ffn_clip_x"].reshape(1, 1)),
        })
    return maps


def post_l3f_b(state, results):
    state["l3f_h"] = [results[c]["h"] for c in range(8)]
    state["l3f_hmax8"] = np.concatenate(
        [results[c]["hmaxo"].reshape(1, 1) for c in range(8)], axis=1)


def build_l3f_c():
    p = Prog()
    com = Common(p)
    TQ = 1024
    x_d = p.dram("x", [D, TH], F32)
    wd = load_w(p, "wd", L3I, D)
    dsc = load_rows(p, "dsc", D)
    MT = L3I // 128
    hq = [p.sbuf(f"hq{j}", [128, TQ], BF16) for j in range(MT)]
    hbuf = p.sbuf("hbuf", [128, TQ], F32)
    hd = p.dram("h", [L3I, TH], F32)
    g8 = p.sbuf("g8", [1, 8], F32)
    p.dma("sync", g8[:], p.dram("hmax8", [1, 8], F32)[:, :], writes=("g8",))
    cl = p.sbuf("cl", [1, 1], F32)
    p.dma("sync", cl[:], p.dram("clipd", [1, 1], F32)[:, :], writes=("cl",))
    gm = p.sbuf("gmv", [1, 1], F32)
    p.t_op("vector", "reduce_max", gm[:], g8[:], AX.X, reads=("g8",), writes=("gmv",))
    sig = p.sbuf("sig", [1, 1], F32)
    p.t_op("scalar", "activation", sig[:], cl[:], AF.Sigmoid, reads=("cl",), writes=("sig",))
    sc = p.sbuf("sc", [1, 1], F32)
    p.t_op("vector", "tensor_tensor", sc[:], sig[:], gm[:], ALU.mult,
           reads=("sig", "gmv"), writes=("sc",))
    p.t_op("vector", "tensor_scalar", sc[:], sc[:], 1.0 / 127.0, None, ALU.mult,
           reads=("sc",), writes=("sc",))
    rsc = p.sbuf("rsc", [1, 1], F32)
    p.t_op("vector", "reciprocal", rsc[:], sc[:], reads=("sc",), writes=("rsc",))
    rb = bcast_scalar(p, "rb", rsc[:], "rsc", com)
    scb = bcast_scalar(p, "scb", sc[:], "sc", com)
    it = p.sbuf("i32", [128, 512], I32)
    mt_ = p.sbuf("mtt", [128, 512], F32)
    mt2 = p.sbuf("mtt2", [128, 512], F32)
    xt = [p.sbuf(f"xr{k}", [128, TQ], F32) for k in range(KT)]
    smd = [p.sbuf(f"sm{m}", [128, 1], F32) for m in range(KT)]
    for m in range(KT):
        p.t_op("vector", "tensor_scalar", smd[m][:], dsc[m][:], scb[:], None,
               ALU.mult, reads=("dsc", tkey(scb)), writes=(tkey(smd[m]),))
    d_out = p.dram("xo", [D, TH], F32, kind="ExternalOutput")
    for h0 in (0, TQ):
        for j in range(MT):
            p.dma("sync", hbuf[:], hd[j * 128:(j + 1) * 128, h0:h0 + TQ],
                  writes=("hbuf",))
            for c0 in range(0, TQ, 512):
                s = slice(c0, c0 + 512)
                p.t_op("vector", "tensor_scalar", mt_[:], hbuf[:, s], rb[:],
                       None, ALU.mult, reads=("hbuf", tkey(rb)), writes=("mtq",))
                round_ints(p, f"r{j}", mt_[:], "mtq", mt_[:], "mtq", it[:], mt2[:])
                p.t_op("vector", "tensor_scalar", mt_[:], mt_[:], 127.0, -128.0,
                       ALU.min, ALU.max, reads=("mtq",), writes=("mtq",))
                p.t_op("vector", "tensor_copy", hq[j][:, s], mt_[:],
                       reads=("mtq",), writes=(tkey(hq[j]),))
        for k in range(KT):
            p.dma("sync", xt[k][:], x_d[k * 128:(k + 1) * 128, h0:h0 + TQ],
                  writes=(f"xr{k}",))
        for m in range(KT):
            for c0 in range(0, TQ, 512):
                ps = com.pool.get()
                for j in range(MT):
                    p.mm(ps[:, :], wd[j][:, m * 128:(m + 1) * 128],
                         hq[j][:, c0:c0 + 512], start=(j == 0),
                         stop=(j == MT - 1), reads=(f"wd_w{j}", tkey(hq[j])),
                         writes=(tkey(ps),), group=f"d{h0}_{m}_{c0}")
                p.t_op("vector", "scalar_tensor_tensor", xt[m][:, c0:c0 + 512],
                       ps[:, :], smd[m][:], xt[m][:, c0:c0 + 512], ALU.mult,
                       ALU.add, reads=(tkey(ps), tkey(smd[m]), f"xr{m}"),
                       writes=(f"xr{m}",))
            p.dma("gpsimd", d_out[m * 128:(m + 1) * 128, h0:h0 + TQ],
                  xt[m][:], reads=(f"xr{m}",))
    p.finish()
    return p.finalize()


def host_l3f_c(state):
    inp = state["inp"]
    w = state["l3f_w"]
    maps = []
    for c in range(8):
        maps.append({
            "x": xshard(state["x"], c),
            "h": as_f32(state["l3f_h"][c]),
            "wd": as_bf(w["wd_qT"]),
            "dsc": as_f32(w["wd_sc"].reshape(-1, 1)),
            "hmax8": as_f32(state["l3f_hmax8"]),
            "clipd": as_f32(inp["L3_ffn_clip_down"].reshape(1, 1)),
        })
    return maps


def post_l3f_c(state, results):
    state["x"] = unshard_x(results)


def build_l2aa_kv():
    p = Prog()
    com = Common(p)
    TQ = 1024
    x_d = p.dram("x", [D, TH], BF16)
    wq = load_w(p, "wq", D, D)
    wk = load_w(p, "wk", D, D)
    wv = load_w(p, "wv", D, D)
    bq = load_rows(p, "bq", D)
    bk = load_rows(p, "bk", D)
    bvr = p.sbuf("bvr", [1, D], F32)
    p.dma("sync", bvr[:], p.dram("bv", [1, D], F32)[:, :], writes=("bvr",))
    bvb = bcast_row(p, "bvb", bvr[:], "bvr", D, com)
    q_d = p.dram("qo", [D, TH], BF16, kind="ExternalOutput")
    k_d = p.dram("ko", [D, TH], BF16, kind="ExternalOutput")
    v_d = p.dram("vo", [TH, D], BF16, kind="ExternalOutput")
    xt = [p.sbuf(f"xq{k}", [128, TQ], BF16) for k in range(KT)]
    vbuf = p.sbuf("vbuf", [128, 512], BF16)
    for h0 in (0, TQ):
        for k in range(KT):
            p.dma("sync", xt[k][:], x_d[k * 128:(k + 1) * 128, h0:h0 + TQ],
                  writes=(f"xq{k}",))
        xn = ln_d(p, "ln", xt, TQ, com, out_dtype=BF16, in_bf=True,
                  x_keys=[f"xq{k}" for k in range(KT)])
        for (w, bias, od, nm) in ((wq, bq, q_d, "q"), (wk, bk, k_d, "k")):
            ot = proj(p, f"{nm}{h0}", xn, w, D, TQ, com,
                      bias=[t[:] for t in bias] if bias else None,
                      bias_key=nm == "q" and "bq" or "bk",
                      evac_func="bias_add" if bias else None)
            for m in range(KT):
                p.dma("gpsimd", od[m * 128:(m + 1) * 128, h0:h0 + TQ],
                      ot[m][:], reads=(tkey(ot[m]),))
        for tt in range(TQ // 128):
            for c0 in range(0, D, 512):
                ps = com.pool.get()
                for k in range(KT):
                    p.mm(ps[:, :], xn[k][:, tt * 128:(tt + 1) * 128],
                         wv[k][:, c0:c0 + 512], start=(k == 0),
                         stop=(k == KT - 1), reads=(tkey(xn[k]), f"wv_w{k}"),
                         writes=(tkey(ps),), group=f"v{h0}_{tt}_{c0}")
                p.t_op("vector", "tensor_tensor", vbuf[:], ps[:, :],
                       bvb[:, c0:c0 + 512], ALU.add,
                       reads=(tkey(ps), tkey(bvb)), writes=("vbuf",))
                p.dma("gpsimd", v_d[h0 + tt * 128:h0 + (tt + 1) * 128,
                                    c0:c0 + 512], vbuf[:], reads=("vbuf",))
    p.finish()
    return p.finalize()


def host_l2aa_kv(state):
    inp = state["inp"]
    maps = []
    for c in range(8):
        maps.append({
            "x": as_bf(xshard(state["x"], c)),
            "wq": as_bf(inp["L2_attn_a_W_qkv"][0:3072:3].reshape(-1)[0:0] if False else qkv_slice(inp, "q").T),
            "wk": as_bf(qkv_slice(inp, "k").T),
            "wv": as_bf(qkv_slice(inp, "v").T),
            "bq": as_f32(qkv_bias(inp, "q").reshape(-1, 1)),
            "bk": as_f32(qkv_bias(inp, "k").reshape(-1, 1)),
            "bv": as_f32(qkv_bias(inp, "v").reshape(1, -1)),
        })
    return maps


def qkv_slice(inp, which):
    W = inp["L2_attn_a_W_qkv"]
    off = {"q": 0, "k": 64, "v": 128}[which]
    rows = np.concatenate([np.arange(h * 192 + off, h * 192 + off + 64)
                           for h in range(16)])
    return W[rows, :]


def qkv_bias(inp, which):
    b = inp["L2_attn_a_b_qkv"]
    off = {"q": 0, "k": 64, "v": 128}[which]
    rows = np.concatenate([np.arange(h * 192 + off, h * 192 + off + 64)
                           for h in range(16)])
    return b[rows]


def post_l2aa_kv(state, results):
    state["l2aa_q"] = [results[c]["qo"] for c in range(8)]
    state["l2aa_k"] = [results[c]["ko"] for c in range(8)]
    state["l2aa_v"] = [results[c]["vo"] for c in range(8)]


def build_l2aa_at():
    p = Prog()
    com = Common(p)
    NH, HD = 16, 64
    q_d = p.dram("q", [D, TH], BF16)
    k_d = p.dram("kf", [D, TFULL], BF16)
    v_d = p.dram("vf", [TFULL, NH * 65], BF16)
    x_d = p.dram("x", [D, TH], F32)
    wo = load_w(p, "wo", D, D)
    bo = load_rows(p, "bo", D)
    xt = load_dmajor(p, "x", TH)
    qt = [p.sbuf(f"qt{k}", [128, TH], BF16) for k in range(KT)]
    for k in range(KT):
        p.dma("sync", qt[k][:], q_d[k * 128:(k + 1) * 128, :], writes=(f"qt{k}",))
    attn = [p.sbuf(f"attn{k}", [128, TH], BF16) for k in range(KT)]
    kg = [p.sbuf(f"kg{j}", [128, TFULL], BF16) for j in range(2)]
    vg = [p.sbuf(f"vg{t}", [128, 4 * 65], BF16) for t in range(TFULL // 128)]
    psA = [p.psum(f"psA{i}", [128, 512], F32) for i in range(2)]
    psO = p.psum("psO", [65, 512], F32)
    eP = [p.sbuf(f"eP{i}", [128, 512], BF16) for i in range(2)]
    dnr = p.sbuf("dnr", [1, 512], F32)
    onb = p.sbuf("onb", [128, 512], F32)
    for g in range(4):
        for j in range(2):
            p.dma("sync", kg[j][:], k_d[g * 256 + j * 128:g * 256 + (j + 1) * 128, :],
                  writes=(f"kg{j}",))
        for t in range(TFULL // 128):
            p.dma("sync", vg[t][:], v_d[t * 128:(t + 1) * 128,
                                        g * 4 * 65:(g + 1) * 4 * 65],
                  writes=(f"vg{t}",))
        for hh in range(4):
            h = g * 4 + hh
            krow = hh * 64
            ktile = kg[hh // 2]
            kbase = 64 * (hh % 2)
            qtile = qt[h // 2]
            qbase = 64 * (h % 2)
            for qc in range(TH // 512):
                for kt in range(TFULL // 128):
                    pa = psA[kt % 2]
                    ep = eP[kt % 2]
                    p.mm(pa[0:128, :],
                         ktile[kbase:kbase + 64, kt * 128:(kt + 1) * 128],
                         qtile[qbase:qbase + 64, qc * 512:(qc + 1) * 512],
                         start=True, stop=True,
                         reads=(f"kg{hh // 2}", f"qt{h // 2}"),
                         writes=(tkey(pa),))
                    p.t_op("scalar", "activation", ep[:], pa[0:128, :], AF.Exp,
                           scale=0.125, reads=(tkey(pa),), writes=(tkey(ep),))
                    p.mm(psO[:, :], vg[kt][:, hh * 65:(hh + 1) * 65], ep[:],
                         start=(kt == 0), stop=(kt == TFULL // 128 - 1),
                         reads=(f"vg{kt}", tkey(ep)), writes=("psO",),
                         group=f"o{h}_{qc}")
                p.t_op("vector", "reciprocal", dnr[:], psO[64:65, :],
                       reads=("psO",), writes=("dnr",))
                psB = com.pool.get()
                p.mm(psB[:, :], com.ones_f[:], dnr[:], start=True, stop=True,
                     reads=("c_ones_f", "dnr"), writes=(tkey(psB),))
                p.t_op("vector", "tensor_copy", onb[:], psB[:, :],
                       reads=(tkey(psB),), writes=("onb",))
                p.t_op("vector", "tensor_tensor",
                       attn[h // 2][qbase:qbase + 64, qc * 512:(qc + 1) * 512],
                       psO[0:64, :], onb[0:64, :], ALU.mult,
                       reads=("psO", "onb"), writes=(f"attn{h // 2}",))
    xkeys = [f"x_t{k}" for k in range(KT)]
    proj(p, "out", attn, wo, D, TH, com,
         out_aps=[t[:] for t in xt], out_keys=xkeys,
         bias=[t[:] for t in bo], bias_key="bo",
         accum_into=xt, accum_keys=xkeys)
    d_out = p.dram("xo", [D, TH], F32, kind="ExternalOutput")
    for k in range(KT):
        p.dma("gpsimd", d_out[k * 128:(k + 1) * 128, :], xt[k][:], reads=(xkeys[k],))
    p.finish()
    return p.finalize()


def host_l2aa_at(state):
    inp = state["inp"]
    maps = []
    for c in range(8):
        b, hf = c // 2, c % 2
        c0, c1 = 2 * b, 2 * b + 1
        kf = np.concatenate([state["l2aa_k"][c0], state["l2aa_k"][c1]], axis=1)
        v = np.concatenate([state["l2aa_v"][c0], state["l2aa_v"][c1]], axis=0)
        vE = np.ones((TFULL, 16 * 65), BF)
        for h in range(16):
            vE[:, h * 65:h * 65 + 64] = v[:, h * 64:(h + 1) * 64]
        maps.append({
            "q": state["l2aa_q"][c].astype(BF),
            "kf": kf.astype(BF), "vf": vE,
            "x": xshard(state["x"], c),
            "wo": as_bf(inp["L2_attn_a_W_out"].T),
            "bo": as_f32(inp["L2_attn_a_b_out"].reshape(-1, 1)),
        })
    return maps


def post_l2aa_at(state, results):
    state["x"] = unshard_x(results)


def lerp_mix(p, tag, h, hkeys, tm, tmkey, out_tiles, TQ, com, fm_key=None):
    for k in range(len(h)):
        e = ["vector", "gpsimd"][k % 2]
        ok = tkey(out_tiles[k])
        cur = h[k][:, 1:TQ + 1]
        prev = h[k][:, 0:TQ]
        p.t_op(e, "tensor_sub", out_tiles[k][:, 0:TQ], cur, prev,
               reads=(hkeys[k],), writes=(ok,))
        p.t_op(e, "tensor_scalar", out_tiles[k][:, 0:TQ], out_tiles[k][:, 0:TQ],
               tm[k][:], None, ALU.mult, reads=(ok, tmkey), writes=(ok,))
        p.t_op(e, "tensor_add", out_tiles[k][:, 0:TQ], out_tiles[k][:, 0:TQ],
               prev, reads=(ok, hkeys[k]), writes=(ok,))


def build_l0a_proj():
    p = Prog()
    com = Common(p)
    TQ = 512
    x_d = p.dram("x", [D, TH + 1], BF16)
    wk = load_w(p, "wk", D, D)
    wv = load_w(p, "wv", D, D)
    wr = load_w(p, "wr", D, D)
    l1g = load_rows(p, "l1g", D)
    l1b = load_rows(p, "l1b", D)
    tmk = load_rows(p, "tmk", D)
    tmv = load_rows(p, "tmv", D)
    tmr = load_rows(p, "tmr", D)
    fm = p.sbuf("fmv", [128, 1], F32)
    p.dma("sync", fm[:], p.dram("fm", [128, 1], F32)[:, :], writes=("fmv",))
    k_d = p.dram("ko", [D, TH], BF16, kind="ExternalOutput")
    v_d = p.dram("vo", [D, TH], BF16, kind="ExternalOutput")
    s_d = p.dram("so", [D, TH], BF16, kind="ExternalOutput")
    xt = [p.sbuf(f"xq{k}", [128, TQ + 1], BF16) for k in range(KT)]
    mix = [p.sbuf(f"mx{k}", [128, TQ], BF16) for k in range(KT)]
    for q0 in range(0, TH, TQ):
        for k in range(KT):
            p.dma("sync", xt[k][:], x_d[k * 128:(k + 1) * 128, q0:q0 + TQ + 1],
                  writes=(f"xq{k}",))
        h0 = ln_d(p, "h0", xt, TQ + 1, com, out_dtype=BF16, in_bf=True,
                  x_keys=[f"xq{k}" for k in range(KT)])
        h1 = ln_d(p, "h1", h0, TQ + 1, com, g=[t[:] for t in l1g],
                  b=[t[:] for t in l1b], gb_key="l1g", out_dtype=BF16,
                  in_bf=True, scratch="lnw2")
        if q0 == 0:
            for k in range(KT):
                p.t_op("vector", "tensor_scalar", h1[k][:, 0:1], h1[k][:, 0:1],
                       fm[:], None, ALU.mult, reads=(tkey(h1[k]), "fmv"),
                       writes=(tkey(h1[k]),))
        for (tm, tmn, w, od, ef) in ((tmk, "tmk", wk, k_d, None),
                                     (tmv, "tmv", wv, v_d, None),
                                     (tmr, "tmr", wr, s_d, AF.Sigmoid)):
            lerp_mix(p, tmn, h1, keys(h1), tm, tmn, mix, TQ, com)
            ot = proj(p, f"p{tmn}", mix, w, D, TQ, com, evac_func=ef)
            for m in range(KT):
                p.dma("gpsimd", od[m * 128:(m + 1) * 128, q0:q0 + TQ],
                      ot[m][:], reads=(tkey(ot[m]),))
    p.finish()
    return p.finalize()


def host_l0a_proj(state, pref="L0_attn_a"):
    inp = state["inp"]
    maps = []
    for c in range(8):
        xs = xshard(state["x"], c, halo=1)
        maps.append({
            "x": as_bf(xs),
            "wk": as_bf(inp[f"{pref}_Wk"].T), "wv": as_bf(inp[f"{pref}_Wv"].T),
            "wr": as_bf(inp[f"{pref}_Wr"].T),
            "l1g": as_f32(inp[f"{pref}_ln1_g"].reshape(-1, 1)),
            "l1b": as_f32(inp[f"{pref}_ln1_b"].reshape(-1, 1)),
            "tmk": as_f32(inp[f"{pref}_tm_k"].reshape(-1, 1)),
            "tmv": as_f32(inp[f"{pref}_tm_v"].reshape(-1, 1)),
            "tmr": as_f32(inp[f"{pref}_tm_r"].reshape(-1, 1)),
            "fm": as_f32(np.full((128, 1), 0.0 if c % 2 == 0 else 1.0, np.float32)),
        })
    return maps


def post_l0a_proj(state, results):
    for nm, key in (("ko", "l0a_k"), ("vo", "l0a_v"), ("so", "l0a_sr")):
        full = np.zeros((NB, D, TFULL), BF)
        for c in range(8):
            b, h = c // 2, c % 2
            full[b, :, h * TH:(h + 1) * TH] = results[c][nm]
        state[key] = full


def build_l0a_wkv():
    p = Prog()
    com = Common(p)
    NL = 512
    k_d = p.dram("k", [NL, TFULL], BF16)
    v_d = p.dram("v", [NL, TFULL], BF16)
    s_d = p.dram("sr", [NL, TFULL], BF16)
    td = load_rows(p, "td", NL)
    tf = load_rows(p, "tf", NL)
    o_d = p.dram("yo", [NL, TFULL], F32, kind="ExternalOutput")
    kt = p.sbuf("ktl", [128, TFULL], BF16)
    vt = p.sbuf("vtl", [128, TFULL], BF16)
    st = p.sbuf("stl", [128, TFULL], BF16)
    EK = p.sbuf("EK", [128, TFULL], F32)
    EKV = p.sbuf("EKV", [128, TFULL], F32)
    num = p.sbuf("num", [128, TFULL], F32)
    den = p.sbuf("den", [128, TFULL], F32)
    ab = p.sbuf("ab", [128, TFULL], F32)
    for lt in range(NL // 128):
        r = slice(lt * 128, (lt + 1) * 128)
        p.dma("sync", kt[:], k_d[r, :], writes=("ktl",))
        p.dma("sync", vt[:], v_d[r, :], writes=("vtl",))
        p.dma("sync", st[:], s_d[r, :], writes=("stl",))
        w1 = p.sbuf("w1", [128, 1], F32)
        eu = p.sbuf("eu", [128, 1], F32)
        p.t_op("scalar", "activation", w1[:], td[lt][:], AF.Exp,
               reads=("td",), writes=("w1",))
        p.t_op("scalar", "activation", w1[:], w1[:], AF.Exp, scale=-1.0,
               reads=("w1",), writes=("w1",))
        p.t_op("scalar", "activation", eu[:], tf[lt][:], AF.Exp,
               reads=("tf",), writes=("eu",))
        p.t_op("vector", "memset", ab[:], 1.0, writes=("ab",))
        p.t_op("vector", "tensor_scalar", ab[:], ab[:], w1[:], None, ALU.mult,
               reads=("ab", "w1"), writes=("ab",))
        p.t_op("scalar", "activation", EK[:], kt[:], AF.Exp,
               reads=("ktl",), writes=("EK",))
        p.t_op("vector", "tensor_tensor", EKV[:], EK[:], vt[:], ALU.mult,
               reads=("EK", "vtl"), writes=("EKV",))
        p.t_op("vector", "tensor_tensor_scan", num[:], ab[:], EKV[:], 0.0,
               ALU.mult, ALU.add, reads=("ab", "EKV"), writes=("num",))
        p.t_op("vector", "tensor_tensor_scan", den[:], ab[:], EK[:], 0.0,
               ALU.mult, ALU.add, reads=("ab", "EK"), writes=("den",))
        p.t_op("vector", "tensor_scalar", EKV[:], EKV[:], eu[:], None, ALU.mult,
               reads=("EKV", "eu"), writes=("EKV",))
        p.t_op("vector", "tensor_tensor", EKV[:, 1:TFULL], EKV[:, 1:TFULL],
               num[:, 0:TFULL - 1], ALU.add, reads=("EKV", "num"), writes=("EKV",))
        p.t_op("vector", "tensor_scalar", EK[:], EK[:], eu[:], None, ALU.mult,
               reads=("EK", "eu"), writes=("EK",))
        p.t_op("vector", "tensor_tensor", EK[:, 1:TFULL], EK[:, 1:TFULL],
               den[:, 0:TFULL - 1], ALU.add, reads=("EK", "den"), writes=("EK",))
        p.t_op("vector", "reciprocal", num[:], EK[:], reads=("EK",), writes=("num",))
        p.t_op("vector", "tensor_tensor", EKV[:], EKV[:], num[:], ALU.mult,
               reads=("EKV", "num"), writes=("EKV",))
        p.t_op("vector", "tensor_tensor", EKV[:], EKV[:], st[:], ALU.mult,
               reads=("EKV", "stl"), writes=("EKV",))
        p.dma("gpsimd", o_d[r, :], EKV[:], reads=("EKV",))
    p.finish()
    return p.finalize()


def host_l0a_wkv(state, pref="L0_attn_a"):
    inp = state["inp"]
    maps = []
    for c in range(8):
        b, h = c // 2, c % 2
        ch = slice(h * 512, (h + 1) * 512)
        maps.append({
            "k": np.ascontiguousarray(state["l0a_k"][b][ch]),
            "v": np.ascontiguousarray(state["l0a_v"][b][ch]),
            "sr": np.ascontiguousarray(state["l0a_sr"][b][ch]),
            "td": as_f32(inp[f"{pref}_time_decay"][ch].reshape(-1, 1)),
            "tf": as_f32(inp[f"{pref}_time_first"][ch].reshape(-1, 1)),
        })
    return maps


def post_l0a_wkv(state, results):
    full = np.zeros((NB, D, TFULL), np.float32)
    for c in range(8):
        b, h = c // 2, c % 2
        full[b, h * 512:(h + 1) * 512, :] = results[c]["yo"]
    state["l0a_rwkv"] = full


def build_l0a_cm1():
    p = Prog()
    com = Common(p)
    TQ = 512
    x_d = p.dram("x", [D, TH + 1], F32)
    r_d = p.dram("rw", [D, TH + 1], BF16)
    wo = load_w(p, "wo", D, D)
    wcr = load_w(p, "wcr", D, D)
    wck = load_w(p, "wck", D, 4096)
    l2g = load_rows(p, "l2g", D)
    l2b = load_rows(p, "l2b", D)
    cmk = load_rows(p, "cmk", D)
    cmr = load_rows(p, "cmr", D)
    fm = p.sbuf("fmv", [128, 1], F32)
    p.dma("sync", fm[:], p.dram("fm", [128, 1], F32)[:, :], writes=("fmv",))
    xs_d = p.dram("xs", [D, TH], F32, kind="ExternalOutput")
    kk_d = p.dram("kk", [4096, TH], BF16, kind="ExternalOutput")
    sr_d = p.dram("sr2", [D, TH], BF16, kind="ExternalOutput")
    xt = [p.sbuf(f"xq{k}", [128, TQ + 1], F32) for k in range(KT)]
    rt = [p.sbuf(f"rq{k}", [128, TQ + 1], BF16) for k in range(KT)]
    mix = [p.sbuf(f"mx{k}", [128, TQ], BF16) for k in range(KT)]
    for q0 in range(0, TH, TQ):
        xkeys = [f"xq{k}" for k in range(KT)]
        for k in range(KT):
            p.dma("sync", xt[k][:], x_d[k * 128:(k + 1) * 128, q0:q0 + TQ + 1],
                  writes=(xkeys[k],))
            p.dma("sync", rt[k][:], r_d[k * 128:(k + 1) * 128, q0:q0 + TQ + 1],
                  writes=(f"rq{k}",))
        h0 = ln_d(p, "h0", xt, TQ + 1, com, out_dtype=F32, x_keys=xkeys)
        proj(p, "xl", rt, wo, D, TQ + 1, com,
             out_aps=[t[:] for t in h0], out_keys=keys(h0),
             accum_into=h0, accum_keys=keys(h0))
        for k in range(KT):
            p.t_op("vector", "tensor_add", xt[k][:, 1:TQ + 1], xt[k][:, 1:TQ + 1],
                   h0[k][:, 1:TQ + 1], reads=(xkeys[k], tkey(h0[k])),
                   writes=(xkeys[k],))
            p.dma("gpsimd", xs_d[k * 128:(k + 1) * 128, q0:q0 + TQ],
                  xt[k][:, 1:TQ + 1], reads=(xkeys[k],))
        h2 = ln_d(p, "h2", h0, TQ + 1, com, g=[t[:] for t in l2g],
                  b=[t[:] for t in l2b], gb_key="l2g", out_dtype=BF16)
        if q0 == 0:
            for k in range(KT):
                p.t_op("vector", "tensor_scalar", h2[k][:, 0:1], h2[k][:, 0:1],
                       fm[:], None, ALU.mult, reads=(tkey(h2[k]), "fmv"),
                       writes=(tkey(h2[k]),))
        lerp_mix(p, "cmk", h2, keys(h2), cmk, "cmk", mix, TQ, com)
        kbuf = [p.sbuf(f"kbuf{i}", [128, TQ], BF16) for i in range(2)]
        for m in range(32):
            ps = com.pool.get()
            for k in range(KT):
                p.mm(ps[:, :], wck[k][:, m * 128:(m + 1) * 128], mix[k][:],
                     start=(k == 0), stop=(k == KT - 1),
                     reads=(f"wck_w{k}", tkey(mix[k])), writes=(tkey(ps),),
                     group=f"kk{q0}_{m}")
            kb = kbuf[m % 2]
            p.t_op("scalar", "activation", kb[:], ps[:, :], AF.Relu,
                   reads=(tkey(ps),), writes=(tkey(kb),))
            p.t_op("vector", "tensor_mul", kb[:], kb[:], kb[:],
                   reads=(tkey(kb),), writes=(tkey(kb),))
            p.dma("gpsimd", kk_d[m * 128:(m + 1) * 128, q0:q0 + TQ], kb[:],
                  reads=(tkey(kb),))
        lerp_mix(p, "cmr", h2, keys(h2), cmr, "cmr", mix, TQ, com)
        srt = proj(p, "sr", mix, wcr, D, TQ, com, evac_func=AF.Sigmoid)
        for m in range(KT):
            p.dma("gpsimd", sr_d[m * 128:(m + 1) * 128, q0:q0 + TQ],
                  srt[m][:], reads=(tkey(srt[m]),))
    p.finish()
    return p.finalize()


def host_l0a_cm1(state, pref="L0_attn_a"):
    inp = state["inp"]
    maps = []
    for c in range(8):
        b, h = c // 2, c % 2
        t0 = h * TH - 1
        rw = np.zeros((D, TH + 1), np.float32)
        lo = max(t0, 0)
        rw[:, (1 if t0 < 0 else 0):] = state["l0a_rwkv"][b][:, lo:(h + 1) * TH]
        maps.append({
            "x": xshard(state["x"], c, halo=1),
            "rw": as_bf(rw),
            "wo": as_bf(inp[f"{pref}_Wo"].T),
            "wcr": as_bf(inp[f"{pref}_Wcr"].T),
            "wck": as_bf(inp[f"{pref}_Wck"].T),
            "l2g": as_f32(inp[f"{pref}_ln2_g"].reshape(-1, 1)),
            "l2b": as_f32(inp[f"{pref}_ln2_b"].reshape(-1, 1)),
            "cmk": as_f32(inp[f"{pref}_cm_k"].reshape(-1, 1)),
            "cmr": as_f32(inp[f"{pref}_cm_r"].reshape(-1, 1)),
            "fm": as_f32(np.full((128, 1), 0.0 if h == 0 else 1.0, np.float32)),
        })
    return maps


def post_l0a_cm1(state, results):
    state["l0a_xs"] = [results[c]["xs"] for c in range(8)]
    state["l0a_kk"] = [results[c]["kk"] for c in range(8)]
    state["l0a_sr2"] = [results[c]["sr2"] for c in range(8)]


def build_l0a_cm2():
    p = Prog()
    com = Common(p)
    TQ = 512
    xs_d = p.dram("xs", [D, TH], F32)
    kk_d = p.dram("kk", [4096, TH], BF16)
    sr_d = p.dram("sr2", [D, TH], BF16)
    wcv = load_w(p, "wcv", 4096, D)
    d_out = p.dram("xo", [D, TH], F32, kind="ExternalOutput")
    kkt = [p.sbuf(f"kq{j}", [128, TQ], BF16) for j in range(32)]
    srt = [p.sbuf(f"sq{m}", [128, TQ], BF16) for m in range(KT)]
    xt = [p.sbuf(f"xq{m}", [128, TQ], F32) for m in range(KT)]
    for q0 in range(0, TH, TQ):
        for j in range(32):
            p.dma("sync", kkt[j][:], kk_d[j * 128:(j + 1) * 128, q0:q0 + TQ],
                  writes=(f"kq{j}",))
        for m in range(KT):
            p.dma("sync", srt[m][:], sr_d[m * 128:(m + 1) * 128, q0:q0 + TQ],
                  writes=(f"sq{m}",))
            p.dma("sync", xt[m][:], xs_d[m * 128:(m + 1) * 128, q0:q0 + TQ],
                  writes=(f"xq{m}",))
        for m in range(KT):
            ps = com.pool.get()
            for j in range(32):
                p.mm(ps[:, :], wcv[j][:, m * 128:(m + 1) * 128], kkt[j][:],
                     start=(j == 0), stop=(j == 31),
                     reads=(f"wcv_w{j}", f"kq{j}"), writes=(tkey(ps),),
                     group=f"kv{q0}_{m}")
            p.t_op("vector", "tensor_tensor", srt[m][:], srt[m][:], ps[:, :],
                   ALU.mult, reads=(f"sq{m}", tkey(ps)), writes=(f"sq{m}",))
            p.t_op("vector", "tensor_tensor", xt[m][:], xt[m][:], srt[m][:],
                   ALU.add, reads=(f"xq{m}", f"sq{m}"), writes=(f"xq{m}",))
            p.dma("gpsimd", d_out[m * 128:(m + 1) * 128, q0:q0 + TQ],
                  xt[m][:], reads=(f"xq{m}",))
    p.finish()
    return p.finalize()


def host_l0a_cm2(state, pref="L0_attn_a"):
    inp = state["inp"]
    return [{"xs": as_f32(state["l0a_xs"][c]),
             "kk": state["l0a_kk"][c].astype(BF),
             "sr2": state["l0a_sr2"][c].astype(BF),
             "wcv": as_bf(inp[f"{pref}_Wcv"].T)} for c in range(8)]


def post_l0a_cm2(state, results):
    x = np.zeros((NB, TFULL, D), np.float32)
    for c in range(8):
        b, h = c // 2, c % 2
        x[b, h * TH:(h + 1) * TH, :] = results[c]["xo"].T
    state["x"] = x


def build_l0f():
    p = Prog()
    com = Common(p)
    TQ = 512
    x_d = p.dram("x", [D, TH], F32)
    hms = load_w(p, "hm", D, D)
    w_d = p.dram("w", [D, D], F32)
    ident = load_ident(p)
    d_out = p.dram("xo", [D, TH], F32, kind="ExternalOutput")
    wt = []
    for k in range(KT):
        t = p.sbuf(f"wrow{k}", [128, D], F32)
        p.dma("sync", t[:], w_d[k * 128:(k + 1) * 128, :], writes=(f"wrow{k}",))
        wt.append(t)
    wq = [p.sbuf(f"wq{k}", [128, D], BF16) for k in range(KT)]
    it = p.sbuf("i32w", [128, D], I32)
    mt2 = p.sbuf("mt2w", [128, D], F32)
    wsc = p.sbuf("wsc", [128, KT], F32)
    for k in range(KT):
        am = p.sbuf("wam", [128, 1], F32)
        p.t_op("vector", "tensor_reduce", am[:], wt[k][:], AX.X, ALU.add,
               apply_absolute_value=True, reads=(f"wrow{k}",), writes=("wam",))
        p.t_op("vector", "tensor_scalar", wsc[:, k:k + 1], am[:], 1.0 / D, None,
               ALU.mult, reads=("wam",), writes=("wsc",))
        rs = p.sbuf("wrs", [128, 1], F32)
        p.t_op("vector", "reciprocal", rs[:], wsc[:, k:k + 1], reads=("wsc",),
               writes=("wrs",))
        p.t_op("vector", "tensor_scalar", wt[k][:], wt[k][:], rs[:], None,
               ALU.mult, reads=(f"wrow{k}", "wrs"), writes=(f"wrow{k}",))
        p.t_op("vector", "tensor_scalar", wt[k][:], wt[k][:], 1.0, -1.0,
               ALU.min, ALU.max, reads=(f"wrow{k}",), writes=(f"wrow{k}",))
        round_ints(p, f"wr{k}", wt[k][:], f"wrow{k}", wt[k][:], f"wrow{k}",
                   it[:], mt2[:])
        wb = p.sbuf("wbf", [128, D], BF16)
        p.t_op("vector", "tensor_copy", wb[:], wt[k][:], reads=(f"wrow{k}",),
               writes=("wbf",))
        for ct in range(KT):
            transpose_128(p, "wtr", wb[:, ct * 128:(ct + 1) * 128], "wbf",
                          wq[ct][:, k * 128:(k + 1) * 128], f"wq{ct}", ident, com)
    xt = [p.sbuf(f"xq{k}", [128, TQ], F32) for k in range(KT)]
    for q0 in range(0, TH, TQ):
        xkeys = [f"xq{k}" for k in range(KT)]
        for k in range(KT):
            p.dma("sync", xt[k][:], x_d[k * 128:(k + 1) * 128, q0:q0 + TQ],
                  writes=(xkeys[k],))
        xn = ln_d(p, "ln", xt, TQ, com, out_dtype=F32, precise=True,
                  x_keys=xkeys)
        xhi = [p.sbuf(f"xhi{k}", [128, TQ], BF16) for k in range(KT)]
        xlo = [p.sbuf(f"xlo{k}", [128, TQ], BF16) for k in range(KT)]
        for k in range(KT):
            e = ["vector", "gpsimd"][k % 2]
            p.t_op(e, "tensor_copy", xhi[k][:], xn[k][:], reads=(tkey(xn[k]),),
                   writes=(tkey(xhi[k]),))
            p.t_op(e, "tensor_sub", xlo[k][:], xn[k][:], xhi[k][:],
                   reads=(tkey(xn[k]), tkey(xhi[k])), writes=(tkey(xlo[k]),))
        xqiT = [p.sbuf(f"xqiT{k}", [128, TQ], BF16) for k in range(KT)]
        sxr = p.sbuf("sxr", [1, TQ], F32)
        for tt in range(TQ // 128):
            pst = com.pool.get()
            xhbuf = p.sbuf("xhbuf", [128, D], F32)
            for c0 in range(0, D, 512):
                for k in range(KT):
                    p.mm(pst[:, :], xhi[k][:, tt * 128:(tt + 1) * 128],
                         hms[k][:, c0:c0 + 512], start=(k == 0), stop=False,
                         reads=(tkey(xhi[k]), f"hm_w{k}"), writes=(tkey(pst),),
                         group=f"xh{q0}_{tt}_{c0}")
                for k in range(KT):
                    p.mm(pst[:, :], xlo[k][:, tt * 128:(tt + 1) * 128],
                         hms[k][:, c0:c0 + 512], start=False, stop=(k == KT - 1),
                         reads=(tkey(xlo[k]), f"hm_w{k}"), writes=(tkey(pst),),
                         group=f"xh{q0}_{tt}_{c0}")
                p.t_op("vector", "tensor_copy", xhbuf[:, c0:c0 + 512],
                       pst[:, :], reads=(tkey(pst),), writes=("xhbuf",))
                pst = com.pool.get()
            amx = p.sbuf("amx", [128, 1], F32)
            p.t_op("vector", "reduce_max", amx[:], xhbuf[:], AX.X,
                   apply_absolute_value=True, reads=("xhbuf",), writes=("amx",))
            p.t_op("vector", "tensor_scalar", amx[:], amx[:], 1e-5, 1.0 / 7.0,
                   ALU.max, ALU.mult, reads=("amx",), writes=("amx",))
            rs = p.sbuf("qrs", [128, 1], F32)
            p.t_op("vector", "reciprocal", rs[:], amx[:], reads=("amx",),
                   writes=("qrs",))
            p.t_op("vector", "tensor_scalar", xhbuf[:], xhbuf[:], rs[:], None,
                   ALU.mult, reads=("xhbuf", "qrs"), writes=("xhbuf",))
            it2 = p.sbuf("i32q", [128, D], I32)
            mtq = p.sbuf("mtq2", [128, D], F32)
            round_ints(p, "qr", xhbuf[:], "xhbuf", xhbuf[:], "xhbuf", it2[:],
                       mtq[:])
            xqb = p.sbuf("xqb", [128, D], BF16)
            p.t_op("vector", "tensor_copy", xqb[:], xhbuf[:], reads=("xhbuf",),
                   writes=("xqb",))
            dscr = p.dram("sx_scr", [128, 1], F32, kind="Internal")
            p.dma("gpsimd", dscr[:, :], amx[:], reads=("amx",), writes=("sx_scr",))
            p.dma("gpsimd", sxr[:, tt * 128:(tt + 1) * 128],
                  dscr[:, :].rearrange("p one -> one p"),
                  reads=("sx_scr",), writes=("sxr",))
            for ct in range(KT):
                transpose_128(p, "xtr", xqb[:, ct * 128:(ct + 1) * 128], "xqb",
                              xqiT[ct][:, tt * 128:(tt + 1) * 128],
                              f"xqiT{ct}", ident, com)
        sxb = bcast_row(p, "sxb", sxr[:], "sxr", TQ, com)
        for m in range(KT):
            ps = com.pool.get()
            for k in range(KT):
                p.mm(ps[:, :], wq[k][:, m * 128:(m + 1) * 128], xqiT[k][:],
                     start=(k == 0), stop=(k == KT - 1),
                     reads=(f"wq{k}", tkey(xqiT[k])), writes=(tkey(ps),),
                     group=f"y{q0}_{m}")
            yb = p.sbuf("ybuf", [128, TQ], F32)
            p.t_op("vector", "tensor_scalar", yb[:], ps[:, :], wsc[:, m:m + 1],
                   None, ALU.mult, reads=(tkey(ps), "wsc"), writes=("ybuf",))
            p.t_op("vector", "tensor_mul", yb[:], yb[:], sxb[:],
                   reads=("ybuf", tkey(sxb)), writes=("ybuf",))
            p.t_op("vector", "tensor_add", xt[m][:], xt[m][:], yb[:],
                   reads=(xkeys[m], "ybuf"), writes=(xkeys[m],))
            p.dma("gpsimd", d_out[m * 128:(m + 1) * 128, q0:q0 + TQ],
                  xt[m][:], reads=(xkeys[m],))
    p.finish()
    return p.finalize()


def _hadamard1024():
    h = np.array([[1.0]], dtype=np.float64)
    while h.shape[0] < 1024:
        h = np.block([[h, h], [h, -h]])
    return (h / 32.0).astype(np.float32)


def host_l0f(state):
    inp = state["inp"]
    H = _hadamard1024()
    return [{"x": xshard(state["x"], c), "hm": as_bf(H),
             "w": as_f32(inp["L0_ffn_weight"]), "ident": get_ident()}
            for c in range(8)]


def post_l0f(state, results):
    state["x"] = unshard_x(results)


def build_resproj(din=D, with_bias=False):
    p = Prog()
    com = Common(p)
    x_d = p.dram("x", [D, TH], F32)
    y_d = p.dram("y", [din, TH], BF16)
    wo = load_w(p, "wo", din, D)
    bias = load_rows(p, "bo", D) if with_bias else None
    xt = load_dmajor(p, "x", TH)
    yt = []
    for k in range(din // 128):
        t = p.sbuf(f"yy{k}", [128, TH], BF16)
        p.dma("sync", t[:], y_d[k * 128:(k + 1) * 128, :], writes=(f"yy{k}",))
        yt.append(t)
    xkeys = [f"x_t{k}" for k in range(KT)]
    proj(p, "o", yt, wo, D, TH, com, out_aps=[t[:] for t in xt],
         out_keys=xkeys, accum_into=xt, accum_keys=xkeys,
         bias=[t[:] for t in bias] if with_bias else None,
         bias_key="bo" if with_bias else None)
    d_out = p.dram("xo", [D, TH], F32, kind="ExternalOutput")
    for k in range(KT):
        p.dma("gpsimd", d_out[k * 128:(k + 1) * 128, :], xt[k][:], reads=(xkeys[k],))
    p.finish()
    return p.finalize()


def host_resproj(state, ykey, wkey, bkey=None, wT=True):
    inp = state["inp"]
    maps = []
    for c in range(8):
        b, h = c // 2, c % 2
        y = state[ykey][b][:, h * TH:(h + 1) * TH]
        m = {"x": xshard(state["x"], c), "y": y.astype(BF),
             "wo": as_bf(inp[wkey].T if wT else inp[wkey])}
        if bkey:
            m["bo"] = as_f32(inp[bkey].reshape(-1, 1))
        maps.append(m)
    return maps


def build_addres():
    p = Prog()
    x_d = p.dram("x", [D, TH], F32)
    y_d = p.dram("y", [D, TH], F32)
    xt = load_dmajor(p, "x", TH)
    yt = load_dmajor(p, "y", TH)
    d_out = p.dram("xo", [D, TH], F32, kind="ExternalOutput")
    for k in range(KT):
        p.t_op("vector", "tensor_add", xt[k][:], xt[k][:], yt[k][:],
               reads=(f"x_t{k}", f"y_t{k}"), writes=(f"x_t{k}",))
        p.dma("gpsimd", d_out[k * 128:(k + 1) * 128, :], xt[k][:],
              reads=(f"x_t{k}",))
    p.finish()
    return p.finalize()


def host_addres(state, ykey):
    maps = []
    for c in range(8):
        b, h = c // 2, c % 2
        y = state[ykey][b][:, h * TH:(h + 1) * TH]
        maps.append({"x": xshard(state["x"], c), "y": as_f32(y)})
    return maps


def build_l2f_a():
    p = Prog()
    com = Common(p)
    TQ = 2048
    x_d = p.dram("x", [D, TH], BF16)
    w1_d = p.dram("w1", [128, 32 * 128], BF16)
    w2_d = p.dram("w2", [128, 32 * 512], BF16)
    h_d = p.dram("h", [16384, TH], BF16, kind="ExternalOutput")
    xt = [p.sbuf(f"xq{k}", [128, TQ], BF16) for k in range(KT)]
    for k in range(KT):
        p.dma("sync", xt[k][:], x_d[k * 128:(k + 1) * 128, :], writes=(f"xq{k}",))
    xn = ln_d(p, "ln", xt, TQ, com, out_dtype=BF16, in_bf=True,
              x_keys=[f"xq{k}" for k in range(KT)])
    w1b = p.sbuf("w1b", [128, 128], BF16)
    w2b = p.sbuf("w2b", [128, 512], BF16)
    bd = [p.sbuf(f"bd{i}", [128, 512], BF16) for i in range(32)]
    for rg in range(8):
        for h in range(4):
            i = rg * 4 + h
            p.dma("sync", w1b[:], w1_d[:, i * 128:(i + 1) * 128], writes=("w1b",))
            p.dma("sync", w2b[:], w2_d[:, i * 512:(i + 1) * 512], writes=("w2b",))
            ps = com.pool.get()
            p.mm(ps[:, :], w1b[:], w2b[:], start=True, stop=True,
                 reads=("w1b", "w2b"), writes=(tkey(ps),))
            p.t_op("vector", "tensor_copy", bd[i][:], ps[:, :],
                   reads=(tkey(ps),), writes=(f"bd{i}",))
    hbuf = p.sbuf("hbuf", [128, 512], BF16)
    for rg in range(8):
        for h in range(4):
            i = rg * 4 + h
            for m in range(4):
                for c0 in range(0, TQ, 512):
                    ps = com.pool.get()
                    p.mm(ps[:, :], bd[i][:, m * 128:(m + 1) * 128],
                         xn[rg][:, c0:c0 + 512], start=True, stop=True,
                         reads=(f"bd{i}", tkey(xn[rg])), writes=(tkey(ps),))
                    p.t_op("scalar", "activation", hbuf[:], ps[:, :], AF.Gelu,
                           reads=(tkey(ps),), writes=("hbuf",))
                    row = rg * 2048 + h * 512 + m * 128
                    p.dma("gpsimd", h_d[row:row + 128, c0:c0 + 512], hbuf[:],
                          reads=("hbuf",))
    p.finish()
    return p.finalize()


def host_l2f_a(state):
    inp = state["inp"]
    K = inp["L2_ffn_rail_maker"]
    U = inp["L2_ffn_rail_up_proj"]
    W1 = np.zeros((128, 32 * 128), np.float32)
    W2 = np.zeros((128, 32 * 512), np.float32)
    for rg in range(8):
        for h in range(4):
            i = rg * 4 + h
            for rl in range(8):
                r = rg * 8 + rl
                W1[rl * 16:(rl + 1) * 16, i * 128 + rl * 16:i * 128 + (rl + 1) * 16] = K[h, r].T
                W2[rl * 16:(rl + 1) * 16, i * 512 + rl * 64:i * 512 + (rl + 1) * 64] = U[h, r].T
    maps = []
    for c in range(8):
        maps.append({"x": as_bf(xshard(state["x"], c)),
                     "w1": as_bf(W1), "w2": as_bf(W2)})
    return maps


def post_l2f_a(state, results):
    state["l2f_h"] = [results[c]["h"] for c in range(8)]


def build_l2f_b():
    p = Prog()
    com = Common(p, npool=8)
    HI = 16384
    h_d = p.dram("h", [HI, TH], BF16)
    w_d = p.dram("w", [HI, D], BF16)
    x_d = p.dram("x", [D, TH], F32)
    xt = load_dmajor(p, "x", TH)
    xkeys = [f"x_t{k}" for k in range(KT)]
    wb = [p.sbuf(f"wb{i}", [128, D], BF16) for i in range(2)]
    hb = [p.sbuf(f"hb{i}", [128, 512], BF16) for i in range(2)]
    for c0 in range(0, TH, 512):
        pss = [com.pool.get() for _ in range(KT)]
        for j in range(HI // 128):
            jb = j % 2
            p.dma("sync", wb[jb][:], w_d[j * 128:(j + 1) * 128, :],
                  writes=(f"wb{jb}",))
            p.dma("scalar", hb[jb][:], h_d[j * 128:(j + 1) * 128, c0:c0 + 512],
                  writes=(f"hb{jb}",))
            for m in range(KT):
                p.mm(pss[m][:, :], wb[jb][:, m * 128:(m + 1) * 128], hb[jb][:],
                     start=(j == 0), stop=(j == HI // 128 - 1),
                     reads=(f"wb{jb}", f"hb{jb}"), writes=(tkey(pss[m]),),
                     group=f"cp{c0}_{m}")
        for m in range(KT):
            p.t_op("vector", "tensor_tensor", xt[m][:, c0:c0 + 512],
                   xt[m][:, c0:c0 + 512], pss[m][:, :], ALU.add,
                   reads=(xkeys[m], tkey(pss[m])), writes=(xkeys[m],))
    d_out = p.dram("xo", [D, TH], F32, kind="ExternalOutput")
    for k in range(KT):
        p.dma("gpsimd", d_out[k * 128:(k + 1) * 128, :], xt[k][:],
              reads=(xkeys[k],))
    p.finish()
    return p.finalize()


def host_l2f_b(state):
    inp = state["inp"]
    W = inp["L2_ffn_w_cproj"]
    perm = np.zeros(16384, np.int64)
    pos = 0
    for rg in range(8):
        for h in range(4):
            for rl in range(8):
                r = rg * 8 + rl
                perm[pos:pos + 64] = h * 4096 + r * 64 + np.arange(64)
                pos += 64
    WT = W[:, perm].T
    return [{"h": state["l2f_h"][c], "w": as_bf(WT),
             "x": xshard(state["x"], c)} for c in range(8)]


def post_l2f_b(state, results):
    state["x"] = unshard_x(results)


def build_l1ab_a():
    p = Prog()
    com = Common(p)
    x_d = p.dram("x", [D, TH], BF16)
    xt = [p.sbuf(f"xq{k}", [128, TH], BF16) for k in range(KT)]
    for k in range(KT):
        p.dma("sync", xt[k][:], x_d[k * 128:(k + 1) * 128, :], writes=(f"xq{k}",))
    xn = ln_d(p, "ln", xt, TH, com, out_dtype=F32, in_bf=True,
              x_keys=[f"xq{k}" for k in range(KT)])
    P = [p.sbuf(f"P{k}", [128, 33], F32) for k in range(KT)]
    for l in range(33):
        lo = max(64 * l - 63, 0)
        hi = min(64 * l, TH - 1)
        for k in range(KT):
            p.t_op("vector", "tensor_reduce",
                   P[k][:, l:l + 1], xn[k][:, lo:hi + 1], AX.X, ALU.add,
                   reads=(tkey(xn[k]),), writes=(f"P{k}",))
    o_d = p.dram("po", [D, 33], F32, kind="ExternalOutput")
    for k in range(KT):
        p.dma("gpsimd", o_d[k * 128:(k + 1) * 128, :], P[k][:], reads=(f"P{k}",))
    p.finish()
    return p.finalize()


def host_l1ab_a(state):
    return [{"x": as_bf(xshard(state["x"], c))} for c in range(8)]


def post_l1ab_a(state, results):
    state["l1ab_p"] = [results[c]["po"] for c in range(8)]


def build_l1ab_b():
    p = Prog()
    com = Common(p)
    NH = 16
    TQ = 1024
    x_d = p.dram("x", [D, TH], F32)
    A_d = p.dram("pa", [D, 64], F32)
    B_d = p.dram("pb", [D, 64], F32)
    il_d = p.dram("ilen", [1, 64], F32)
    msk_d = p.dram("msk", [TH, 64], BF16)
    wq = load_w(p, "wq", D, D)
    wo = load_w(p, "wo", D, D)
    bq = load_rows(p, "bq", D)
    bk = load_rows(p, "bk", D)
    bo = load_rows(p, "bo", D)
    bvr = p.sbuf("bvr", [1, D], F32)
    p.dma("sync", bvr[:], p.dram("bv", [1, D], F32)[:, :], writes=("bvr",))
    wbuf = [p.sbuf(f"wb{k}", [128, D], BF16) for k in range(KT)]
    At = [p.sbuf(f"At{k}", [128, 64], F32) for k in range(KT)]
    Bt = [p.sbuf(f"Bt{k}", [128, 64], F32) for k in range(KT)]
    for k in range(KT):
        p.dma("sync", At[k][:], A_d[k * 128:(k + 1) * 128, :], writes=(f"At{k}",))
        p.dma("sync", Bt[k][:], B_d[k * 128:(k + 1) * 128, :], writes=(f"Bt{k}",))
    ilr = p.sbuf("ilr", [1, 64], F32)
    p.dma("sync", ilr[:], il_d[:, :], writes=("ilr",))
    ilb = bcast_row(p, "ilb", ilr[:], "ilr", 64, com)
    summ = [p.sbuf(f"sm{k}", [128, 64], BF16) for k in range(KT)]
    for k in range(KT):
        p.t_op("vector", "tensor_add", At[k][:], At[k][:], Bt[k][:],
               reads=(f"At{k}", f"Bt{k}"), writes=(f"At{k}",))
        p.t_op("vector", "tensor_mul", summ[k][:], At[k][:], ilb[:, 0:64],
               reads=(f"At{k}", tkey(ilb)), writes=(tkey(summ[k]),))
    wk_d = p.dram("wk", [D, D], BF16)
    wv_d = p.dram("wv", [D, D], BF16)
    for k in range(KT):
        p.dma("sync", wbuf[k][:], wk_d[k * 128:(k + 1) * 128, :], writes=(f"wb{k}",))
    kS = proj(p, "k", summ, wbuf, D, 64, com, bias=[t[:] for t in bk],
              bias_key="bk", evac_func="bias_add", chunk=64)
    vS = p.sbuf("vS", [64, D], BF16)
    bvb = bcast_row(p, "bvb", bvr[:], "bvr", D, com)
    for k in range(KT):
        p.dma("sync", wbuf[k][:], wv_d[k * 128:(k + 1) * 128, :], writes=(f"wb{k}",))
    for c0 in range(0, D, 512):
        ps = com.pool.get()
        for k in range(KT):
            p.mm(ps[0:64, :], summ[k][:, 0:64], wbuf[k][:, c0:c0 + 512],
                 start=(k == 0), stop=(k == KT - 1),
                 reads=(tkey(summ[k]), f"wb{k}"), writes=(tkey(ps),),
                 group=f"vs{c0}")
        p.t_op("vector", "tensor_tensor", vS[:, c0:c0 + 512], ps[0:64, :],
               bvb[0:64, c0:c0 + 512], ALU.add,
               reads=(tkey(ps), tkey(bvb)), writes=("vS",))
    ident = load_ident(p)
    xt = [p.sbuf(f"xh{k}", [128, TQ], F32) for k in range(KT)]
    mskt = [p.sbuf(f"msk{t}", [128, 64], BF16) for t in range(TQ // 128)]
    sc = p.sbuf("scb", [128, 64], F32)
    eP = p.sbuf("ePb", [128, 64], BF16)
    rmx = p.sbuf("rmx", [128, 1], F32)
    dnn = p.sbuf("dnn", [128, 1], F32)
    psT = p.psum("psT", [64, 128], BF16)
    PT = p.sbuf("PTb", [64, 128], BF16)
    d_out = p.dram("xo", [D, TH], F32, kind="ExternalOutput")
    xkeys = [f"xh{k}" for k in range(KT)]
    for h0 in (0, TQ):
        for k in range(KT):
            p.dma("sync", xt[k][:], x_d[k * 128:(k + 1) * 128, h0:h0 + TQ],
                  writes=(xkeys[k],))
        for t in range(TQ // 128):
            p.dma("scalar", mskt[t][:], msk_d[h0 + t * 128:h0 + (t + 1) * 128, :],
                  writes=(f"msk{t}",))
        xn = ln_d(p, "ln", xt, TQ, com, out_dtype=BF16, x_keys=xkeys)
        q = proj(p, "q", xn, wq, D, TQ, com, bias=[t[:] for t in bq],
                 bias_key="bq", evac_func="bias_add")
        attn = xn
        for h in range(NH):
            qtile = q[h // 2]
            base = 64 * (h % 2)
            for qt in range(TQ // 128):
                ps = com.pool.get()
                p.mm(ps[:, 0:64], qtile[base:base + 64, qt * 128:(qt + 1) * 128],
                     kS[h // 2][base:base + 64, 0:64], start=True, stop=True,
                     reads=(tkey(qtile), tkey(kS[h // 2])), writes=(tkey(ps),))
                p.t_op("vector", "scalar_tensor_tensor", sc[:], ps[:, 0:64],
                       0.125, mskt[qt][:], ALU.mult, ALU.add,
                       reads=(tkey(ps), f"msk{qt}"), writes=("scb",))
                p.t_op("vector", "reduce_max", rmx[:], sc[:], AX.X,
                       reads=("scb",), writes=("rmx",))
                p.t_op("vector", "tensor_scalar", rmx[:], rmx[:], -1.0, None,
                       ALU.mult, reads=("rmx",), writes=("rmx",))
                p.t_op("scalar", "activation", eP[:], sc[:], AF.Exp,
                       bias=rmx[:], accum_out=dnn[:],
                       reads=("scb", "rmx"), writes=("ePb", "dnn"))
                p.t_op("vector", "reciprocal", dnn[:], dnn[:], reads=("dnn",),
                       writes=("dnn",))
                p.t_op("vector", "tensor_scalar", eP[:], eP[:], dnn[:], None,
                       ALU.mult, reads=("ePb", "dnn"), writes=("ePb",))
                p.transp(psT[:], eP[:], ident[:], reads=("ePb", "ident"),
                         writes=("psT",))
                p.t_op("vector", "tensor_copy", PT[:], psT[:], reads=("psT",),
                       writes=("PTb",))
                ps2 = com.pool.get()
                p.mm(ps2[0:64, 0:128], vS[:, h * 64:(h + 1) * 64], PT[:],
                     start=True, stop=True, reads=("vS", "PTb"),
                     writes=(tkey(ps2),))
                p.t_op("vector", "tensor_copy",
                       attn[h // 2][base:base + 64, qt * 128:(qt + 1) * 128],
                       ps2[0:64, 0:128], reads=(tkey(ps2),),
                       writes=(tkey(attn[h // 2]),))
        proj(p, "out", attn, wo, D, TQ, com, out_aps=[t[:] for t in xt],
             out_keys=xkeys, bias=[t[:] for t in bo], bias_key="bo",
             accum_into=xt, accum_keys=xkeys)
        for k in range(KT):
            p.dma("gpsimd", d_out[k * 128:(k + 1) * 128, h0:h0 + TQ],
                  xt[k][:], reads=(xkeys[k],))
    p.finish()
    return p.finalize()


def host_l1ab_b(state):
    inp = state["inp"]
    maps = []
    ilen = np.ones((1, 64), np.float32) / 64.0
    ilen[0, 0] = 1.0
    for c in range(8):
        b, hf = c // 2, c % 2
        p0 = state["l1ab_p"][2 * b]
        p1 = state["l1ab_p"][2 * b + 1]
        A = np.zeros((D, 64), np.float32)
        Bb = np.zeros((D, 64), np.float32)
        A[:, 0:33] = p0
        A[:, 33:64] = p1[:, 1:32]
        Bb[:, 32] = p1[:, 0]
        tglob = hf * TH + np.arange(TH)
        jmax = tglob // 64
        msk = np.where(np.arange(64)[None, :] <= jmax[:, None], 0.0, -30000.0)
        maps.append({
            "x": xshard(state["x"], c),
            "pa": A, "pb": Bb, "ilen": ilen,
            "msk": msk.astype(BF),
            "wq": as_bf(inp["L1_attn_b_Wq"]), "bq": as_f32(inp["L1_attn_b_bq"].reshape(-1, 1)),
            "wk": as_bf(inp["L1_attn_b_Wk"]), "bk": as_f32(inp["L1_attn_b_bk"].reshape(-1, 1)),
            "wv": as_bf(inp["L1_attn_b_Wv"]), "bv": as_f32(inp["L1_attn_b_bv"].reshape(1, -1)),
            "wo": as_bf(inp["L1_attn_b_Wo"]), "bo": as_f32(inp["L1_attn_b_bo"].reshape(-1, 1)),
            "ident": get_ident(),
        })
    return maps


def post_l1ab_b(state, results):
    state["x"] = unshard_x(results)


def build_l1aa_a():
    p = Prog()
    com = Common(p)
    TQ = 2048
    x_d = p.dram("x", [D, TH], BF16)
    xt = [p.sbuf(f"xq{k}", [128, TQ], BF16) for k in range(KT)]
    for k in range(KT):
        p.dma("sync", xt[k][:], x_d[k * 128:(k + 1) * 128, :], writes=(f"xq{k}",))
    wxdt = load_w(p, "wxdt", D, 64)
    bxdt = load_rows(p, "bxdt", 64)
    wdt = load_w(p, "wdt", 64, D)
    bdt = load_rows(p, "bdt", D)
    wB = load_w(p, "wB", D, 16)
    bB = load_rows(p, "bB", 16)
    wC = load_w(p, "wC", D, 16)
    bC = load_rows(p, "bC", 16)
    Dv = load_rows(p, "Dv", D)
    un = ln_d(p, "ln", xt, TQ, com, out_dtype=BF16, in_bf=True,
              x_keys=[f"xq{k}" for k in range(KT)])
    xdt = proj(p, "xdt", un, wxdt, 64, TQ, com, bias=[bxdt[0][0:64, :]],
               bias_key="bxdt", evac_func="bias_add")
    dtp = proj(p, "dtp", xdt, wdt, D, TQ, com, bias=[t[:] for t in bdt],
               bias_key="bdt", evac_func="bias_add", out_dtype=BF16,
               x_keys=[tkey(xdt[0])])
    dt_d = p.dram("dto", [D, TQ], BF16, kind="ExternalOutput")
    du_d = p.dram("dtu", [D, TQ], BF16, kind="ExternalOutput")
    Du_d = p.dram("Du", [D, TQ], F32, kind="ExternalOutput")
    t1 = p.sbuf("t1b", [128, TQ], F32)
    t2 = p.sbuf("t2b", [128, TQ], F32)
    dtb = p.sbuf("dtb", [128, TQ], BF16)
    for k in range(KT):
        dk = tkey(dtp[k])
        p.t_op("scalar", "activation", t1[:], dtp[k][:], AF.Abs,
               reads=(dk,), writes=("t1b",))
        p.t_op("scalar", "activation", t1[:], t1[:], AF.Exp, scale=-1.0,
               reads=("t1b",), writes=("t1b",))
        one, onek = com.const(1.0)
        p.t_op("scalar", "activation", t1[:], t1[:], AF.Ln, bias=one[:, 0:1],
               reads=("t1b", onek), writes=("t1b",))
        p.t_op("scalar", "activation", t2[:], dtp[k][:], AF.Relu,
               reads=(dk,), writes=("t2b",))
        p.t_op("vector", "tensor_tensor", dtb[:], t1[:], t2[:], ALU.add,
               reads=("t1b", "t2b"), writes=("dtbk",))
        p.dma("gpsimd", dt_d[k * 128:(k + 1) * 128, :], dtb[:], reads=("dtbk",))
        dub = p.sbuf("dub", [128, TQ], BF16)
        p.t_op("vector", "tensor_tensor", dub[:], dtb[:], un[k][:], ALU.mult,
               reads=("dtbk", tkey(un[k])), writes=("dub",))
        p.dma("sync", du_d[k * 128:(k + 1) * 128, :], dub[:], reads=("dub",))
        p.t_op("vector", "tensor_scalar", t2[:], un[k][:], Dv[k][:], None,
               ALU.mult, reads=(tkey(un[k]), "Dv"), writes=("t2b",))
        p.dma("scalar", Du_d[k * 128:(k + 1) * 128, :], t2[:], reads=("t2b",))
    Bo = proj(p, "Bo", un, wB, 16, TQ, com, bias=[bB[0][0:16, :]],
              bias_key="bB", evac_func="bias_add", out_dtype=F32, chunk=512)
    Co = proj(p, "Co", un, wC, 16, TQ, com, bias=[bC[0][0:16, :]],
              bias_key="bC", evac_func="bias_add", out_dtype=F32, chunk=512)
    B_d = p.dram("Bo_d", [16, TQ], F32, kind="ExternalOutput")
    C_d = p.dram("Co_d", [16, TQ], F32, kind="ExternalOutput")
    p.dma("gpsimd", B_d[:, :], Bo[0][0:16, :], reads=(tkey(Bo[0]),))
    p.dma("gpsimd", C_d[:, :], Co[0][0:16, :], reads=(tkey(Co[0]),))
    p.finish()
    return p.finalize()


def host_l1aa_a(state):
    inp = state["inp"]
    maps = []
    for c in range(8):
        maps.append({
            "x": as_bf(xshard(state["x"], c)),
            "wxdt": as_bf(inp["L1_attn_a_W_xdt"].T),
            "bxdt": as_f32(inp["L1_attn_a_b_xdt"].reshape(-1, 1)),
            "wdt": as_bf(inp["L1_attn_a_W_dt"].T),
            "bdt": as_f32(inp["L1_attn_a_b_dt"].reshape(-1, 1)),
            "wB": as_bf(inp["L1_attn_a_W_B"].T),
            "bB": as_f32(inp["L1_attn_a_b_B"].reshape(-1, 1)),
            "wC": as_bf(inp["L1_attn_a_W_C"].T),
            "bC": as_f32(inp["L1_attn_a_b_C"].reshape(-1, 1)),
            "Dv": as_f32(inp["L1_attn_a_D"].reshape(-1, 1)),
        })
    return maps


def _pad128(v):
    out = np.zeros((128, 1), np.float32)
    out[0:v.shape[0], 0] = v
    return out


def post_l1aa_a(state, results):
    for nm, key, dt_ in (("dto", "l1aa_dt", BF), ("dtu", "l1aa_dtu", BF),
                         ("Du", "l1aa_Du", np.float32), ("Bo_d", "l1aa_B", np.float32),
                         ("Co_d", "l1aa_C", np.float32)):
        rows = results[0][nm].shape[0]
        full = np.zeros((NB, rows, TFULL), dt_)
        for c in range(8):
            b, h = c // 2, c % 2
            full[b, :, h * TH:(h + 1) * TH] = results[c][nm]
        state[key] = full


def build_l1aa_b():
    p = Prog()
    com = Common(p)
    NL = 512
    TQ = 2048
    A_d = p.dram("Al", [NL, 16], F32)
    dt_d = p.dram("dt", [NL, TFULL], BF16)
    du_d = p.dram("dtu", [NL, TFULL], BF16)
    Du_d = p.dram("Du", [NL, TFULL], F32)
    B_d = p.dram("B", [16, TFULL], F32)
    C_d = p.dram("C", [16, TFULL], F32)
    y_d = p.dram("yo", [NL, TFULL], F32, kind="ExternalOutput")
    At = p.sbuf("At", [128, 4 * 16], F32)
    for lt in range(4):
        p.dma("sync", At[:, lt * 16:(lt + 1) * 16],
              A_d[lt * 128:(lt + 1) * 128, :], writes=("At",))
    p.t_op("scalar", "activation", At[:], At[:], AF.Exp, reads=("At",),
           writes=("At",))
    p.t_op("vector", "tensor_scalar", At[:], At[:], -1.0, None, ALU.mult,
           reads=("At",), writes=("At",))
    Br = p.sbuf("Br", [1, TQ], F32)
    Cr = p.sbuf("Cr", [1, TQ], F32)
    dt = [p.sbuf(f"dt{lt}", [128, TQ], BF16) for lt in range(4)]
    du = [p.sbuf(f"du{lt}", [128, TQ], BF16) for lt in range(4)]
    y = [p.sbuf(f"y{lt}", [128, TQ], F32) for lt in range(4)]
    ab = p.sbuf("abuf", [128, TQ], F32)
    hb = p.sbuf("hbuf", [128, TQ], F32)
    xb = p.sbuf("xbuf", [128, TQ], BF16)
    Bb = p.sbuf("Bb", [128, TQ], BF16)
    Cb = p.sbuf("Cb", [128, TQ], BF16)
    carry = [p.sbuf(f"cr{lt}", [128, 16], F32) for lt in range(4)]
    for lt in range(4):
        p.t_op("vector", "memset", carry[lt][:], 0.0, writes=(f"cr{lt}",))
    for h0 in (0, TQ):
        for lt in range(4):
            r = slice(lt * 128, (lt + 1) * 128)
            p.dma("sync", dt[lt][:], dt_d[r, h0:h0 + TQ], writes=(f"dt{lt}",))
            p.dma("scalar", du[lt][:], du_d[r, h0:h0 + TQ], writes=(f"du{lt}",))
            p.dma("gpsimd", y[lt][:], Du_d[r, h0:h0 + TQ], writes=(f"y{lt}",))
        for n in range(16):
            p.dma("sync", Br[:], B_d[n:n + 1, h0:h0 + TQ], writes=("Br",))
            p.dma("sync", Cr[:], C_d[n:n + 1, h0:h0 + TQ], writes=("Cr",))
            for c0 in range(0, TQ, 512):
                ps = com.pool.get()
                p.mm(ps[:, :], com.ones_f[:], Br[:, c0:c0 + 512], start=True,
                     stop=True, reads=("c_ones_f", "Br"), writes=(tkey(ps),))
                p.t_op("vector", "tensor_copy", Bb[:, c0:c0 + 512], ps[:, :],
                       reads=(tkey(ps),), writes=("Bb",))
                ps2 = com.pool.get()
                p.mm(ps2[:, :], com.ones_f[:], Cr[:, c0:c0 + 512], start=True,
                     stop=True, reads=("c_ones_f", "Cr"), writes=(tkey(ps2),))
                p.t_op("vector", "tensor_copy", Cb[:, c0:c0 + 512], ps2[:, :],
                       reads=(tkey(ps2),), writes=("Cb",))
            for lt in range(4):
                p.t_op("scalar", "activation", ab[:], dt[lt][:], AF.Exp,
                       scale=At[:, lt * 16 + n:lt * 16 + n + 1],
                       reads=(f"dt{lt}", "At"), writes=("abuf",))
                p.t_op("gpsimd", "tensor_tensor", xb[:], du[lt][:], Bb[:],
                       ALU.mult, reads=(f"du{lt}", "Bb"), writes=("xbuf",))
                eng = "vector" if lt % 2 == 0 else "gpsimd"
                p.t_op(eng, "tensor_tensor_scan", hb[:], ab[:], xb[:],
                       carry[lt][:, n:n + 1], ALU.mult, ALU.add,
                       reads=("abuf", "xbuf", f"cr{lt}"), writes=("hbuf",))
                p.t_op("vector", "tensor_copy", carry[lt][:, n:n + 1],
                       hb[:, TQ - 1:TQ], reads=("hbuf",), writes=(f"cr{lt}",))
                p.t_op("gpsimd", "tensor_tensor", hb[:], hb[:], Cb[:], ALU.mult,
                       reads=("hbuf", "Cb"), writes=("hbuf",))
                p.t_op("vector", "tensor_tensor", y[lt][:], y[lt][:], hb[:],
                       ALU.add, reads=(f"y{lt}", "hbuf"), writes=(f"y{lt}",))
        for lt in range(4):
            p.dma("gpsimd", y_d[lt * 128:(lt + 1) * 128, h0:h0 + TQ],
                  y[lt][:], reads=(f"y{lt}",))
    p.finish()
    return p.finalize()


def host_l1aa_b(state):
    inp = state["inp"]
    maps = []
    for c in range(8):
        b, h = c // 2, c % 2
        ch = slice(h * 512, (h + 1) * 512)
        maps.append({
            "Al": as_f32(inp["L1_attn_a_A_log"][ch]),
            "dt": state["l1aa_dt"][b][ch],
            "dtu": np.ascontiguousarray(state["l1aa_dtu"][b][ch]),
            "Du": as_f32(state["l1aa_Du"][b][ch]),
            "B": as_f32(state["l1aa_B"][b]),
            "C": as_f32(state["l1aa_C"][b]),
        })
    return maps


def post_l1aa_b(state, results):
    full = np.zeros((NB, D, TFULL), np.float32)
    for c in range(8):
        b, h = c // 2, c % 2
        full[b, h * 512:(h + 1) * 512, :] = results[c]["yo"]
    state["l1aa_y"] = full


def cheb_slot(bufs3, m):
    return bufs3[(m - 1) % 3]


def build_l0ab_a():
    p = Prog()
    com = Common(p)
    TQ = 512
    SC = 0.125
    x_d = p.dram("x", [D, TH], BF16)
    wq_d = p.dram("wq", [D, D], BF16)
    wk_d = p.dram("wk", [D, D], BF16)
    wv_d = p.dram("wv", [D, D], BF16)
    wbuf = [p.sbuf(f"wb{k}", [128, D], BF16) for k in range(KT)]
    wq = wk = wv = wbuf
    qo_d = p.dram("qo", [D, TH], BF16, kind="ExternalOutput")
    kvp_d = p.dram("kvp", [11 * 64, D], F32, kind="ExternalOutput")
    xt = [p.sbuf(f"xq{k}", [128, TQ], BF16) for k in range(KT)]
    ktm = [p.sbuf(f"ktm{t}", [128, D], BF16) for t in range(TQ // 128)]
    vtm = [p.sbuf(f"vtm{t}", [128, D], BF16) for t in range(TQ // 128)]
    T0 = p.sbuf("T0", [128, D], BF16)
    p.t_op("vector", "memset", T0[:], 1.0, writes=("T0",))
    Tb = [[p.sbuf(f"Tb{i}_{t}", [128, D], BF16) for t in range(TQ // 128)]
          for i in range(3)]
    kv_acc = [p.sbuf(f"kva{j}", [128, D], F32) for j in range(6)]
    kvps = p.psum("kvps", [64, D], F32)
    qb = p.sbuf("qb", [128, TQ], BF16)
    qf = p.sbuf("qf", [128, TQ], F32)
    for h0 in (0, 512, 1024, 1536):
        for k in range(KT):
            p.dma("sync", xt[k][:], x_d[k * 128:(k + 1) * 128, h0:h0 + TQ],
                  writes=(f"xq{k}",))
        xn = ln_d(p, "ln", xt, TQ, com, out_dtype=BF16, in_bf=True,
                  x_keys=[f"xq{k}" for k in range(KT)])
        for k in range(KT):
            p.dma("sync", wbuf[k][:], wq_d[k * 128:(k + 1) * 128, :],
                  writes=(f"wb{k}",))
        for m in range(KT):
            for c0 in range(0, TQ, 512):
                ps = com.pool.get()
                for kk in range(KT):
                    p.mm(ps[:, :], wq[kk][:, m * 128:(m + 1) * 128],
                         xn[kk][:, c0:c0 + 512], start=(kk == 0),
                         stop=(kk == KT - 1), reads=(f"wb{kk}", tkey(xn[kk])),
                         writes=(tkey(ps),), group=f"q{h0}_{m}_{c0}")
                p.t_op("vector", "tensor_scalar", qf[:, c0:c0 + 512], ps[:, :],
                       SC, 1.0, ALU.mult, ALU.min, reads=(tkey(ps),),
                       writes=("qf",))
            p.t_op("vector", "tensor_scalar", qb[:], qf[:], -1.0, None,
                   ALU.max, reads=("qf",), writes=("qb",))
            p.dma("gpsimd", qo_d[m * 128:(m + 1) * 128, h0:h0 + TQ], qb[:],
                  reads=("qb",))
        for (wd_, wn, dst, clip) in ((wk_d, "wk", ktm, True), (wv_d, "wv", vtm, False)):
            for k in range(KT):
                p.dma("sync", wbuf[k][:], wd_[k * 128:(k + 1) * 128, :],
                      writes=(f"wb{k}",))
            for tt in range(TQ // 128):
                for c0 in range(0, D, 512):
                    ps = com.pool.get()
                    for kk in range(KT):
                        p.mm(ps[:, :], xn[kk][:, tt * 128:(tt + 1) * 128],
                             wbuf[kk][:, c0:c0 + 512], start=(kk == 0),
                             stop=(kk == KT - 1),
                             reads=(tkey(xn[kk]), f"wb{kk}"),
                             writes=(tkey(ps),), group=f"{wn}{h0}_{tt}_{c0}")
                    if clip:
                        p.t_op("vector", "tensor_scalar",
                               dst[tt][:, c0:c0 + 512], ps[:, :], SC, 1.0,
                               ALU.mult, ALU.min, reads=(tkey(ps),),
                               writes=(tkey(dst[tt]),))
                        p.t_op("vector", "tensor_scalar",
                               dst[tt][:, c0:c0 + 512], dst[tt][:, c0:c0 + 512],
                               -1.0, None, ALU.max, reads=(tkey(dst[tt]),),
                               writes=(tkey(dst[tt]),))
                    else:
                        p.t_op("vector", "tensor_copy",
                               dst[tt][:, c0:c0 + 512], ps[:, :],
                               reads=(tkey(ps),), writes=(tkey(dst[tt]),))
        for m in range(11):
            if m == 1:
                for t in range(TQ // 128):
                    p.t_op("vector", "tensor_copy", cheb_slot(Tb, 1)[t][:],
                           ktm[t][:], reads=(tkey(ktm[t]),),
                           writes=(tkey(cheb_slot(Tb, 1)[t]),))
            elif m >= 2:
                for t in range(TQ // 128):
                    e = ["vector", "gpsimd"][t % 2]
                    dstt = cheb_slot(Tb, m)[t]
                    p1t = cheb_slot(Tb, m - 1)[t]
                    p2t = T0 if m == 2 else cheb_slot(Tb, m - 2)[t]
                    tbk = p.sbuf(f"chb{t % 2}", [128, D], F32)
                    p.t_op(e, "tensor_tensor", tbk[:], ktm[t][:], p1t[:],
                           ALU.mult, reads=(tkey(ktm[t]), tkey(p1t)),
                           writes=(tkey(tbk),))
                    p.t_op(e, "scalar_tensor_tensor", dstt[:], tbk[:], 2.0,
                           p2t[:], ALU.mult, ALU.subtract,
                           reads=(tkey(tbk), tkey(p2t)), writes=(tkey(dstt),))
            for h in range(16):
                for tt in range(TQ // 128):
                    src_t = T0 if m == 0 else cheb_slot(Tb, m)[tt]
                    p.mm(kvps[:, h * 64:(h + 1) * 64],
                         src_t[:, h * 64:(h + 1) * 64],
                         vtm[tt][:, h * 64:(h + 1) * 64],
                         start=(tt == 0), stop=(tt == TQ // 128 - 1),
                         reads=(tkey(src_t), tkey(vtm[tt])), writes=("kvps",),
                         group=f"kv{h0}_{m}_{h}")
            jt, jo = (m * 64) // 128, (m * 64) % 128
            if h0 == 0:
                p.t_op("vector", "tensor_copy", kv_acc[jt][jo:jo + 64, :],
                       kvps[:, :], reads=("kvps",), writes=(f"kva{jt}",))
            else:
                p.t_op("vector", "tensor_tensor", kv_acc[jt][jo:jo + 64, :],
                       kv_acc[jt][jo:jo + 64, :], kvps[:, :], ALU.add,
                       reads=(f"kva{jt}", "kvps"), writes=(f"kva{jt}",))
    for j in range(6):
        r = 128 if j < 5 else 64
        p.dma("gpsimd", kvp_d[j * 128:j * 128 + r, :], kv_acc[j][0:r, :],
              reads=(f"kva{j}",))
    p.finish()
    return p.finalize()


def host_l0ab_a(state):
    inp = state["inp"]
    Wi = inp["L0_attn_b_w_in"]
    maps = []
    for c in range(8):
        maps.append({
            "x": as_bf(xshard(state["x"], c)),
            "wq": as_bf(Wi[0:1024].T), "wk": as_bf(Wi[1024:2048].T),
            "wv": as_bf(Wi[2048:3072].T),
        })
    return maps


def post_l0ab_a(state, results):
    state["l0ab_q"] = [results[c]["qo"] for c in range(8)]
    state["l0ab_kv"] = [results[c]["kvp"] for c in range(8)]


def build_l0ab_b():
    p = Prog()
    com = Common(p)
    TQ = 1024
    x_d = p.dram("x", [D, TH], F32)
    q_d = p.dram("q", [D, TH], BF16)
    kv0_d = p.dram("kv0", [11 * 64, D], F32)
    kv1_d = p.dram("kv1", [11 * 64, D], F32)
    bp_d = p.dram("bpat", [11 * 64, D], BF16)
    wo = load_w(p, "wo", D, D)
    kvt = [p.sbuf(f"kvt{m}", [128, D], BF16) for m in range(11)]
    a_b = p.sbuf("kva", [128, D], F32)
    b_b = p.sbuf("kvb", [128, D], F32)
    bp_b = p.sbuf("bpb", [128, D], BF16)
    for m in range(11):
        for half in (0, 64):
            p.dma("sync", a_b[half:half + 64, :], kv0_d[m * 64:(m + 1) * 64, :],
                  writes=("kva",))
            p.dma("sync", b_b[half:half + 64, :], kv1_d[m * 64:(m + 1) * 64, :],
                  writes=("kvb",))
            p.dma("sync", bp_b[half:half + 64, :], bp_d[m * 64:(m + 1) * 64, :],
                  writes=("bpb",))
        p.t_op("vector", "tensor_add", a_b[:, :], a_b[:, :], b_b[:, :],
               reads=("kva", "kvb"), writes=("kva",))
        p.t_op("vector", "tensor_mul", kvt[m][:, :], a_b[:, :],
               bp_b[:, :], reads=("kva", "bpb"), writes=(tkey(kvt[m]),))
    qt = [p.sbuf(f"qt{k}", [128, TQ], BF16) for k in range(KT)]
    xt = [p.sbuf(f"xh{k}", [128, TQ], F32) for k in range(KT)]
    xkeys = [f"xh{k}" for k in range(KT)]
    T0 = p.sbuf("T0", [128, TQ], BF16)
    p.t_op("vector", "memset", T0[:], 1.0, writes=("T0",))
    Tb = [[p.sbuf(f"Tq{i}_{k}", [128, TQ], BF16) for k in range(KT)]
          for i in range(3)]
    tb2 = [p.sbuf(f"tb2_{i}", [128, TQ], F32) for i in range(2)]
    attf = [p.sbuf(f"af{k}", [128, TQ], F32) for k in range(KT)]
    attn = [p.sbuf(f"at{k}", [128, TQ], BF16) for k in range(KT)]
    d_out = p.dram("xo", [D, TH], F32, kind="ExternalOutput")
    for h0 in (0, TQ):
        for k in range(KT):
            p.dma("sync", qt[k][:], q_d[k * 128:(k + 1) * 128, h0:h0 + TQ],
                  writes=(f"qt{k}",))
            p.dma("scalar", xt[k][:], x_d[k * 128:(k + 1) * 128, h0:h0 + TQ],
                  writes=(xkeys[k],))
        for m in range(11):
            if m == 1:
                for k in range(KT):
                    p.t_op("vector", "tensor_copy", cheb_slot(Tb, 1)[k][:],
                           qt[k][:], reads=(f"qt{k}",),
                           writes=(tkey(cheb_slot(Tb, 1)[k]),))
            elif m >= 2:
                for k in range(KT):
                    e = ["vector", "gpsimd"][k % 2]
                    dstt = cheb_slot(Tb, m)[k]
                    p1t = cheb_slot(Tb, m - 1)[k]
                    tbk = tb2[k % 2]
                    p.t_op(e, "tensor_tensor", tbk[:], qt[k][:], p1t[:],
                           ALU.mult, reads=(f"qt{k}", tkey(p1t)),
                           writes=(tkey(tbk),))
                    if m == 2:
                        p.t_op(e, "tensor_scalar", dstt[:], tbk[:], 2.0, -1.0,
                               ALU.mult, ALU.add, reads=(tkey(tbk),),
                               writes=(tkey(dstt),))
                    else:
                        p2t = cheb_slot(Tb, m - 2)[k]
                        p.t_op(e, "scalar_tensor_tensor", dstt[:], tbk[:], 2.0,
                               p2t[:], ALU.mult, ALU.subtract,
                               reads=(tkey(tbk), tkey(p2t)), writes=(tkey(dstt),))
            for h in range(16):
                if m == 0:
                    src_t, base, sk = T0, 64 * (h % 2), "T0"
                else:
                    src_t = cheb_slot(Tb, m)[h // 2]
                    base = 64 * (h % 2)
                    sk = tkey(src_t)
                for qc in range(TQ // 512):
                    ps = com.pool.get()
                    p.mm(ps[base:base + 64, :],
                         kvt[m][base:base + 64, h * 64:(h + 1) * 64],
                         src_t[base:base + 64, qc * 512:(qc + 1) * 512],
                         start=True, stop=True,
                         reads=(tkey(kvt[m]), sk), writes=(tkey(ps),))
                    hb = 64 * (h % 2)
                    if m == 0:
                        p.t_op("vector", "tensor_copy",
                               attf[h // 2][hb:hb + 64, qc * 512:(qc + 1) * 512],
                               ps[base:base + 64, :], reads=(tkey(ps),),
                               writes=(tkey(attf[h // 2]),))
                    else:
                        p.t_op("vector", "tensor_tensor",
                               attf[h // 2][hb:hb + 64, qc * 512:(qc + 1) * 512],
                               attf[h // 2][hb:hb + 64, qc * 512:(qc + 1) * 512],
                               ps[base:base + 64, :], ALU.add,
                               reads=(tkey(attf[h // 2]), tkey(ps)),
                               writes=(tkey(attf[h // 2]),))
        for k in range(KT):
            p.t_op("vector", "tensor_copy", attn[k][:], attf[k][:],
                   reads=(tkey(attf[k]),), writes=(tkey(attn[k]),))
        proj(p, "out", attn, wo, D, TQ, com, out_aps=[t[:] for t in xt],
             out_keys=xkeys, accum_into=xt, accum_keys=xkeys)
        for k in range(KT):
            p.dma("gpsimd", d_out[k * 128:(k + 1) * 128, h0:h0 + TQ],
                  xt[k][:], reads=(xkeys[k],))
    p.finish()
    return p.finalize()


def host_l0ab_b(state):
    inp = state["inp"]
    beta = inp["L0_attn_b_beta"]
    bpat = np.zeros((704, D), np.float32)
    for m in range(11):
        for h in range(16):
            bpat[m * 64:(m + 1) * 64, h * 64:(h + 1) * 64] = beta[h, m]
    maps = []
    for c in range(8):
        b, hf = c // 2, c % 2
        maps.append({
            "x": xshard(state["x"], c),
            "q": state["l0ab_q"][c],
            "kv0": state["l0ab_kv"][2 * b], "kv1": state["l0ab_kv"][2 * b + 1],
            "bpat": bpat.astype(BF),
            "wo": as_bf(inp["L0_attn_b_w_out"].T),
        })
    return maps


def post_l0ab_b(state, results):
    state["x"] = unshard_x(results)


def build_l2ab_a():
    p = Prog()
    com = Common(p)
    TQ = 1024
    x_d = p.dram("x", [D, TH], BF16)
    wq = load_w(p, "wq", D, D)
    wk = load_w(p, "wk", D, D)
    wg = load_w(p, "wg", D, D)
    wv_d = p.dram("wv", [D, D], BF16)
    n1g = load_rows(p, "n1g", D)
    n1b = load_rows(p, "n1b", D)
    bgr = load_rows(p, "bg", D)
    q_o = p.dram("qo", [D, TH], BF16, kind="ExternalOutput")
    k_o = p.dram("ko", [D, TH], BF16, kind="ExternalOutput")
    v_o = p.dram("vo", [TH, D], BF16, kind="ExternalOutput")
    g_o = p.dram("go", [D, TH], BF16, kind="ExternalOutput")
    xt = [p.sbuf(f"xq{k}", [128, TQ], BF16) for k in range(KT)]
    wvb = [p.sbuf(f"wvb{k}", [128, D], BF16) for k in range(KT)]
    for k in range(KT):
        p.dma("scalar", wvb[k][:], wv_d[k * 128:(k + 1) * 128, :],
              writes=(f"wvb{k}",))
    vbuf = p.sbuf("vbuf", [128, 512], BF16)
    for h0 in (0, TQ):
        for k in range(KT):
            p.dma("sync", xt[k][:], x_d[k * 128:(k + 1) * 128, h0:h0 + TQ],
                  writes=(f"xq{k}",))
        h00 = ln_d(p, "h0", xt, TQ, com, out_dtype=BF16, in_bf=True,
                   x_keys=[f"xq{k}" for k in range(KT)])
        xn = ln_d(p, "xn", h00, TQ, com, g=[t[:] for t in n1g],
                  b=[t[:] for t in n1b], gb_key="n1g", out_dtype=BF16,
                  in_bf=True)
        for (w, nm, od, ef, bias, bk_) in (
                (wq, "q", q_o, None, None, None),
                (wk, "k", k_o, None, None, None),
                (wg, "g", g_o, AF.Silu, [t[:] for t in bgr], "bg")):
            ot = proj(p, f"p{nm}", xn, w, D, TQ, com, evac_func=ef,
                      bias=bias, bias_key=bk_)
            for m in range(KT):
                p.dma("gpsimd", od[m * 128:(m + 1) * 128, h0:h0 + TQ],
                      ot[m][:], reads=(tkey(ot[m]),))
        for tt in range(TQ // 128):
            for c0 in range(0, D, 512):
                ps = com.pool.get()
                for k in range(KT):
                    p.mm(ps[:, :], xn[k][:, tt * 128:(tt + 1) * 128],
                         wvb[k][:, c0:c0 + 512], start=(k == 0),
                         stop=(k == KT - 1), reads=(tkey(xn[k]), f"wvb{k}"),
                         writes=(tkey(ps),), group=f"v{h0}_{tt}_{c0}")
                p.t_op("vector", "tensor_copy", vbuf[:], ps[:, :],
                       reads=(tkey(ps),), writes=("vbuf",))
                p.dma("gpsimd", v_o[h0 + tt * 128:h0 + (tt + 1) * 128,
                                    c0:c0 + 512], vbuf[:], reads=("vbuf",))
    p.finish()
    return p.finalize()


def host_l2ab_a(state):
    inp = state["inp"]
    maps = []
    for c in range(8):
        maps.append({
            "x": as_bf(xshard(state["x"], c)),
            "wq": as_bf(inp["L2_attn_b_wq"]), "wk": as_bf(inp["L2_attn_b_wk"]),
            "wv": as_bf(inp["L2_attn_b_wv"]), "wg": as_bf(inp["L2_attn_b_wg"]),
            "n1g": as_f32(inp["L2_attn_b_norm1_w"].reshape(-1, 1)),
            "n1b": as_f32(inp["L2_attn_b_norm1_b"].reshape(-1, 1)),
            "bg": as_f32(inp["L2_attn_b_bg"].reshape(-1, 1)),
        })
    return maps


def post_l2ab_a(state, results):
    for nm, key in (("qo", "l2ab_q"), ("ko", "l2ab_k"), ("go", "l2ab_g")):
        full = np.zeros((NB, D, TFULL), BF)
        for c in range(8):
            b, h = c // 2, c % 2
            full[b, :, h * TH:(h + 1) * TH] = results[c][nm]
        state[key] = full
    fullv = np.zeros((NB, TFULL, D), BF)
    for c in range(8):
        b, h = c // 2, c % 2
        fullv[b, h * TH:(h + 1) * TH, :] = results[c]["vo"]
    state["l2ab_v"] = fullv


def build_l2ab_b():
    p = Prog()
    com = Common(p, npool=2)
    q_d = p.dram("q", [512, TFULL], BF16)
    kd_d = p.dram("kd", [512, TFULL], BF16)
    k_d = p.dram("kt", [TFULL, 512], BF16)
    v_d = p.dram("vt", [TFULL, 512], BF16)
    dm_d = p.dram("dmask", [128, 4 * 256], BF16)
    xi_d = p.dram("xi", [512, 128], BF16)
    xip_d = p.dram("xip", [128, 512], BF16)
    gc_d = p.dram("gc", [512, 1], F32)
    r_o = p.dram("ro", [512, TFULL], BF16, kind="ExternalOutput")
    qt = [p.sbuf(f"qt{j}", [128, TFULL], BF16) for j in range(4)]
    kdm = [p.sbuf(f"kdm{j}", [128, TFULL], BF16) for j in range(4)]
    for j in range(4):
        p.dma("sync", qt[j][:], q_d[j * 128:(j + 1) * 128, :], writes=(f"qt{j}",))
        p.dma("sync", kdm[j][:], kd_d[j * 128:(j + 1) * 128, :], writes=(f"kdm{j}",))
    ktm = [p.sbuf(f"ktm{t}", [128, 512], BF16) for t in range(32)]
    vtm = [p.sbuf(f"vtm{t}", [128, 512], BF16) for t in range(32)]
    for t in range(32):
        p.dma("scalar", ktm[t][:], k_d[t * 128:(t + 1) * 128, :], writes=(f"ktm{t}",))
        p.dma("scalar", vtm[t][:], v_d[t * 128:(t + 1) * 128, :], writes=(f"vtm{t}",))
    dmt = [p.sbuf(f"dmt{pr}", [128, 256], BF16) for pr in range(4)]
    xit = [p.sbuf(f"xit{j}", [128, 128], BF16) for j in range(4)]
    xipt = p.sbuf("xipt", [128, 512], BF16)
    gct = [p.sbuf(f"gct{j}", [128, 1], F32) for j in range(4)]
    p.dma("sync", xipt[:], xip_d[:, :], writes=("xipt",))
    for pr in range(4):
        p.dma("sync", dmt[pr][:], dm_d[:, pr * 256:(pr + 1) * 256],
              writes=(f"dmt{pr}",))
    for j in range(4):
        p.dma("sync", xit[j][:], xi_d[j * 128:(j + 1) * 128, :], writes=(f"xit{j}",))
        p.dma("sync", gct[j][:], gc_d[j * 128:(j + 1) * 128, :], writes=(f"gct{j}",))
    qx = p.sbuf("qx", [128, 128], BF16)
    S = [p.sbuf(f"S{pr}", [128, 64], F32) for pr in range(4)]
    Sb = [p.sbuf(f"Sb{pr}", [128, 64], BF16) for pr in range(4)]
    for pr in range(4):
        p.t_op("vector", "memset", S[pr][:], 0.0, writes=(f"S{pr}",))
        p.t_op("vector", "memset", Sb[pr][:], 0.0, writes=(f"Sb{pr}",))
    psS = p.psum("psS", [128, 256], F32)
    Ab = p.sbuf("Abu", [128, 256], BF16)
    kbb = p.sbuf("kbb", [128, 512], BF16)
    psO = [p.psum(f"psO{i}", [64, 128], F32) for i in range(2)]
    psKV = [p.psum(f"psKV{i}", [64, 64], F32) for i in range(2)]
    ret = [p.sbuf(f"ret{j}", [128, TFULL], BF16) for j in range(4)]
    for c in range(32):
        p.t_op("vector", "tensor_mul", kbb[:], ktm[c][:], xipt[:],
               reads=(f"ktm{c}", "xipt"), writes=("kbb",))
        for pr in range(4):
            p.t_op("gpsimd", "tensor_mul", qx[:],
                   qt[pr][:, c * 128:(c + 1) * 128], xit[pr][:],
                   reads=(f"qt{pr}", f"xit{pr}"), writes=("qx",))
            for hi, (base, off) in enumerate(((0, 0), (64, 128))):
                hh = 2 * pr + hi
                p.mm(psS[:, off:off + 128],
                     kdm[pr][base:base + 64, c * 128:(c + 1) * 128],
                     qt[pr][base:base + 64, c * 128:(c + 1) * 128],
                     start=True, stop=True,
                     reads=(f"kdm{pr}", f"qt{pr}"), writes=("psS",))
            p.t_op("vector", "tensor_tensor", Ab[:], psS[:, :], dmt[pr][:],
                   ALU.mult, reads=("psS", f"dmt{pr}"), writes=("Abu",))
            for hi, (base, off) in enumerate(((0, 0), (64, 128))):
                hh = 2 * pr + hi
                po = psO[hi]
                p.mm(po[:, :], vtm[c][:, hh * 64:(hh + 1) * 64],
                     Ab[:, off:off + 128], start=True, stop=False,
                     reads=(f"vtm{c}", "Abu"), writes=(f"psO{hi}",),
                     group=f"o{c}_{pr}_{hi}")
                p.mm(po[:, :], Sb[pr][base:base + 64, :],
                     qx[base:base + 64, :],
                     start=False, stop=True,
                     reads=(f"Sb{pr}", "qx"), writes=(f"psO{hi}",),
                     group=f"o{c}_{pr}_{hi}")
                p.t_op("vector", "tensor_copy",
                       ret[pr][base:base + 64, c * 128:(c + 1) * 128],
                       po[:, :], reads=(f"psO{hi}",), writes=(f"ret{pr}",))
                pk = psKV[hi]
                p.mm(pk[:, :], kbb[:, hh * 64:(hh + 1) * 64],
                     vtm[c][:, hh * 64:(hh + 1) * 64], start=True, stop=True,
                     reads=("kbb", f"vtm{c}"), writes=(f"psKV{hi}",))
            p.t_op("vector", "tensor_scalar", S[pr][:], S[pr][:], gct[pr][:],
                   None, ALU.mult, reads=(f"S{pr}", f"gct{pr}"),
                   writes=(f"S{pr}",))
            for hi, base in ((0, 0), (1, 64)):
                p.t_op("vector", "tensor_tensor", S[pr][base:base + 64, :],
                       S[pr][base:base + 64, :], psKV[hi][:, :], ALU.add,
                       reads=(f"S{pr}", f"psKV{hi}"), writes=(f"S{pr}",))
            p.t_op("vector", "tensor_copy", Sb[pr][:], S[pr][:],
                   reads=(f"S{pr}",), writes=(f"Sb{pr}",))
    for j in range(4):
        p.dma("gpsimd", r_o[j * 128:(j + 1) * 128, :], ret[j][:],
              reads=(f"ret{j}",))
    p.finish()
    return p.finalize()


def host_l2ab_b(state):
    maps = []
    for c in range(8):
        b, hf = c // 2, c % 2
        hsl = slice(hf * 512, (hf + 1) * 512)
        heads = range(hf * 8, (hf + 1) * 8)
        gam = 1.0 - 2.0 ** (-5.0 - np.arange(16))
        dm = np.zeros((128, 4 * 256), np.float32)
        xi = np.zeros((512, 128), np.float32)
        xip = np.zeros((128, 512), np.float32)
        gc = np.zeros((512, 1), np.float32)
        ml = np.arange(128)
        for i, h in enumerate(heads):
            g = gam[h]
            pr, hi = i // 2, i % 2
            mask = np.where(ml[None, :] >= ml[:, None],
                            0.125 * g ** (ml[None, :] - ml[:, None]), 0.0)
            dm[:, pr * 256 + hi * 128: pr * 256 + (hi + 1) * 128] = mask
            xi[i * 64:(i + 1) * 64, :] = 0.125 * (g ** (ml + 1.0))[None, :]
            xip[:, i * 64:(i + 1) * 64] = (g ** (127.0 - ml))[:, None]
            gc[i * 64:(i + 1) * 64, 0] = g ** 128.0
        maps.append({
            "q": np.ascontiguousarray(state["l2ab_q"][b][hsl]),
            "kd": np.ascontiguousarray(state["l2ab_k"][b][hsl]),
            "kt": np.ascontiguousarray(state["l2ab_k"][b].T[:, hsl]),
            "vt": np.ascontiguousarray(state["l2ab_v"][b][:, hsl]),
            "dmask": dm.astype(BF), "xi": xi.astype(BF),
            "xip": xip.astype(BF), "gc": gc,
        })
    return maps


def post_l2ab_b(state, results):
    full = np.zeros((NB, D, TFULL), BF)
    for c in range(8):
        b, hf = c // 2, c % 2
        full[b, hf * 512:(hf + 1) * 512, :] = results[c]["ro"]
    state["l2ab_ret"] = full


def build_l2ab_c():
    p = Prog()
    com = Common(p, npool=3)
    TQ = 512
    HID = 2730
    MT = (HID + 127) // 128
    x_d = p.dram("x", [D, TH], F32)
    r_d = p.dram("ret", [D, TH], BF16)
    g_d = p.dram("gate", [D, TH], BF16)
    wo = load_w(p, "wo", D, D)
    bo = load_rows(p, "bo", D)
    gnw = load_rows(p, "gnw", D)
    gnb = load_rows(p, "gnb", D)
    n2g = load_rows(p, "n2g", D)
    n2b = load_rows(p, "n2b", D)
    wu_d = p.dram("wu", [D, HID], BF16)
    wg2_d = p.dram("wg2", [D, HID], BF16)
    bu = load_rows(p, "bu", HID)
    bg2 = load_rows(p, "bg2", HID)
    wo2 = load_w(p, "wo2", HID, D)
    bo2 = load_rows(p, "bo2", D)
    ones2 = p.sbuf("ones2", [128, 2], BF16)
    i2 = p.sbuf("i2", [2, 128], F32)
    p.dma("sync", ones2[:], p.dram("ones2_d", [128, 2], BF16)[:, :], writes=("ones2",))
    p.dma("sync", i2[:], p.dram("i2_d", [2, 128], F32)[:, :], writes=("i2",))
    wus = [[p.sbuf(f"wus{i}_{k}", [128, 128], BF16) for k in range(KT)]
           for i in range(2)]
    wgs = [[p.sbuf(f"wgs{i}_{k}", [128, 128], BF16) for k in range(KT)]
           for i in range(2)]
    d_out = p.dram("xo", [D, TH], F32, kind="ExternalOutput")
    xt = [p.sbuf(f"xq{k}", [128, TQ], F32) for k in range(KT)]
    rt = [p.sbuf(f"rq{k}", [128, TQ], BF16) for k in range(KT)]
    gt = [p.sbuf(f"gq{k}", [128, TQ], BF16) for k in range(KT)]
    xkeys = [f"xq{k}" for k in range(KT)]
    sums = p.sbuf("gsum", [2, TQ], F32)
    sumq = p.sbuf("gsq", [2, TQ], F32)
    sqb = p.sbuf("sqb", [128, 512], BF16)
    ubuf = p.sbuf("ubuf", [128, TQ], BF16)
    gbuf = p.sbuf("gbuf", [128, TQ], BF16)
    mbuf = [p.sbuf(f"mbuf{j}", [128, TQ], BF16) for j in range(MT)]
    for q0 in range(0, TH, TQ):
        for k in range(KT):
            p.dma("sync", xt[k][:], x_d[k * 128:(k + 1) * 128, q0:q0 + TQ],
                  writes=(xkeys[k],))
            p.dma("sync", rt[k][:], r_d[k * 128:(k + 1) * 128, q0:q0 + TQ],
                  writes=(f"rq{k}",))
            p.dma("sync", gt[k][:], g_d[k * 128:(k + 1) * 128, q0:q0 + TQ],
                  writes=(f"gq{k}",))
        h00 = ln_d(p, "h0", xt, TQ, com, out_dtype=F32, x_keys=xkeys)
        for k in range(KT):
            p.t_op("vector", "tensor_add", xt[k][:], xt[k][:], h00[k][:],
                   reads=(xkeys[k], tkey(h00[k])), writes=(xkeys[k],))
        for k in range(KT):
            ps = com.ps_s if False else None
            pss = p.psum("gn_ps", [2, 512], F32)
            psq = p.psum("gn_psq", [2, 512], F32)
            p.mm(pss[:, :], ones2[:], rt[k][:], start=True, stop=True,
                 reads=("ones2", f"rq{k}"), writes=("gn_ps",))
            p.t_op("scalar", "activation", sqb[:], rt[k][:], AF.Square,
                   reads=(f"rq{k}",), writes=("sqb",))
            p.mm(psq[:, :], ones2[:], sqb[:], start=True, stop=True,
                 reads=("ones2", "sqb"), writes=("gn_psq",))
            p.t_op("vector", "tensor_scalar", sums[:], pss[:, :], 1.0 / 64, None,
                   ALU.mult, reads=("gn_ps",), writes=("gsum",))
            p.t_op("vector", "tensor_scalar", sumq[:], psq[:, :], 1.0 / 64, None,
                   ALU.mult, reads=("gn_psq",), writes=("gsq",))
            p.t_op("vector", "tensor_tensor", sumq[:], sumq[:], sums[:],
                   ALU.subtract, reads=("gsq", "gsum"), writes=("gsq",))                 if False else None
            tmp2 = p.sbuf("gtmp", [2, TQ], F32)
            p.t_op("scalar", "activation", tmp2[:], sums[:], AF.Square,
                   reads=("gsum",), writes=("gtmp",))
            p.t_op("vector", "tensor_tensor", sumq[:], sumq[:], tmp2[:],
                   ALU.subtract, reads=("gsq", "gtmp"), writes=("gsq",))
            epst, epsk = com.const(1e-5)
            p.t_op("scalar", "activation", tmp2[:], sumq[:], AF.Ln,
                   bias=epst[0:2, :], reads=("gsq", epsk), writes=("gtmp",))
            p.t_op("scalar", "activation", sumq[:], tmp2[:], AF.Exp, scale=-0.5,
                   reads=("gtmp",), writes=("gsq",))
            psb = com.pool.get()
            p.mm(psb[:, :], i2[:], sums[:], start=True, stop=True,
                 reads=("i2", "gsum"), writes=(tkey(psb),))
            mb2 = p.sbuf("gmb", [128, TQ], F32)
            p.t_op("vector", "tensor_copy", mb2[:], psb[:, :],
                   reads=(tkey(psb),), writes=("gmb",))
            psb2 = com.pool.get()
            p.mm(psb2[:, :], i2[:], sumq[:], start=True, stop=True,
                 reads=("i2", "gsq"), writes=(tkey(psb2),))
            rb2 = p.sbuf("grb", [128, TQ], F32)
            p.t_op("vector", "tensor_copy", rb2[:], psb2[:, :],
                   reads=(tkey(psb2),), writes=("grb",))
            p.t_op("vector", "tensor_sub", rt[k][:], rt[k][:], mb2[:],
                   reads=(f"rq{k}", "gmb"), writes=(f"rq{k}",))
            p.t_op("vector", "tensor_mul", rt[k][:], rt[k][:], rb2[:],
                   reads=(f"rq{k}", "grb"), writes=(f"rq{k}",))
            p.t_op("vector", "tensor_scalar", rt[k][:], rt[k][:], gnw[k][:],
                   gnb[k][:], ALU.mult, ALU.add, reads=(f"rq{k}", "gnw", "gnb"),
                   writes=(f"rq{k}",))
            p.t_op("vector", "tensor_mul", rt[k][:], rt[k][:], gt[k][:],
                   reads=(f"rq{k}", f"gq{k}"), writes=(f"rq{k}",))
        proj(p, "xs", rt, wo, D, TQ, com, out_aps=[t[:] for t in xt],
             out_keys=xkeys, bias=[t[:] for t in bo], bias_key="bo",
             accum_into=xt, accum_keys=xkeys)
        x2 = ln_d(p, "x2", xt, TQ, com, g=[t[:] for t in n2g],
                  b=[t[:] for t in n2b], gb_key="n2g", out_dtype=BF16,
                  x_keys=xkeys)
        for j in range(MT):
            mr = min(128, HID - j * 128)
            jb = j % 2
            for k in range(KT):
                p.dma("scalar", wus[jb][k][:, 0:mr],
                      wu_d[k * 128:(k + 1) * 128, j * 128:j * 128 + mr],
                      writes=(f"wus{jb}_{k}",))
                p.dma("sync", wgs[jb][k][:, 0:mr],
                      wg2_d[k * 128:(k + 1) * 128, j * 128:j * 128 + mr],
                      writes=(f"wgs{jb}_{k}",))
            psu = com.pool.get()
            psg = com.pool.get()
            for k in range(KT):
                p.mm(psu[0:mr, :], wus[jb][k][:, 0:mr],
                     x2[k][:], start=(k == 0), stop=(k == KT - 1),
                     reads=(f"wus{jb}_{k}", tkey(x2[k])), writes=(tkey(psu),),
                     group=f"u{q0}_{j}")
                p.mm(psg[0:mr, :], wgs[jb][k][:, 0:mr],
                     x2[k][:], start=(k == 0), stop=(k == KT - 1),
                     reads=(f"wgs{jb}_{k}", tkey(x2[k])), writes=(tkey(psg),),
                     group=f"g{q0}_{j}")
            p.t_op("scalar", "activation", ubuf[0:mr, :], psu[0:mr, :], AF.Silu,
                   bias=bu[j][0:mr, :], reads=(tkey(psu), "bu"), writes=("ubuf",))
            p.t_op("vector", "tensor_scalar", gbuf[0:mr, :], psg[0:mr, :],
                   bg2[j][0:mr, :], None, ALU.add, reads=(tkey(psg), "bg2"),
                   writes=("gbuf",))
            p.t_op("vector", "tensor_mul", mbuf[j][0:mr, :], ubuf[0:mr, :],
                   gbuf[0:mr, :], reads=("ubuf", "gbuf"), writes=(tkey(mbuf[j]),))
        proj(p, "fo", mbuf, wo2, D, TQ, com, out_aps=[t[:] for t in xt],
             out_keys=xkeys, bias=[t[:] for t in bo2], bias_key="bo2",
             accum_into=xt, accum_keys=xkeys)
        for k in range(KT):
            p.dma("gpsimd", d_out[k * 128:(k + 1) * 128, q0:q0 + TQ],
                  xt[k][:], reads=(xkeys[k],))
    p.finish()
    return p.finalize()


def host_l2ab_c(state):
    inp = state["inp"]
    ones2 = np.zeros((128, 2), np.float32)
    ones2[0:64, 0] = 1.0
    ones2[64:128, 1] = 1.0
    i2 = np.zeros((2, 128), np.float32)
    i2[0, 0:64] = 1.0
    i2[1, 64:128] = 1.0
    w_in = inp["L2_attn_b_w_in"]
    b_in = inp["L2_attn_b_b_in"]
    maps = []
    for c in range(8):
        b, hf = c // 2, c % 2
        maps.append({
            "x": xshard(state["x"], c),
            "ret": np.ascontiguousarray(state["l2ab_ret"][b][:, hf * TH:(hf + 1) * TH]),
            "gate": np.ascontiguousarray(state["l2ab_g"][b][:, hf * TH:(hf + 1) * TH]),
            "wo": as_bf(inp["L2_attn_b_wo"]),
            "bo": as_f32(inp["L2_attn_b_bo"].reshape(-1, 1)),
            "gnw": as_f32(inp["L2_attn_b_gn_w"].reshape(-1, 1)),
            "gnb": as_f32(inp["L2_attn_b_gn_b"].reshape(-1, 1)),
            "n2g": as_f32(inp["L2_attn_b_norm2_w"].reshape(-1, 1)),
            "n2b": as_f32(inp["L2_attn_b_norm2_b"].reshape(-1, 1)),
            "wu": as_bf(w_in[:, 0:2730]), "wg2": as_bf(w_in[:, 2730:5460]),
            "bu": as_f32(b_in[0:2730].reshape(-1, 1)),
            "bg2": as_f32(b_in[2730:5460].reshape(-1, 1)),
            "wo2": as_bf(inp["L2_attn_b_w_out"]),
            "bo2": as_f32(inp["L2_attn_b_b_out"].reshape(-1, 1)),
            "ones2_d": ones2.astype(BF), "i2_d": as_f32(i2),
        })
    return maps


def post_l2ab_c(state, results):
    state["x"] = unshard_x(results)


def build_l3ab_a():
    p = Prog()
    com = Common(p)
    TQ = 1024
    x_d = p.dram("x", [D, TH], BF16)
    wq = load_w(p, "wq", D, D)
    wk = load_w(p, "wk", D, D)
    wg = load_w(p, "wg", D, D)
    wv_d = p.dram("wv", [D, D], BF16)
    bgr = load_rows(p, "bg", D)
    q_o = p.dram("qo", [D, TH], BF16, kind="ExternalOutput")
    k_o = p.dram("ko", [D, TH], BF16, kind="ExternalOutput")
    v_o = p.dram("vo", [TH, D], BF16, kind="ExternalOutput")
    lg_o = p.dram("lgo", [D, TH], F32, kind="ExternalOutput")
    xt = [p.sbuf(f"xq{k}", [128, TQ], BF16) for k in range(KT)]
    wvb = [p.sbuf(f"wvb{k}", [128, D], BF16) for k in range(KT)]
    for k in range(KT):
        p.dma("scalar", wvb[k][:], wv_d[k * 128:(k + 1) * 128, :],
              writes=(f"wvb{k}",))
    vbuf = p.sbuf("vbuf", [128, 512], BF16)
    t1 = p.sbuf("t1g", [128, TQ], F32)
    t2 = p.sbuf("t2g", [128, TQ], F32)
    for h0 in (0, TQ):
        for k in range(KT):
            p.dma("sync", xt[k][:], x_d[k * 128:(k + 1) * 128, h0:h0 + TQ],
                  writes=(f"xq{k}",))
        xn = ln_d(p, "ln", xt, TQ, com, out_dtype=BF16, in_bf=True,
                  x_keys=[f"xq{k}" for k in range(KT)])
        for (w, nm, od) in ((wq, "q", q_o), (wk, "k", k_o)):
            ot = proj(p, f"p{nm}", xn, w, D, TQ, com)
            for m in range(KT):
                p.dma("gpsimd", od[m * 128:(m + 1) * 128, h0:h0 + TQ],
                      ot[m][:], reads=(tkey(ot[m]),))
        gp = proj(p, "pg", xn, wg, D, TQ, com, bias=[t[:] for t in bgr],
                  bias_key="bg", evac_func="bias_add", out_dtype=F32)
        one, onek = com.const(1.0)
        for m in range(KT):
            gk = tkey(gp[m])
            p.t_op("scalar", "activation", t1[:], gp[m][:], AF.Abs,
                   reads=(gk,), writes=("t1g",))
            p.t_op("scalar", "activation", t1[:], t1[:], AF.Exp, scale=-1.0,
                   reads=("t1g",), writes=("t1g",))
            p.t_op("scalar", "activation", t1[:], t1[:], AF.Ln, bias=one[:, 0:1],
                   reads=("t1g", onek), writes=("t1g",))
            p.t_op("vector", "tensor_scalar", t2[:], gp[m][:], 0.0, None,
                   ALU.min, reads=(gk,), writes=("t2g",))
            p.t_op("vector", "tensor_sub", t2[:], t2[:], t1[:],
                   reads=("t2g", "t1g"), writes=("t2g",))
            p.dma("gpsimd", lg_o[m * 128:(m + 1) * 128, h0:h0 + TQ], t2[:],
                  reads=("t2g",))
        for tt in range(TQ // 128):
            for c0 in range(0, D, 512):
                ps = com.pool.get()
                for k in range(KT):
                    p.mm(ps[:, :], xn[k][:, tt * 128:(tt + 1) * 128],
                         wvb[k][:, c0:c0 + 512], start=(k == 0),
                         stop=(k == KT - 1), reads=(tkey(xn[k]), f"wvb{k}"),
                         writes=(tkey(ps),), group=f"v{h0}_{tt}_{c0}")
                p.t_op("vector", "tensor_copy", vbuf[:], ps[:, :],
                       reads=(tkey(ps),), writes=("vbuf",))
                p.dma("gpsimd", v_o[h0 + tt * 128:h0 + (tt + 1) * 128,
                                    c0:c0 + 512], vbuf[:], reads=("vbuf",))
    p.finish()
    return p.finalize()


def host_l3ab_a(state):
    inp = state["inp"]
    maps = []
    for c in range(8):
        maps.append({
            "x": as_bf(xshard(state["x"], c)),
            "wq": as_bf(inp["L3_attn_b_Wq"]), "wk": as_bf(inp["L3_attn_b_Wk"]),
            "wv": as_bf(inp["L3_attn_b_Wv"]), "wg": as_bf(inp["L3_attn_b_Wg"]),
            "bg": as_f32(inp["L3_attn_b_bg"].reshape(-1, 1)),
        })
    return maps


def post_l3ab_a(state, results):
    for nm, key in (("qo", "l3ab_q"), ("ko", "l3ab_k")):
        full = np.zeros((NB, D, TFULL), BF)
        for c in range(8):
            b, h = c // 2, c % 2
            full[b, :, h * TH:(h + 1) * TH] = results[c][nm]
        state[key] = full
    full = np.zeros((NB, D, TFULL), np.float32)
    for c in range(8):
        b, h = c // 2, c % 2
        full[b, :, h * TH:(h + 1) * TH] = results[c]["lgo"]
    state["l3ab_lg"] = full
    fullv = np.zeros((NB, TFULL, D), BF)
    for c in range(8):
        b, h = c // 2, c % 2
        fullv[b, h * TH:(h + 1) * TH, :] = results[c]["vo"]
    state["l3ab_v"] = fullv


def build_l3ab_b():
    p = Prog()
    com = Common(p, npool=2)
    CW = 32
    NC_ = TFULL // CW
    q_d = p.dram("q", [512, TFULL], BF16)
    k_d = p.dram("k", [512, TFULL], BF16)
    lg_d = p.dram("lg", [512, TFULL], F32)
    v_d = p.dram("vch", [CW, NC_ * 512], BF16)
    mk_d = p.dram("shmask", [128, 5 * TFULL], BF16)
    tril_d = p.dram("trilT", [CW, CW], BF16)
    ident = load_ident(p)
    r_o = p.dram("ro", [512, TFULL], BF16, kind="ExternalOutput")
    qt = [p.sbuf(f"qt{j}", [128, TFULL], BF16) for j in range(4)]
    kt = [p.sbuf(f"kt{j}", [128, TFULL], BF16) for j in range(4)]
    lgb = p.sbuf("lgb", [128, TFULL], F32)
    for j in range(4):
        p.dma("sync", qt[j][:], q_d[j * 128:(j + 1) * 128, :], writes=(f"qt{j}",))
        p.dma("sync", kt[j][:], k_d[j * 128:(j + 1) * 128, :], writes=(f"kt{j}",))
    vg = [p.sbuf(f"vg{i}", [CW, 16 * 512], BF16) for i in range(2)]
    mskb = p.sbuf("mskb", [128, TFULL], BF16)
    trilT = p.sbuf("trilTs", [CW, CW], BF16)
    p.dma("sync", trilT[:], tril_d[:, :], writes=("trilTs",))
    tshb = p.sbuf("tshb", [128, TFULL], F32)
    gamt = [p.sbuf(f"gam{j}", [128, NC_], F32) for j in range(4)]
    enl = p.sbuf("enl", [128, TFULL], F32)
    for j in range(4):
        p.dma("scalar", lgb[:], lg_d[j * 128:(j + 1) * 128, :], writes=("lgb",))
        for si, s in enumerate((1, 2, 4, 8, 16)):
            p.dma("sync", mskb[:], mk_d[:, si * TFULL:(si + 1) * TFULL],
                  writes=("mskb",))
            p.t_op("vector", "tensor_mul", tshb[:, s:TFULL],
                   lgb[:, 0:TFULL - s], mskb[:, s:TFULL],
                   reads=("lgb", "mskb"), writes=("tshb",))
            p.t_op("vector", "tensor_tensor", lgb[:, s:TFULL],
                   lgb[:, s:TFULL], tshb[:, s:TFULL], ALU.add,
                   reads=("lgb", "tshb"), writes=("lgb",))
        p.t_op("vector", "tensor_scalar", lgb[:], lgb[:], -75.0, None,
               ALU.max, reads=("lgb",), writes=("lgb",))
        p.t_op("scalar", "activation", tshb[:], lgb[:], AF.Exp,
               reads=("lgb",), writes=("tshb",))
        p.t_op("vector", "tensor_mul", qt[j][:], qt[j][:], tshb[:],
               reads=(f"qt{j}", "tshb"), writes=(f"qt{j}",))
        p.t_op("vector", "tensor_copy", gamt[j][:],
               tshb[:, CW - 1:TFULL:CW], reads=("tshb",), writes=(f"gam{j}",))
        p.t_op("scalar", "activation", enl[:], lgb[:], AF.Exp, scale=-1.0,
               reads=("lgb",), writes=("enl",))
        p.t_op("vector", "tensor_mul", kt[j][:], kt[j][:], enl[:],
               reads=(f"kt{j}", "enl"), writes=(f"kt{j}",))
    S = [p.sbuf(f"S{pr}", [128, 64], F32) for pr in range(4)]
    Sb = [p.sbuf(f"Sb{pr}", [128, 64], BF16) for pr in range(4)]
    for pr in range(4):
        p.t_op("vector", "memset", S[pr][:], 0.0, writes=(f"S{pr}",))
        p.t_op("vector", "memset", Sb[pr][:], 0.0, writes=(f"Sb{pr}",))
    psS = p.psum("psS", [CW, 64], F32)
    Ab = p.sbuf("Abu", [CW, 64], BF16)
    psT = p.psum("psTg", [CW, 128], BF16)
    kbT = p.sbuf("kbT", [CW, 128], BF16)
    psO = [p.psum(f"psO{i}", [64, CW], F32) for i in range(2)]
    psKV = [p.psum(f"psKV{i}", [64, 64], F32) for i in range(2)]
    ret = [p.sbuf(f"ret{j}", [128, TFULL], BF16) for j in range(4)]
    for c in range(NC_):
        cs = slice(c * CW, (c + 1) * CW)
        gi = (c // 16) % 2
        if c % 16 == 0:
            p.dma("sync", vg[gi][:], v_d[:, c * 512:(c + 16) * 512],
                  writes=(f"vg{gi}",))
        vcol = (c % 16) * 512
        for pr in range(4):
            kbs = p.sbuf("kbs", [128, CW], BF16)
            p.t_op("gpsimd", "tensor_scalar", kbs[:], kt[pr][:, cs],
                   gamt[pr][:, c:c + 1], None, ALU.mult,
                   reads=(f"kt{pr}", f"gam{pr}"), writes=("kbs",))
            p.transp(psT[:, 0:128], kbs[:], ident[:],
                     reads=("kbs", "ident"), writes=("psTg",))
            p.t_op("vector", "tensor_copy", kbT[:], psT[:, 0:128],
                   reads=("psTg",), writes=("kbT",))
            for hi, base in ((0, 0), (1, 64)):
                hh = 2 * pr + hi
                p.mm(psS[:, hi * 32:(hi + 1) * 32],
                     kt[pr][base:base + 64, cs], qt[pr][base:base + 64, cs],
                     start=True, stop=True, reads=(f"kt{pr}", f"qt{pr}"),
                     writes=("psS",))
            p.t_op("vector", "tensor_tensor", Ab[:], psS[:, :],
                   trilT[:].rearrange("m t -> m t") if False else psS[:, :],
                   ALU.mult, reads=("psS",), writes=("Abu",)) \
                if False else None
            p.t_op("vector", "tensor_mul", Ab[:, 0:32], psS[:, 0:32],
                   trilT[:], reads=("psS", "trilTs"), writes=("Abu",))
            p.t_op("vector", "tensor_mul", Ab[:, 32:64], psS[:, 32:64],
                   trilT[:], reads=("psS", "trilTs"), writes=("Abu",))
            for hi, base in ((0, 0), (1, 64)):
                hh = 2 * pr + hi
                po = psO[hi]
                p.mm(po[:, :], vg[gi][:, vcol + hh * 64:vcol + (hh + 1) * 64],
                     Ab[:, hi * 32:(hi + 1) * 32], start=True, stop=False,
                     reads=(f"vg{gi}", "Abu"), writes=(f"psO{hi}",),
                     group=f"o{c}_{pr}_{hi}")
                p.mm(po[:, :], Sb[pr][base:base + 64, :],
                     qt[pr][base:base + 64, cs], start=False, stop=True,
                     reads=(f"Sb{pr}", f"qt{pr}"), writes=(f"psO{hi}",),
                     group=f"o{c}_{pr}_{hi}")
                p.t_op("vector", "tensor_copy", ret[pr][base:base + 64, cs],
                       po[:, :], reads=(f"psO{hi}",), writes=(f"ret{pr}",))
                pk = psKV[hi]
                p.mm(pk[:, :], kbT[:, base:base + 64],
                     vg[gi][:, vcol + hh * 64:vcol + (hh + 1) * 64],
                     start=True, stop=True, reads=("kbT", f"vg{gi}"),
                     writes=(f"psKV{hi}",))
            p.t_op("vector", "tensor_scalar", S[pr][:], S[pr][:],
                   gamt[pr][:, c:c + 1], None, ALU.mult,
                   reads=(f"S{pr}", f"gam{pr}"), writes=(f"S{pr}",))
            for hi, base in ((0, 0), (1, 64)):
                p.t_op("vector", "tensor_tensor", S[pr][base:base + 64, :],
                       S[pr][base:base + 64, :], psKV[hi][:, :], ALU.add,
                       reads=(f"S{pr}", f"psKV{hi}"), writes=(f"S{pr}",))
            p.t_op("vector", "tensor_copy", Sb[pr][:], S[pr][:],
                   reads=(f"S{pr}",), writes=(f"Sb{pr}",))
    for j in range(4):
        p.dma("gpsimd", r_o[j * 128:(j + 1) * 128, :], ret[j][:],
              reads=(f"ret{j}",))
    p.finish()
    return p.finalize()


def host_l3ab_b(state):
    maps = []
    CW = 32
    shmask = np.zeros((128, 5 * TFULL), np.float32)
    for si, s in enumerate((1, 2, 4, 8, 16)):
        m = ((np.arange(TFULL) % CW) >= s).astype(np.float32)
        shmask[:, si * TFULL:(si + 1) * TFULL] = m[None, :]
    tril = np.tril(np.ones((CW, CW), np.float32)).T
    for c in range(8):
        b, hf = c // 2, c % 2
        hsl = slice(hf * 512, (hf + 1) * 512)
        v = state["l3ab_v"][b][:, hsl].astype(BF)
        vch = np.ascontiguousarray(
            v.reshape(TFULL // CW, CW, 512).transpose(1, 0, 2).reshape(CW, -1))
        maps.append({
            "q": np.ascontiguousarray(state["l3ab_q"][b][hsl]),
            "k": np.ascontiguousarray(state["l3ab_k"][b][hsl]),
            "lg": as_f32(state["l3ab_lg"][b][hsl]),
            "vch": vch,
            "shmask": shmask.astype(BF),
            "trilT": tril.astype(BF),
            "ident": get_ident(),
        })
    return maps


def post_l3ab_b(state, results):
    full = np.zeros((NB, D, TFULL), BF)
    for c in range(8):
        b, hf = c // 2, c % 2
        full[b, hf * 512:(hf + 1) * 512, :] = results[c]["ro"]
    state["l3ab_ret"] = full


def build_l3aa_a():
    p = Prog()
    com = Common(p)
    TQ = 512
    x_d = p.dram("x", [D, TH], BF16)
    win = load_w(p, "win", D, D)
    bin_ = load_rows(p, "bin", D)
    wgt = load_w(p, "wgt", D, D)
    bgt = load_rows(p, "bgt", D)
    wB = load_w(p, "wB", D, D)
    wA = load_w(p, "wA", D, D)
    g_o = p.dram("go", [D, TH], BF16, kind="ExternalOutput")
    u2_o = p.dram("u2o", [D, TH], F32, kind="ExternalOutput")
    xt = [p.sbuf(f"xq{k}", [128, TQ], BF16) for k in range(KT)]
    for h0 in (0, 512, 1024, 1536):
        for k in range(KT):
            p.dma("sync", xt[k][:], x_d[k * 128:(k + 1) * 128, h0:h0 + TQ],
                  writes=(f"xq{k}",))
        xn = ln_d(p, "ln", xt, TQ, com, out_dtype=BF16, in_bf=True,
                  x_keys=[f"xq{k}" for k in range(KT)])
        inp = proj(p, "inp", xn, win, D, TQ, com, bias=[t[:] for t in bin_],
                   bias_key="bin", evac_func="bias_add", out_dtype=F32)
        inb = [p.sbuf(f"inb{m}", [128, TQ], BF16) for m in range(KT)]
        for m in range(KT):
            p.t_op("vector", "tensor_copy", inb[m][:], inp[m][:],
                   reads=(tkey(inp[m]),), writes=(tkey(inb[m]),))
        gt = proj(p, "g", inb, wgt, D, TQ, com, bias=[t[:] for t in bgt],
                  bias_key="bgt", evac_func=AF.Sigmoid, out_dtype=BF16)
        for m in range(KT):
            p.dma("gpsimd", g_o[m * 128:(m + 1) * 128, h0:h0 + TQ], gt[m][:],
                  reads=(tkey(gt[m]),))
        cc = [p.sbuf(f"cc{m}", [128, TQ], BF16) for m in range(KT)]
        for m in range(KT):
            e = ["vector", "gpsimd"][m % 2]
            p.t_op(e, "tensor_scalar", cc[m][:], gt[m][:], -1.0, 1.0,
                   ALU.mult, ALU.add, reads=(tkey(gt[m]),), writes=(tkey(cc[m]),))
            p.t_op(e, "tensor_mul", cc[m][:], cc[m][:], inb[m][:],
                   reads=(tkey(cc[m]), tkey(inb[m])), writes=(tkey(cc[m]),))
        u2 = proj(p, "u2", inb, wB, D, TQ, com, out_dtype=F32)
        proj(p, "u2b", cc, wA, D, TQ, com,
             out_aps=[t[:] for t in u2], out_keys=keys(u2),
             accum_into=u2, accum_keys=keys(u2))
        for m in range(KT):
            p.dma("gpsimd", u2_o[m * 128:(m + 1) * 128, h0:h0 + TQ], u2[m][:],
                  reads=(tkey(u2[m]),))
    p.finish()
    return p.finalize()


def host_l3aa_a(state):
    inp = state["inp"]
    maps = []
    for c in range(8):
        maps.append({
            "x": as_bf(xshard(state["x"], c)),
            "win": as_bf(inp["L3_attn_a_W_in"].T),
            "bin": as_f32(inp["L3_attn_a_b_in"].reshape(-1, 1)),
            "wgt": as_bf(inp["L3_attn_a_W_gate"].T),
            "bgt": as_f32(inp["L3_attn_a_b_gate"].reshape(-1, 1)),
            "wB": as_bf(inp["L3_attn_a_B"].T),
            "wA": as_bf(inp["L3_attn_a_A"].T),
        })
    return maps


def post_l3aa_a(state, results):
    for nm, key, dt_ in (("go", "l3aa_g", BF), ("u2o", "l3aa_u2", np.float32)):
        full = np.zeros((NB, D, TFULL), dt_)
        for c in range(8):
            b, h = c // 2, c % 2
            full[b, :, h * TH:(h + 1) * TH] = results[c][nm]
        state[key] = full


def build_l3aa_b(nsteps=1024):
    p = Prog()
    g_d = p.dram("gL", [128, nsteps * 32], BF16)
    u_d = p.dram("u2L", [128, nsteps * 32], F32)
    a_d = p.dram("at", [D, D], BF16)
    s0_d = p.dram("s0", [128, 32], F32)
    so_d = p.dram("so", [128, 32], F32, kind="ExternalOutput")
    st_d = p.dram("states", [128, nsteps * 32], BF16, kind="ExternalOutput")
    at = [p.sbuf(f"at{k}", [128, D], BF16) for k in range(KT)]
    for k in range(KT):
        p.dma("sync", at[k][:], a_d[k * 128:(k + 1) * 128, :], writes=(f"at{k}",))
    CH = min(128, nsteps)
    gb = [p.sbuf(f"gb{i}", [128, CH * 32], BF16) for i in range(2)]
    ub = [p.sbuf(f"ub{i}", [128, CH * 32], F32) for i in range(2)]
    sb = [p.sbuf(f"sb{i}", [128, CH * 32], BF16) for i in range(2)]
    sig = p.sbuf("sig", [128, 32], F32)
    p.dma("sync", sig[:], s0_d[:, :], writes=("sig",))
    blend = p.sbuf("blend", [128, 32], BF16)
    zbuf = p.sbuf("zbuf", [128, 32], F32)
    zps = [p.psum(f"zps{i}", [128, 32], F32) for i in range(2)]
    for c0 in range(0, nsteps, CH):
        ci = (c0 // CH) % 2
        p.dma("sync", gb[ci][:], g_d[:, c0 * 32:(c0 + CH) * 32],
              writes=(f"gb{ci}",))
        p.dma("scalar", ub[ci][:], u_d[:, c0 * 32:(c0 + CH) * 32],
              writes=(f"ub{ci}",))
        for tl in range(CH):
            off = tl * 32
            zp = zps[tl % 2]
            p.t_op("vector", "tensor_mul", blend[:], gb[ci][:, off:off + 32],
                   sig[:], ALU.bypass if False else None, reads=(f"gb{ci}", "sig"),
                   writes=("blend",)) if False else None
            p.t_op("vector", "tensor_tensor", blend[:],
                   gb[ci][:, off:off + 32], sig[:], ALU.mult,
                   reads=(f"gb{ci}", "sig"), writes=("blend",))
            for m in range(KT):
                for k in range(KT):
                    p.mm(zp[:, m * 4:(m + 1) * 4],
                         at[k][:, m * 128:(m + 1) * 128],
                         blend[:, k * 4:(k + 1) * 4],
                         start=(k == 0), stop=(k == KT - 1),
                         reads=(f"at{k}", "blend"), writes=(tkey(zp),),
                         group=f"z{c0}_{tl}_{m}")
            p.t_op("vector", "tensor_tensor", zbuf[:], zp[:, :],
                   ub[ci][:, off:off + 32], ALU.add,
                   reads=(tkey(zp), f"ub{ci}"), writes=("zbuf",))
            p.t_op("scalar", "activation", sig[:], zbuf[:], AF.Tanh,
                   reads=("zbuf",), writes=("sig",))
            p.t_op("gpsimd", "tensor_copy", sb[ci][:, off:off + 32], sig[:],
                   reads=("sig",), writes=(f"sb{ci}",))
        p.dma("gpsimd", st_d[:, c0 * 32:(c0 + CH) * 32], sb[ci][:],
              reads=(f"sb{ci}",))
    p.dma("gpsimd", so_d[:, :], sig[:], reads=("sig",))
    p.finish()
    return p.finalize()


def host_l3aa_b(state, seg, nsteps=1024):
    g = state["l3aa_g"]
    u2 = state["l3aa_u2"]
    t0 = seg * nsteps
    gsl = np.ascontiguousarray(g[:, :, t0:t0 + nsteps]).astype(np.float32)
    usl = np.ascontiguousarray(u2[:, :, t0:t0 + nsteps])
    def lay(a):
        a2 = a.reshape(4, 8, 128, nsteps)
        a2 = a2.transpose(2, 3, 1, 0)
        return np.ascontiguousarray(a2.reshape(128, nsteps * 32))
    gL = lay(gsl).astype(BF)
    uL = lay(usl).astype(np.float32)
    s0 = state.get("l3aa_s0")
    if s0 is None:
        s0 = np.zeros((128, 32), np.float32)
    maps = []
    for c in range(8):
        maps.append({"gL": gL, "u2L": uL, "at": as_bf(state["inp"]["L3_attn_a_A"].T),
                     "s0": s0})
    return maps


def post_l3aa_b(state, results, seg, nsteps=1024):
    state["l3aa_s0"] = results[0]["so"]
    st = results[0]["states"].astype(np.float32)
    a2 = st.reshape(128, nsteps, 8, 4)
    a2 = a2.transpose(3, 2, 0, 1)
    states = a2.reshape(4, 1024, nsteps)
    if "l3aa_states" not in state:
        state["l3aa_states"] = np.zeros((4, 1024, TFULL), np.float32)
    t0 = seg * nsteps
    state["l3aa_states"][:, :, t0:t0 + nsteps] = states



_NC_CACHE = {}


def _get_nc(name, builder, *args):
    if name not in _NC_CACHE:
        _NC_CACHE[name] = builder(*args)
    return _NC_CACHE[name]


def _run(name, builder, host_fn, post_fn, state, *bargs):
    nc = _get_nc(name, builder, *bargs)
    maps = host_fn(state)
    res = run_bass_kernel_spmd(nc, maps, core_ids=list(range(8)))
    if post_fn is not None:
        post_fn(state, res.results)
    return res.results


def kernel(**inputs):
    state = {"inp": inputs, "x": np.asarray(inputs["L0_attn_a_x"],
                                            dtype=np.float32)}
    idx = np.arange(1024)
    Fm = np.exp(-2j * np.pi * np.outer(idx, idx) / 1024)
    state["consts"] = {"F": Fm, "G": np.conj(Fm) / 1024}

    _run("l0a_proj", build_l0a_proj, host_l0a_proj, post_l0a_proj, state)
    _run("l0a_wkv", build_l0a_wkv, host_l0a_wkv, post_l0a_wkv, state)
    _run("l0a_cm1", build_l0a_cm1, host_l0a_cm1, post_l0a_cm1, state)
    _run("l0a_cm2", build_l0a_cm2, host_l0a_cm2, post_l0a_cm2, state)
    _run("l0f", build_l0f, host_l0f, post_l0f, state)
    _run("l0ab_a", build_l0ab_a, host_l0ab_a, post_l0ab_a, state)
    _run("l0ab_b", build_l0ab_b, host_l0ab_b, post_l0ab_b, state)
    _run("l0f", build_l0f, host_l0f, post_l0f, state)

    _run("l1aa_a", build_l1aa_a, host_l1aa_a, post_l1aa_a, state)
    _run("l1aa_b", build_l1aa_b, host_l1aa_b, post_l1aa_b, state)
    res = _run("addres", build_addres, lambda s: host_addres(s, "l1aa_y"),
               None, state)
    state["x"] = unshard_x(res)
    _run("l1f_m", build_l1f_m, host_l1f_m, post_l1f_m, state)
    _run("l1f", build_l1f, host_l1f, post_l1f, state)
    _run("l1ab_a", build_l1ab_a, host_l1ab_a, post_l1ab_a, state)
    _run("l1ab_b", build_l1ab_b, host_l1ab_b, post_l1ab_b, state)
    _run("l1f", build_l1f, host_l1f, post_l1f, state)

    _run("l2aa_kv", build_l2aa_kv, host_l2aa_kv, post_l2aa_kv, state)
    _run("l2aa_at", build_l2aa_at, host_l2aa_at, post_l2aa_at, state)
    _run("l2f_a", build_l2f_a, host_l2f_a, post_l2f_a, state)
    _run("l2f_b", build_l2f_b, host_l2f_b, post_l2f_b, state)
    _run("l2ab_a", build_l2ab_a, host_l2ab_a, post_l2ab_a, state)
    _run("l2ab_b", build_l2ab_b, host_l2ab_b, post_l2ab_b, state)
    _run("l2ab_c", build_l2ab_c, host_l2ab_c, post_l2ab_c, state)
    _run("l2f_a", build_l2f_a, host_l2f_a, post_l2f_a, state)
    _run("l2f_b", build_l2f_b, host_l2f_b, post_l2f_b, state)

    _run("l3aa_a", build_l3aa_a, host_l3aa_a, post_l3aa_a, state)
    state.pop("l3aa_s0", None)
    nc = _get_nc("l3aa_b", build_l3aa_b)
    for seg in range(4):
        maps = host_l3aa_b(state, seg)
        res = run_bass_kernel_spmd(nc, maps, core_ids=list(range(8)))
        post_l3aa_b(state, res.results, seg)
    state["l3aa_y"] = [None]
    stb = state["l3aa_states"]
    state["l3aa_yst"] = stb
    nc = _get_nc("resproj_b", build_resproj, 1024, True)
    maps = host_resproj(state, "l3aa_yst", "L3_attn_a_W_out",
                        bkey="L3_attn_a_b_out", wT=True)
    res = run_bass_kernel_spmd(nc, maps, core_ids=list(range(8)))
    state["x"] = unshard_x(res.results)

    _run("l3f_w", build_l3f_w, host_l3f_w, post_l3f_w, state)
    _run("l3f_a", build_l3f_a, host_l3f_a, post_l3f_a, state)
    _run("l3f_b", build_l3f_b, lambda s: host_l3f_b(s, 0), post_l3f_b, state)
    _run("l3f_c", build_l3f_c, host_l3f_c, post_l3f_c, state)

    _run("l3ab_a", build_l3ab_a, host_l3ab_a, post_l3ab_a, state)
    _run("l3ab_b", build_l3ab_b, host_l3ab_b, post_l3ab_b, state)
    nc = _get_nc("resproj_nb", build_resproj, 1024, False)
    maps = host_resproj(state, "l3ab_ret", "L3_attn_b_Wo", wT=False)
    res = run_bass_kernel_spmd(nc, maps, core_ids=list(range(8)))
    state["x"] = unshard_x(res.results)

    _run("l3f_a", build_l3f_a, host_l3f_a, post_l3f_a, state)
    _run("l3f_b", build_l3f_b, lambda s: host_l3f_b(s, 1), post_l3f_b, state)
    _run("l3f_c", build_l3f_c, host_l3f_c, post_l3f_c, state)

    return np.ascontiguousarray(state["x"].astype(np.float32))


# version 09 (final)
# speedup vs baseline: 1.0116x; 1.0116x over previous
import numpy as np
from contextlib import ExitStack

import concourse.bass as bass
import concourse.mybir as mybir
from concourse.bass_utils import run_bass_kernel_spmd
from concourse.ap import AP as _AP

F32 = mybir.dt.float32
BF16 = mybir.dt.bfloat16
I32 = mybir.dt.int32
AF = mybir.ActivationFunctionType
ALU = mybir.AluOpType
AX = mybir.AxisListType

COMPUTE_ENGINES = ("tensor", "vector", "scalar", "gpsimd")
ALL_ENGINES = COMPUTE_ENGINES + ("sync",)
DMA_QUEUES = ("sync", "gpsimd", "scalar")


class Op:
    __slots__ = ("eng", "build", "reads", "writes", "idx", "is_dma", "group",
                 "waits", "inc_needed", "sem_val")

    def __init__(self, eng, build, reads, writes, is_dma, group=None):
        self.eng = eng
        self.build = build
        self.reads = reads
        self.writes = writes
        self.is_dma = is_dma
        self.group = group
        self.idx = -1
        self.waits = []
        self.inc_needed = False
        self.sem_val = None


class Prog:

    def __init__(self, name="prog"):
        self.nc = bass.Bass("TRN2", target_bir_lowering=False)
        self.es = ExitStack()
        self.ops = {e: [] for e in ALL_ENGINES}
        self.last_write = {}
        self.readers = {}
        self.sb_bytes = 0
        self.psum_banks = 0
        self._finalized = False
        self._uniq = 0

    def dram(self, name, shape, dtype=F32, kind="ExternalInput"):
        key = ("dr", name)
        if key in getattr(self, "_cache", {}):
            return self._cache[key][0]
        ap = self.nc.dram_tensor(name, list(shape), dtype, kind=kind).ap()
        if not hasattr(self, "_cache"):
            self._cache = {}
        self._cache[key] = (ap, list(shape), dtype)
        return ap

    def sbuf(self, name, shape, dtype=F32):
        key = ("sb", name)
        if key in getattr(self, "_cache", {}):
            t, sh, dt = self._cache[key]
            assert list(sh) == list(shape) and dt == dtype, f"realloc {name} mismatch"
            return t
        t = self.es.enter_context(self.nc.sbuf_tensor(name, list(shape), dtype))
        itemsize = 4 if dtype in (F32, I32) else 2
        self.sb_bytes += int(np.prod(shape[1:])) * itemsize
        if not hasattr(self, "_cache"):
            self._cache = {}
        self._cache[key] = (t, list(shape), dtype)
        return t

    def psum(self, name, shape, dtype=F32):
        key = ("ps", name)
        if key in getattr(self, "_cache", {}):
            t, sh, dt = self._cache[key]
            assert list(sh) == list(shape) and dt == dtype, f"realloc {name} mismatch"
            return t
        t = self.es.enter_context(self.nc.psum_tensor(name, list(shape), dtype))
        itemsize = 4 if dtype == F32 else 2
        self.psum_banks += -(-int(np.prod(shape[1:])) * itemsize // 2048)
        if not hasattr(self, "_cache"):
            self._cache = {}
        self._cache[key] = (t, list(shape), dtype)
        return t

    def op(self, eng, build, reads=(), writes=(), group=None):
        o = Op(eng, build, tuple(reads), tuple(writes), is_dma=False, group=group)
        o.idx = len(self.ops[eng])
        self.ops[eng].append(o)
        self._track(o)
        return o

    def dma(self, queue, out_ap, in_ap, reads=(), writes=()):
        def build(e):
            return e.dma_start(out=out_ap, in_=in_ap)
        o = Op(queue, build, tuple(reads), tuple(writes), is_dma=True)
        o.idx = len(self.ops[queue])
        self.ops[queue].append(o)
        self._track(o)
        return o

    def _track(self, o):
        deps = set()
        for r in o.reads:
            w = self.last_write.get(r)
            if w is not None and w is not o:
                deps.add(w)
        for r in o.writes:
            w = self.last_write.get(r)
            if w is not None and w is not o:
                deps.add(w)
            for rd in self.readers.get(r, ()):
                if rd is not o:
                    deps.add(rd)
        for d in deps:
            if d.eng == o.eng and not d.is_dma and not o.is_dma:
                if o.group is not None and d.group == o.group:
                    continue
            o.waits.append(d)
            d.inc_needed = True
        for r in o.reads:
            self.readers.setdefault(r, []).append(o)
        for r in o.writes:
            self.last_write[r] = o
            self.readers[r] = []

    def finalize(self):
        assert not self._finalized
        self._finalized = True
        nc = self.nc
        self.sem = {}
        for e in ALL_ENGINES:
            self.sem[e] = self.es.enter_context(nc.semaphore(f"sem_{e}"))
            self.sem[e + "_dma"] = self.es.enter_context(nc.semaphore(f"dma_{e}")) \
                if e in DMA_QUEUES else None
        for e in ALL_ENGINES:
            rank = 0
            dma_rank = 0
            for o in self.ops[e]:
                if o.is_dma:
                    dma_rank += 1
                    o.sem_val = dma_rank * 16
                elif o.inc_needed:
                    rank += 1
                    o.sem_val = rank
        engmap = {}
        for e in ALL_ENGINES:
            have = {}
            for o in self.ops[e]:
                wl = []
                for d in o.waits:
                    key = (d.eng, d.is_dma)
                    assert d.sem_val is not None
                    if have.get(key, 0) >= d.sem_val:
                        continue
                    have[key] = d.sem_val
                    sem = self.sem[d.eng + "_dma"] if d.is_dma else self.sem[d.eng]
                    wl.append((sem, d.sem_val))
                o.waits = wl
        block = self.es.enter_context(nc.Block())

        def mk(e_name):
            ops = self.ops[e_name]
            sem_c = self.sem[e_name]
            sem_d = self.sem.get(e_name + "_dma")

            def run(eng):
                for o in ops:
                    if o.is_dma and o.sem_val > 16:
                        eng.wait_ge(sem_d, o.sem_val - 16)
                    for (sem, val) in o.waits:
                        eng.wait_ge(sem, val)
                    ins = o.build(eng)
                    if o.is_dma:
                        ins.then_inc(sem_d, 16)
                    elif o.inc_needed:
                        ins.then_inc(sem_c, 1)
            return run

        for e_name, deco in (("sync", block.sync), ("vector", block.vector),
                             ("scalar", block.scalar), ("tensor", block.tensor),
                             ("gpsimd", block.gpsimd)):
            if self.ops[e_name]:
                deco(mk(e_name))
        self.es.close()
        return nc

    def finish(self):
        tails = []
        for q in DMA_QUEUES:
            dmas = [o for o in self.ops[q] if o.is_dma]
            if dmas:
                tails.append(dmas[-1])
        def build(eng):
            return eng.nop()
        o = Op("gpsimd", build, (), (), is_dma=False)
        o.idx = len(self.ops["gpsimd"])
        self.ops["gpsimd"].append(o)
        for d in tails:
            o.waits.append(d)
            d.inc_needed = True

    def t_op(self, eng, method, out, *ins, reads=(), writes=(), **kw):
        if eng == "gpsimd" and method in ("tensor_scalar", "tensor_single_scalar",
                                          "scalar_tensor_tensor", "tensor_reduce",
                                          "reduce_max", "reduce_sum",
                                          "tensor_tensor_scan"):
            eng = "vector"
        def build(e):
            return getattr(e, method)(out, *ins, **kw)
        return self.op(eng, build, reads=reads, writes=writes)

    def mm(self, out, lhsT, rhs, start, stop, reads=(), writes=(), group=None, **kw):
        def build(e):
            return e.matmul(out, lhsT, rhs, start=start, stop=stop, **kw)
        return self.op("tensor", build, reads=reads, writes=writes, group=group)

    def transp(self, out, in_, ident, reads=(), writes=(), **kw):
        def build(e):
            return e.transpose(out, in_, ident, **kw)
        return self.op("tensor", build, reads=reads, writes=writes)

    def uniq(self):
        self._uniq += 1
        return f"_u{self._uniq}"


D = 1024
TFULL = 4096
TH = 2048
KT = D // 128
NB = 4

import ml_dtypes
BF = ml_dtypes.bfloat16


def tkey(t):
    try:
        return t.tensor.name
    except AttributeError:
        return t.name


def keys(tiles):
    return [tkey(t) for t in tiles]


class PsumPool:
    def __init__(self, p, name, n=4, shape=(128, 512), dtype=F32):
        self.tiles = [p.psum(f"{name}{i}", list(shape), dtype) for i in range(n)]
        self.i = 0

    def get(self):
        t = self.tiles[self.i % len(self.tiles)]
        self.i += 1
        return t


class Common:
    def __init__(self, p, npool=4):
        self.p = p
        self.ones_bf = p.sbuf("c_ones_bf", [128, 1], BF16)
        self.ones_f = p.sbuf("c_ones_f", [1, 128], F32)
        p.t_op("vector", "memset", self.ones_bf[:], 1.0, writes=("c_ones_bf",))
        p.t_op("vector", "memset", self.ones_f[:], 1.0, writes=("c_ones_f",))
        self.pool = PsumPool(p, "c_ps", n=npool)
        self._consts = {}

    def const(self, val):
        key = float(val)
        if key not in self._consts:
            i = len(self._consts)
            t = self.p.sbuf(f"c_const{i}", [128, 1], F32)
            self.p.t_op("vector", "memset", t[:], key, writes=(f"c_const{i}",))
            self._consts[key] = (t, f"c_const{i}")
        return self._consts[key]


def load_dmajor(p, name, T, dtype=F32, rows=D):
    d = p.dram(name, [rows, T], dtype)
    tiles = []
    for k in range(rows // 128):
        t = p.sbuf(f"{name}_t{k}", [128, T], dtype)
        p.dma("sync", t[:], d[k * 128:(k + 1) * 128, :], writes=(f"{name}_t{k}",))
        tiles.append(t)
    return tiles


def store_dmajor(p, name, tiles, T, dtype=F32, queue="gpsimd"):
    rows = 128 * len(tiles)
    d = p.dram(name, [rows, T], dtype, kind="ExternalOutput")
    for k, t in enumerate(tiles):
        p.dma(queue, d[k * 128:(k + 1) * 128, :], t[:, 0:T], reads=(tkey(t),))
    return d


def load_w(p, name, din, dout, dtype=BF16):
    d = p.dram(name, [din, dout], dtype)
    tiles = []
    for k in range((din + 127) // 128):
        r = min(128, din - k * 128)
        t = p.sbuf(f"{name}_w{k}", [r, dout], dtype)
        p.dma("sync", t[:], d[k * 128:k * 128 + r, :], writes=(f"{name}_w{k}",))
        tiles.append(t)
    return tiles


def load_rows(p, name, rows, cols=1, dtype=F32):
    d = p.dram(name, [rows, cols], dtype)
    tiles = []
    for k in range((rows + 127) // 128):
        r = min(128, rows - k * 128)
        t = p.sbuf(f"{name}_r{k}", [128, cols], dtype)
        p.dma("sync", t[0:r, :], d[k * 128:k * 128 + r, :], writes=(name,))
        tiles.append(t)
    return tiles


def proj(p, tag, x_tiles, w_tiles, dout, T, com,
         out_dtype=BF16, evac_func=None, bias=None,
         evac_eng="vector", chunk=512, out_aps=None, out_keys=None,
         x_keys=None, w_key=None, bias_key=None, scale=None,
         terms=None, accum_into=None, accum_keys=None, x_off=0):
    M = (dout + 127) // 128
    if terms is None:
        terms = [(w_tiles, x_tiles, w_key or keys(w_tiles),
                  x_keys or keys(x_tiles))]
    else:
        terms = [(w, x, wk or keys(w), xk or keys(x))
                 for (w, x, wk, xk) in terms]
    terms = [(w, x, wk if isinstance(wk, (list, tuple)) else [wk] * len(w), xk)
             for (w, x, wk, xk) in terms]
    made = None
    if out_aps is None:
        made = [p.sbuf(f"{tag}_o{m}", [128, T], out_dtype) for m in range(M)]
        out_aps = [t[:] for t in made]
        out_keys = keys(made)
    nterm = len(terms)
    for m in range(M):
        for c0 in range(0, T, chunk):
            cw = min(chunk, T - c0)
            ps = com.pool.get()
            pk = tkey(ps)
            gid = f"{tag}_{m}_{c0}"
            mrows = min(128, dout - m * 128)
            for ti, (wt, xt, wk, xk) in enumerate(terms):
                nk = len(xt)
                for k in range(nk):
                    kr = wt[k].shape[0]
                    p.mm(ps[0:mrows, 0:cw],
                         wt[k][:, m * 128:m * 128 + mrows],
                         xt[k][0:kr, x_off + c0:x_off + c0 + cw],
                         start=(ti == 0 and k == 0),
                         stop=(ti == nterm - 1 and k == nk - 1),
                         reads=(wk[k], xk[k]), writes=(pk,), group=gid)
            oap = out_aps[m]
            ok = out_keys[m]
            osl = oap[0:mrows, c0:c0 + cw]
            if accum_into is not None:
                at, ak = accum_into[m], accum_keys[m]
                if bias is not None:
                    p.t_op("vector", "scalar_tensor_tensor", osl,
                           ps[0:mrows, 0:cw], bias[m], at[0:mrows, c0:c0 + cw],
                           ALU.add, ALU.add,
                           reads=(pk, ak, bias_key), writes=(ok,))
                else:
                    p.t_op("vector", "tensor_tensor", osl,
                           at[0:mrows, c0:c0 + cw], ps[0:mrows, 0:cw], ALU.add,
                           reads=(pk, ak), writes=(ok,))
            elif evac_func == "bias_add":
                p.t_op("vector", "tensor_scalar", osl,
                       ps[0:mrows, 0:cw], bias[m], None, ALU.add,
                       reads=(pk, bias_key), writes=(ok,))
            elif evac_func is not None:
                kw = {}
                rd = [pk]
                if bias is not None:
                    kw["bias"] = bias[m]
                    rd.append(bias_key)
                if scale is not None:
                    kw["scale"] = scale
                p.t_op("scalar", "activation", osl, ps[0:mrows, 0:cw],
                       evac_func, reads=tuple(rd), writes=(ok,), **kw)
            else:
                p.t_op(evac_eng, "tensor_copy", osl, ps[0:mrows, 0:cw],
                       reads=(pk,), writes=(ok,))
    return made


def bcast_row(p, tag, row_ap, row_key, T, com, out=None, dtype=F32):
    if out is None:
        out = p.sbuf(f"{tag}_bc", [128, T], dtype)
    ok = tkey(out)
    for c0 in range(0, T, 512):
        cw = min(512, T - c0)
        ps = com.pool.get()
        pk = tkey(ps)
        p.mm(ps[:, 0:cw], com.ones_f[:], row_ap[:, c0:c0 + cw], start=True,
             stop=True, reads=("c_ones_f", row_key), writes=(pk,))
        p.t_op("vector", "tensor_copy", out[:, c0:c0 + cw], ps[:, 0:cw],
               reads=(pk,), writes=(ok,))
    return out


def ln_d(p, tag, x_tiles, T, com, g=None, b=None, gb_key=None,
         out_dtype=BF16, eps=1e-5, x_keys=None, scratch="lnws", in_bf=None,
         precise=False):
    nk = len(x_tiles)
    Dn = 128 * nk
    x_keys = x_keys or keys(x_tiles)
    if precise:
        if not hasattr(com, "ones_f32c"):
            com.ones_f32c = p.sbuf("c_ones_f32c", [128, 1], F32)
            p.t_op("vector", "memset", com.ones_f32c[:], 1.0, writes=("c_ones_f32c",))
        xb_aps = [t[:, 0:T] for t in x_tiles]
        xb_keys = x_keys
        ones_stat = com.ones_f32c
        ones_key = "c_ones_f32c"
        sqd = F32
    elif in_bf is None:
        xb = [p.sbuf(f"{scratch}_xb{k}", [128, T], BF16) for k in range(nk)]
        for k in range(nk):
            p.t_op("vector", "tensor_copy", xb[k][:], x_tiles[k][:, 0:T],
                   reads=(x_keys[k],), writes=(tkey(xb[k]),))
        xb_keys = [tkey(t) for t in xb]
        xb_aps = [t[:] for t in xb]
    else:
        xb_aps = [t[:, 0:T] for t in x_tiles]
        xb_keys = x_keys
    if not precise:
        ones_stat = com.ones_bf
        ones_key = "c_ones_bf"
        sqd = BF16
    sq = p.sbuf(f"{scratch}_sq{'p' if precise else ''}", [128, 512], sqd)
    sums = p.sbuf(f"{scratch}_sums", [1, T], F32)
    sumq = p.sbuf(f"{scratch}_sumq", [1, T], F32)
    k_sums, k_sumq = tkey(sums), tkey(sumq)
    for c0 in range(0, T, 512):
        cw = min(512, T - c0)
        ps_s = com.pool.get()
        ps_q = com.pool.get()
        for k in range(nk):
            p.mm(ps_s[0:1, 0:cw], ones_stat[:], xb_aps[k][:, c0:c0 + cw],
                 start=(k == 0), stop=(k == nk - 1),
                 reads=(ones_key, xb_keys[k]), writes=(tkey(ps_s),),
                 group=f"{tag}s{c0}")
        for k in range(nk):
            p.t_op("scalar", "activation", sq[:, 0:cw], xb_aps[k][:, c0:c0 + cw],
                   AF.Square, reads=(xb_keys[k],), writes=(tkey(sq),))
            p.mm(ps_q[0:1, 0:cw], ones_stat[:], sq[:, 0:cw],
                 start=(k == 0), stop=(k == nk - 1),
                 reads=(ones_key, tkey(sq)), writes=(tkey(ps_q),),
                 group=f"{tag}q{c0}")
        p.t_op("vector", "tensor_copy", sums[:, c0:c0 + cw], ps_s[0:1, 0:cw],
               reads=(tkey(ps_s),), writes=(k_sums,))
        p.t_op("vector", "tensor_copy", sumq[:, c0:c0 + cw], ps_q[0:1, 0:cw],
               reads=(tkey(ps_q),), writes=(k_sumq,))
    mean = p.sbuf(f"{scratch}_mean", [1, T], F32)
    rstd = p.sbuf(f"{scratch}_rstd", [1, T], F32)
    tmp = p.sbuf(f"{scratch}_tmpr", [1, T], F32)
    km, kr, kt2 = tkey(mean), tkey(rstd), tkey(tmp)
    p.t_op("vector", "tensor_scalar", mean[:], sums[:], 1.0 / Dn, None,
           ALU.mult, reads=(k_sums,), writes=(km,))
    p.t_op("scalar", "activation", tmp[:], mean[:], AF.Square,
           reads=(km,), writes=(kt2,))
    p.t_op("vector", "scalar_tensor_tensor", rstd[:], sumq[:], 1.0 / Dn, tmp[:],
           ALU.mult, ALU.subtract, reads=(k_sumq, kt2), writes=(kr,))
    epst, epsk = com.const(float(eps))
    p.t_op("scalar", "activation", tmp[:], rstd[:], AF.Ln, bias=epst[0:1, :],
           reads=(kr, epsk), writes=(kt2,))
    p.t_op("scalar", "activation", rstd[:], tmp[:], AF.Exp, scale=-0.5,
           reads=(kt2,), writes=(kr,))
    mb = p.sbuf(f"{scratch}_mb", [128, T], F32)
    rb = p.sbuf(f"{scratch}_rb", [128, T], F32)
    kmb, krb = tkey(mb), tkey(rb)
    for (row, rk, dst, dk) in ((mean, km, mb, kmb), (rstd, kr, rb, krb)):
        for c0 in range(0, T, 512):
            cw = min(512, T - c0)
            ps = com.pool.get()
            pk = tkey(ps)
            p.mm(ps[:, 0:cw], com.ones_f[:], row[:, c0:c0 + cw], start=True,
                 stop=True, reads=("c_ones_f", rk), writes=(pk,))
            p.t_op("vector", "tensor_copy", dst[:, c0:c0 + cw], ps[:, 0:cw],
                   reads=(pk,), writes=(dk,))
    out_tiles = [p.sbuf(f"{tag}_n{k}", [128, T], out_dtype) for k in range(nk)]
    engs = ["vector", "gpsimd"]
    for k in range(nk):
        e = engs[k % 2]
        ok = tkey(out_tiles[k])
        p.t_op(e, "tensor_sub", out_tiles[k][:], x_tiles[k][:, 0:T], mb[:],
               reads=(x_keys[k], kmb), writes=(ok,))
        p.t_op(e, "tensor_mul", out_tiles[k][:], out_tiles[k][:], rb[:],
               reads=(ok, krb), writes=(ok,))
        if g is not None:
            p.t_op(e, "tensor_scalar", out_tiles[k][:], out_tiles[k][:],
                   g[k], b[k], ALU.mult, ALU.add,
                   reads=(ok, gb_key), writes=(ok,))
    return out_tiles


LAUNCHES = []


_BFCACHE = {}


def as_bf(a):
    key = (id(a), a.shape if hasattr(a, "shape") else None)
    v = _BFCACHE.get(key)
    if v is not None and v[0] is a:
        return v[1]
    r = np.ascontiguousarray(a.astype(BF))
    _BFCACHE[key] = (a, r)
    return r


def as_f32(a):
    return np.ascontiguousarray(a.astype(np.float32))


def xshard(x, c, halo=0):
    b, h = c // 2, c % 2
    t0 = h * TH - halo
    lo = max(t0, 0)
    sl = x[b, lo:(h + 1) * TH, :].T
    if t0 < 0:
        pad = np.zeros((D, -t0), np.float32)
        sl = np.concatenate([pad, sl], axis=1)
    return np.ascontiguousarray(sl.astype(np.float32))


def unshard_x(results, key="xo"):
    x = np.zeros((NB, TFULL, D), np.float32)
    for c in range(8):
        b, h = c // 2, c % 2
        x[b, h * TH:(h + 1) * TH, :] = results[c][key].T
    return x


def build_l1f_m():
    p = Prog()
    com = Common(p)
    wrt = load_w(p, "wrt", D, D)
    wit = load_w(p, "wit", D, D)
    gr = load_dmajor(p, "gr", D, dtype=BF16)
    gi = load_dmajor(p, "gi", D, dtype=BF16)
    ngi = load_dmajor(p, "ngi", D, dtype=BF16)
    fr = load_w(p, "fr", D, 128)
    nfi = load_w(p, "nfi", D, 128)
    t1 = proj(p, "t1", None, None, D, D, com,
              terms=[(wrt, gr, None, None), (wit, ngi, None, None)])
    t2 = proj(p, "t2", None, None, D, D, com,
              terms=[(wrt, gi, None, None), (wit, gr, None, None)])
    mcol = proj(p, "mcol", None, None, 128, D, com,
                terms=[(fr, t1, None, None), (nfi, t2, None, None)])
    store_dmajor(p, "mo", mcol, D, dtype=BF16)
    p.finish()
    return p.finalize()


def host_l1f_m(state):
    inp = state["inp"]
    Fmat = state["consts"]["F"]
    G = state["consts"]["G"]
    wr = inp["L1_ffn_w_freq_real"]
    wi = inp["L1_ffn_w_freq_imag"]
    maps = []
    for c in range(8):
        maps.append({
            "wrt": as_bf(wr.T), "wit": as_bf(wi.T),
            "gr": as_bf(G.real), "gi": as_bf(G.imag), "ngi": as_bf(-G.imag),
            "fr": as_bf(Fmat.real[:, c * 128:(c + 1) * 128]),
            "nfi": as_bf(-Fmat.imag[:, c * 128:(c + 1) * 128]),
        })
    return maps


def post_l1f_m(state, results):
    M = np.zeros((D, D), BF)
    for c in range(8):
        M[c * 128:(c + 1) * 128, :] = results[c]["mo"].astype(BF)
    state["l1f_M"] = M


def build_l1f():
    p = Prog()
    com = Common(p)
    xt = load_dmajor(p, "x", TH)
    mt = load_w(p, "m", D, D)
    bias = load_rows(p, "bias", D)
    xn = ln_d(p, "ln", xt, TH, com)
    xkeys = [f"x_t{k}" for k in range(KT)]
    proj(p, "y", xn, mt, D, TH, com,
         out_aps=[t[:] for t in xt], out_keys=xkeys,
         bias=[t[:] for t in bias], bias_key="bias",
         accum_into=xt, accum_keys=xkeys)
    d_out = p.dram("xo", [D, TH], F32, kind="ExternalOutput")
    for k in range(KT):
        p.dma("gpsimd", d_out[k * 128:(k + 1) * 128, :], xt[k][:], reads=(xkeys[k],))
    p.finish()
    return p.finalize()


def host_l1f(state):
    inp = state["inp"]
    M = state["l1f_M"]
    b = inp["L1_ffn_bias"].reshape(D, 1)
    return [{"x": xshard(state["x"], c), "m": np.ascontiguousarray(M),
             "bias": as_f32(b)} for c in range(8)]


def post_l1f(state, results):
    state["x"] = unshard_x(results)


def transpose_128(p, tag, src_ap, src_key, dst_ap, dst_key, ident, com):
    ps = p.psum("trps", [128, 128], BF16)
    p.transp(ps[:], src_ap, ident[:], reads=(src_key, "ident"), writes=(tkey(ps),))
    p.t_op("vector", "tensor_copy", dst_ap, ps[:], reads=(tkey(ps),), writes=(dst_key,))


def load_ident(p):
    d = p.dram("ident", [128, 128], BF16)
    t = p.sbuf("ident_t", [128, 128], BF16)
    p.dma("sync", t[:], d[:, :], writes=("ident",))
    return t


def round_ints(p, tag, src_ap, src_key, dst_ap, dst_key, itile, mtile,
               eng="vector"):
    ik, mk = tkey(itile), tkey(mtile)
    sh = list(src_ap.shape)
    ct = p.sbuf(f"ri_{tag}_ct", sh, F32)
    ck = tkey(ct)
    p.t_op(eng, "tensor_copy", itile, src_ap, reads=(src_key,), writes=(ik,))
    p.t_op(eng, "tensor_copy", mtile, itile, reads=(ik,), writes=(mk,))
    p.t_op(eng, "tensor_sub", ct[:], src_ap, mtile, reads=(src_key, mk),
           writes=(ck,))
    p.t_op(eng, "tensor_scalar", dst_ap, ct[:], -0.5, None, ALU.is_le,
           reads=(ck,), writes=(dst_key,))
    p.t_op(eng, "tensor_scalar", ct[:], ct[:], 0.5, None, ALU.is_ge,
           reads=(ck,), writes=(ck,))
    p.t_op(eng, "tensor_sub", ct[:], ct[:], dst_ap, reads=(ck, dst_key),
           writes=(ck,))
    p.t_op(eng, "tensor_add", dst_ap, mtile, ct[:], reads=(mk, ck),
           writes=(dst_key,))


L3I = 2816


def build_l3f_w():
    p = Prog()
    com = Common(p)
    ident = load_ident(p)
    outs = {}
    for (nm, nt, cols) in (("wg", 3, 1024), ("wu", 3, 1024), ("wd", 1, 2816)):
        w = load_dmajor(p, f"{nm}_in", cols, rows=128 * nt)
        scales = p.sbuf(f"{nm}_scales", [128, nt], F32)
        it = p.sbuf(f"{nm}_i32", [128, cols], I32)
        qt = [p.sbuf(f"{nm}_q{j}", [128, cols], BF16) for j in range(nt)]
        rtmp = p.sbuf(f"{nm}_rt", [128, cols], F32)
        for j in range(nt):
            wk = f"{nm}_in_t{j}"
            amax = p.sbuf(f"{nm}_am{j}", [128, 1], F32)
            p.t_op("vector", "reduce_max", amax[:], w[j][:], AX.X,
                   apply_absolute_value=True, reads=(wk,), writes=(tkey(amax),))
            p.t_op("vector", "tensor_scalar", scales[:, j:j + 1], amax[:],
                   1.0 / 127.0, None, ALU.mult, reads=(tkey(amax),),
                   writes=(tkey(scales),))
            rsc = p.sbuf(f"{nm}_rs{j}", [128, 1], F32)
            p.t_op("vector", "reciprocal", rsc[:], amax[:], reads=(tkey(amax),),
                   writes=(tkey(rsc),))
            p.t_op("vector", "tensor_scalar", rtmp[:], w[j][:], rsc[:], 127.0,
                   ALU.mult, ALU.mult, reads=(wk, tkey(rsc)), writes=(tkey(rtmp),))
            mt_ = p.sbuf(f"{nm}_mt", [128, cols], F32)
            round_ints(p, f"{nm}{j}", rtmp[:], tkey(rtmp), rtmp[:], tkey(rtmp),
                       it[:], mt_[:])
            p.t_op("vector", "tensor_copy", qt[j][:], rtmp[:],
                   reads=(tkey(rtmp),), writes=(tkey(qt[j]),))
        od = p.dram(f"{nm}_qT", [cols, 128 * nt], BF16, kind="ExternalOutput")
        for j in range(nt):
            for ct in range(cols // 128):
                tt = p.sbuf(f"{nm}_tt", [128, 128], BF16)
                transpose_128(p, f"{nm}tr", qt[j][:, ct * 128:(ct + 1) * 128],
                              tkey(qt[j]), tt[:], tkey(tt), ident, com)
                p.dma("gpsimd", od[ct * 128:(ct + 1) * 128, j * 128:(j + 1) * 128],
                      tt[:], reads=(tkey(tt),))
        sd = p.dram(f"{nm}_sc_o", [128, nt], F32, kind="ExternalOutput")
        p.dma("gpsimd", sd[:, :], scales[:], reads=(tkey(scales),))
    p.finish()
    return p.finalize()


IDENT = None


def get_ident():
    global IDENT
    if IDENT is None:
        IDENT = np.eye(128).astype(BF)
    return IDENT


def host_l3f_w(state):
    inp = state["inp"]
    maps = []
    wg, wu, wd = (inp["L3_ffn_w_gate"], inp["L3_ffn_w_up"], inp["L3_ffn_w_down"])
    for c in range(8):
        m = {}
        for nm, w, nt in (("wg", wg, 3), ("wu", wu, 3), ("wd", wd, 1)):
            r0 = c * nt * 128
            rows = np.zeros((nt * 128, w.shape[1]), np.float32)
            r1 = min(r0 + nt * 128, w.shape[0])
            if r0 < w.shape[0]:
                rows[0:r1 - r0] = w[r0:r1]
            else:
                rows[:, 0] = 1.0
            if r1 - r0 < nt * 128 and r0 < w.shape[0]:
                rows[r1 - r0:, 0] = 1.0
            m[f"{nm}_in"] = as_f32(rows)
        m["ident"] = get_ident()
        maps.append(m)
    return maps


def post_l3f_w(state, results):
    out = {}
    for nm, rows, cols, nt in (("wg", L3I, 1024, 3), ("wu", L3I, 1024, 3),
                               ("wd", 1024, 2816, 1)):
        qT = np.zeros((cols, rows), BF)
        sc = np.zeros((rows,), np.float32)
        for c in range(8):
            r0 = c * nt * 128
            r1 = min(r0 + nt * 128, rows)
            if r0 >= rows:
                continue
            qT[:, r0:r1] = results[c][f"{nm}_qT"][:, 0:r1 - r0]
            sc[r0:r1] = results[c][f"{nm}_sc_o"].T.reshape(-1)[0:r1 - r0]
        out[nm + "_qT"] = qT
        out[nm + "_sc"] = sc
    state["l3f_w"] = out


def global_max_rows(p, tag, tiles, tile_keys, T, com):
    am = p.sbuf(f"{tag}_am", [128, 1], F32)
    amk = tkey(am)
    for i, t in enumerate(tiles):
        tmp = p.sbuf(f"{tag}_amt", [128, 1], F32)
        p.t_op("vector", "reduce_max", tmp[:], t[:, 0:T], AX.X,
               apply_absolute_value=True, reads=(tile_keys[i],), writes=(tkey(tmp),))
        if i == 0:
            p.t_op("vector", "tensor_copy", am[:], tmp[:], reads=(tkey(tmp),),
                   writes=(amk,))
        else:
            p.t_op("vector", "tensor_max", am[:], am[:], tmp[:],
                   reads=(amk, tkey(tmp)), writes=(amk,))
    return finish_max128(p, tag, am, amk)


def finish_max128(p, tag, am, amk):
    dscr = p.dram(f"{tag}_scr", [128, 1], F32, kind="Internal")
    p.dma("gpsimd", dscr[:, :], am[:], reads=(amk,))
    row = p.sbuf(f"{tag}_row", [1, 128], F32)
    p.dma("gpsimd", row[:], dscr[:, :].rearrange("p one -> one p"), writes=(tkey(row),))
    out = p.sbuf(f"{tag}_gm", [1, 1], F32)
    p.t_op("vector", "reduce_max", out[:], row[:], AX.X,
           reads=(tkey(row),), writes=(tkey(out),))
    return out


def bcast_scalar(p, tag, val_ap, val_key, com):
    ps = com.pool.get()
    out = p.sbuf(f"{tag}_bs", [128, 1], F32)
    p.mm(ps[:, 0:1], com.ones_f[:], val_ap, start=True, stop=True,
         reads=("c_ones_f", val_key), writes=(tkey(ps),))
    p.t_op("vector", "tensor_copy", out[:], ps[:, 0:1], reads=(tkey(ps),),
           writes=(tkey(out),))
    return out


def build_l3f_a():
    p = Prog()
    com = Common(p)
    TQ = 1024
    x_d = p.dram("x", [D, TH], F32)
    xt = [p.sbuf(f"xh{k}", [128, TQ], F32) for k in range(KT)]
    am = p.sbuf("am_all", [128, 1], F32)
    p.t_op("vector", "memset", am[:], 0.0, writes=("am_all",))
    for h0 in (0, TQ):
        for k in range(KT):
            p.dma("sync", xt[k][:], x_d[k * 128:(k + 1) * 128, h0:h0 + TQ],
                  writes=(f"xh{k}",))
        xn = ln_d(p, "ln", xt, TQ, com, out_dtype=F32, precise=True,
                  x_keys=[f"xh{k}" for k in range(KT)])
        for i, t in enumerate(xn):
            tmp = p.sbuf("am_t", [128, 1], F32)
            p.t_op("vector", "reduce_max", tmp[:], t[:], AX.X,
                   apply_absolute_value=True, reads=(tkey(t),), writes=("am_t",))
            p.t_op("vector", "tensor_max", am[:], am[:], tmp[:],
                   reads=("am_all", "am_t"), writes=("am_all",))
    gm = finish_max128(p, "gm", am, "am_all")
    d = p.dram("gmo", [1, 1], F32, kind="ExternalOutput")
    p.dma("gpsimd", d[:, :], gm[:], reads=(tkey(gm),))
    p.finish()
    return p.finalize()


def host_l3f_a(state):
    return [{"x": xshard(state["x"], c)} for c in range(8)]


def post_l3f_a(state, results):
    state["l3f_gmax8"] = np.concatenate(
        [results[c]["gmo"].reshape(1, 1) for c in range(8)], axis=1)


def quant_acts(p, tag, xn, T, com, gmax8_name, clip_name, out_dtype=BF16, out_tiles=None):
    g8 = p.sbuf(f"{tag}_g8", [1, 8], F32)
    d = p.dram(gmax8_name, [1, 8], F32)
    p.dma("sync", g8[:], d[:, :], writes=(tkey(g8),))
    cl = p.sbuf(f"{tag}_cl", [1, 1], F32)
    dc = p.dram(clip_name, [1, 1], F32)
    p.dma("sync", cl[:], dc[:, :], writes=(tkey(cl),))
    gm = p.sbuf(f"{tag}_gm", [1, 1], F32)
    p.t_op("vector", "reduce_max", gm[:], g8[:], AX.X, reads=(tkey(g8),),
           writes=(tkey(gm),))
    sig = p.sbuf(f"{tag}_sig", [1, 1], F32)
    p.t_op("scalar", "activation", sig[:], cl[:], AF.Sigmoid, reads=(tkey(cl),),
           writes=(tkey(sig),))
    sc = p.sbuf(f"{tag}_sc", [1, 1], F32)
    p.t_op("vector", "tensor_tensor", sc[:], sig[:], gm[:], ALU.mult,
           reads=(tkey(sig), tkey(gm)), writes=(tkey(sc),))
    p.t_op("vector", "tensor_scalar", sc[:], sc[:], 1.0 / 127.0, None, ALU.mult,
           reads=(tkey(sc),), writes=(tkey(sc),))
    rsc = p.sbuf(f"{tag}_rsc", [1, 1], F32)
    p.t_op("vector", "reciprocal", rsc[:], sc[:], reads=(tkey(sc),),
           writes=(tkey(rsc),))
    rb = bcast_scalar(p, f"{tag}r", rsc[:], tkey(rsc), com)
    if out_tiles is None:
        qt = [p.sbuf(f"{tag}_q{k}", [128, T], out_dtype) for k in range(len(xn))]
    else:
        qt = out_tiles
    it = p.sbuf(f"{tag}_i32", [128, 512], I32)
    mt = p.sbuf(f"{tag}_mt", [128, 512], F32)
    rt = p.sbuf(f"{tag}_rt", [128, 512], F32)
    for k in range(len(xn)):
        for c0 in range(0, T, 512):
            s = slice(c0, c0 + 512)
            p.t_op("vector", "tensor_scalar", rt[:], xn[k][:, s], rb[:], None,
                   ALU.mult, reads=(tkey(xn[k]), tkey(rb)), writes=(tkey(rt),))
            round_ints(p, f"{tag}{k}", rt[:], tkey(rt), rt[:], tkey(rt), it[:], mt[:])
            p.t_op("vector", "tensor_scalar", rt[:], rt[:], 127.0, -128.0,
                   ALU.min, ALU.max, reads=(tkey(rt),), writes=(tkey(rt),))
            p.t_op("vector", "tensor_copy", qt[k][:, s], rt[:], reads=(tkey(rt),),
                   writes=(tkey(qt[k]),))
    return qt, sc


def build_l3f_b():
    p = Prog()
    com = Common(p)
    TQ = 512
    x_d = p.dram("x", [D, TH], F32)
    wg = load_w(p, "wg", D, L3I)
    wu = load_w(p, "wu", D, L3I)
    gsc = load_rows(p, "gsc", L3I)
    usc = load_rows(p, "usc", L3I)
    hd = p.dram("h", [L3I, TH], BF16, kind="ExternalOutput")
    hmax = p.sbuf("hmax", [128, 1], F32)
    p.t_op("vector", "memset", hmax[:], 0.0, writes=("hmax",))
    gbuf = p.sbuf("gbuf", [128, 512], F32)
    ubuf = p.sbuf("ubuf", [128, 512], F32)
    hbuf = p.sbuf("hbuf", [128, 512], F32)
    hbf = p.sbuf("hbf", [128, 512], BF16)
    hm_t = p.sbuf("hm_t", [128, 1], F32)
    MT = L3I // 128
    xt = [p.sbuf(f"xh{k}", [128, TQ], F32) for k in range(KT)]
    smg = [p.sbuf(f"sg{m}", [128, 1], F32) for m in range(MT)]
    smu = [p.sbuf(f"su{m}", [128, 1], F32) for m in range(MT)]
    first = True
    for h0 in (0, 512, 1024, 1536):
        for k in range(KT):
            p.dma("sync", xt[k][:], x_d[k * 128:(k + 1) * 128, h0:h0 + TQ],
                  writes=(f"xh{k}",))
        xn = ln_d(p, "ln", xt, TQ, com, out_dtype=F32, precise=True,
                  x_keys=[f"xh{k}" for k in range(KT)])
        qt, scx = quant_acts(p, "qa", xn, TQ, com, "gmax8", "clipx")
        if first:
            sxb = bcast_scalar(p, "sx", scx[:], tkey(scx), com)
            for m in range(MT):
                p.t_op("vector", "tensor_scalar", smg[m][:], gsc[m][:], sxb[:],
                       None, ALU.mult, reads=("gsc", tkey(sxb)),
                       writes=(tkey(smg[m]),))
                p.t_op("vector", "tensor_scalar", smu[m][:], usc[m][:], sxb[:],
                       None, ALU.mult, reads=("usc", tkey(sxb)),
                       writes=(tkey(smu[m]),))
            first = False
        for m in range(MT):
            for c0 in range(0, TQ, 512):
                psg = com.pool.get()
                psu = com.pool.get()
                for k in range(KT):
                    p.mm(psg[:, :], wg[k][:, m * 128:(m + 1) * 128],
                         qt[k][:, c0:c0 + 512], start=(k == 0),
                         stop=(k == KT - 1), reads=(f"wg_w{k}", tkey(qt[k])),
                         writes=(tkey(psg),), group=f"g{h0}_{m}_{c0}")
                for k in range(KT):
                    p.mm(psu[:, :], wu[k][:, m * 128:(m + 1) * 128],
                         qt[k][:, c0:c0 + 512], start=(k == 0),
                         stop=(k == KT - 1), reads=(f"wu_w{k}", tkey(qt[k])),
                         writes=(tkey(psu),), group=f"u{h0}_{m}_{c0}")
                p.t_op("scalar", "activation", gbuf[:], psg[:, :], AF.Silu,
                       scale=smg[m][:], reads=(tkey(psg), tkey(smg[m])),
                       writes=("gbuf",))
                p.t_op("scalar", "activation", ubuf[:], psu[:, :], AF.Copy,
                       scale=smu[m][:], reads=(tkey(psu), tkey(smu[m])),
                       writes=("ubuf",))
                p.t_op("vector", "tensor_mul", hbuf[:], gbuf[:], ubuf[:],
                       reads=("gbuf", "ubuf"), writes=("hbuf",))
                p.t_op("gpsimd", "tensor_copy", hbf[:], hbuf[:],
                       reads=("hbuf",), writes=("hbf",))
                p.t_op("vector", "reduce_max", hm_t[:], hbuf[:], AX.X,
                       apply_absolute_value=True, reads=("hbuf",),
                       writes=("hm_t",))
                p.t_op("vector", "tensor_max", hmax[:], hmax[:], hm_t[:],
                       reads=("hmax", "hm_t"), writes=("hmax",))
                p.dma("gpsimd", hd[m * 128:(m + 1) * 128, h0 + c0:h0 + c0 + 512],
                      hbf[:], reads=("hbf",))
    gm = finish_max128(p, "hgm", hmax, "hmax")
    d = p.dram("hmaxo", [1, 1], F32, kind="ExternalOutput")
    p.dma("gpsimd", d[:, :], gm[:], reads=(tkey(gm),))
    p.finish()
    return p.finalize()


def host_l3f_b(state, app):
    inp = state["inp"]
    w = state["l3f_w"]
    maps = []
    for c in range(8):
        maps.append({
            "x": xshard(state["x"], c),
            "wg": as_bf(w["wg_qT"]), "wu": as_bf(w["wu_qT"]),
            "gsc": as_f32(w["wg_sc"].reshape(-1, 1)),
            "usc": as_f32(w["wu_sc"].reshape(-1, 1)),
            "gmax8": as_f32(state["l3f_gmax8"]),
            "clipx": as_f32(inp["L3_ffn_clip_x"].reshape(1, 1)),
        })
    return maps


def post_l3f_b(state, results):
    state["l3f_h"] = [results[c]["h"] for c in range(8)]
    state["l3f_hmax8"] = np.concatenate(
        [results[c]["hmaxo"].reshape(1, 1) for c in range(8)], axis=1)


def build_l3f_c():
    p = Prog()
    com = Common(p)
    TQ = 1024
    x_d = p.dram("x", [D, TH], F32)
    wd = load_w(p, "wd", L3I, D)
    dsc = load_rows(p, "dsc", D)
    MT = L3I // 128
    hq = [p.sbuf(f"hq{j}", [128, TQ], BF16) for j in range(MT)]
    hbuf = p.sbuf("hbuf", [128, TQ], BF16)
    hd = p.dram("h", [L3I, TH], BF16)
    g8 = p.sbuf("g8", [1, 8], F32)
    p.dma("sync", g8[:], p.dram("hmax8", [1, 8], F32)[:, :], writes=("g8",))
    cl = p.sbuf("cl", [1, 1], F32)
    p.dma("sync", cl[:], p.dram("clipd", [1, 1], F32)[:, :], writes=("cl",))
    gm = p.sbuf("gmv", [1, 1], F32)
    p.t_op("vector", "reduce_max", gm[:], g8[:], AX.X, reads=("g8",), writes=("gmv",))
    sig = p.sbuf("sig", [1, 1], F32)
    p.t_op("scalar", "activation", sig[:], cl[:], AF.Sigmoid, reads=("cl",), writes=("sig",))
    sc = p.sbuf("sc", [1, 1], F32)
    p.t_op("vector", "tensor_tensor", sc[:], sig[:], gm[:], ALU.mult,
           reads=("sig", "gmv"), writes=("sc",))
    p.t_op("vector", "tensor_scalar", sc[:], sc[:], 1.0 / 127.0, None, ALU.mult,
           reads=("sc",), writes=("sc",))
    rsc = p.sbuf("rsc", [1, 1], F32)
    p.t_op("vector", "reciprocal", rsc[:], sc[:], reads=("sc",), writes=("rsc",))
    rb = bcast_scalar(p, "rb", rsc[:], "rsc", com)
    scb = bcast_scalar(p, "scb", sc[:], "sc", com)
    it = p.sbuf("i32", [128, 512], I32)
    mt_ = p.sbuf("mtt", [128, 512], F32)
    mt2 = p.sbuf("mtt2", [128, 512], F32)
    xt = [p.sbuf(f"xr{k}", [128, TQ], F32) for k in range(KT)]
    smd = [p.sbuf(f"sm{m}", [128, 1], F32) for m in range(KT)]
    for m in range(KT):
        p.t_op("vector", "tensor_scalar", smd[m][:], dsc[m][:], scb[:], None,
               ALU.mult, reads=("dsc", tkey(scb)), writes=(tkey(smd[m]),))
    d_out = p.dram("xo", [D, TH], F32, kind="ExternalOutput")
    for h0 in (0, TQ):
        for j in range(MT):
            p.dma("sync", hbuf[:], hd[j * 128:(j + 1) * 128, h0:h0 + TQ],
                  writes=("hbuf",))
            for c0 in range(0, TQ, 512):
                s = slice(c0, c0 + 512)
                p.t_op("vector", "tensor_scalar", mt_[:], hbuf[:, s], rb[:],
                       None, ALU.mult, reads=("hbuf", tkey(rb)), writes=("mtq",))
                round_ints(p, f"r{j}", mt_[:], "mtq", mt_[:], "mtq", it[:], mt2[:])
                p.t_op("vector", "tensor_scalar", mt_[:], mt_[:], 127.0, -128.0,
                       ALU.min, ALU.max, reads=("mtq",), writes=("mtq",))
                p.t_op("vector", "tensor_copy", hq[j][:, s], mt_[:],
                       reads=("mtq",), writes=(tkey(hq[j]),))
        for k in range(KT):
            p.dma("sync", xt[k][:], x_d[k * 128:(k + 1) * 128, h0:h0 + TQ],
                  writes=(f"xr{k}",))
        for m in range(KT):
            for c0 in range(0, TQ, 512):
                ps = com.pool.get()
                for j in range(MT):
                    p.mm(ps[:, :], wd[j][:, m * 128:(m + 1) * 128],
                         hq[j][:, c0:c0 + 512], start=(j == 0),
                         stop=(j == MT - 1), reads=(f"wd_w{j}", tkey(hq[j])),
                         writes=(tkey(ps),), group=f"d{h0}_{m}_{c0}")
                p.t_op("vector", "scalar_tensor_tensor", xt[m][:, c0:c0 + 512],
                       ps[:, :], smd[m][:], xt[m][:, c0:c0 + 512], ALU.mult,
                       ALU.add, reads=(tkey(ps), tkey(smd[m]), f"xr{m}"),
                       writes=(f"xr{m}",))
            p.dma("gpsimd", d_out[m * 128:(m + 1) * 128, h0:h0 + TQ],
                  xt[m][:], reads=(f"xr{m}",))
    p.finish()
    return p.finalize()


def host_l3f_c(state):
    inp = state["inp"]
    w = state["l3f_w"]
    maps = []
    for c in range(8):
        maps.append({
            "x": xshard(state["x"], c),
            "h": state["l3f_h"][c],
            "wd": as_bf(w["wd_qT"]),
            "dsc": as_f32(w["wd_sc"].reshape(-1, 1)),
            "hmax8": as_f32(state["l3f_hmax8"]),
            "clipd": as_f32(inp["L3_ffn_clip_down"].reshape(1, 1)),
        })
    return maps


def post_l3f_c(state, results):
    state["x"] = unshard_x(results)


def build_l2aa_kv():
    p = Prog()
    com = Common(p)
    TQ = 1024
    x_d = p.dram("x", [D, TH], BF16)
    wq = load_w(p, "wq", D, D)
    wk = load_w(p, "wk", D, D)
    wv = load_w(p, "wv", D, D)
    bq = load_rows(p, "bq", D)
    bk = load_rows(p, "bk", D)
    bvr = p.sbuf("bvr", [1, D], F32)
    p.dma("sync", bvr[:], p.dram("bv", [1, D], F32)[:, :], writes=("bvr",))
    bvb = bcast_row(p, "bvb", bvr[:], "bvr", D, com)
    q_d = p.dram("qo", [D, TH], BF16, kind="ExternalOutput")
    k_d = p.dram("ko", [D, TH], BF16, kind="ExternalOutput")
    v_d = p.dram("vo", [TH, D], BF16, kind="ExternalOutput")
    xt = [p.sbuf(f"xq{k}", [128, TQ], BF16) for k in range(KT)]
    vbuf = p.sbuf("vbuf", [128, 512], BF16)
    for h0 in (0, TQ):
        for k in range(KT):
            p.dma("sync", xt[k][:], x_d[k * 128:(k + 1) * 128, h0:h0 + TQ],
                  writes=(f"xq{k}",))
        xn = ln_d(p, "ln", xt, TQ, com, out_dtype=BF16, in_bf=True,
                  x_keys=[f"xq{k}" for k in range(KT)])
        for (w, bias, od, nm) in ((wq, bq, q_d, "q"), (wk, bk, k_d, "k")):
            ot = proj(p, f"{nm}{h0}", xn, w, D, TQ, com,
                      bias=[t[:] for t in bias] if bias else None,
                      bias_key=nm == "q" and "bq" or "bk",
                      evac_func="bias_add" if bias else None)
            for m in range(KT):
                p.dma("gpsimd", od[m * 128:(m + 1) * 128, h0:h0 + TQ],
                      ot[m][:], reads=(tkey(ot[m]),))
        for tt in range(TQ // 128):
            for c0 in range(0, D, 512):
                ps = com.pool.get()
                for k in range(KT):
                    p.mm(ps[:, :], xn[k][:, tt * 128:(tt + 1) * 128],
                         wv[k][:, c0:c0 + 512], start=(k == 0),
                         stop=(k == KT - 1), reads=(tkey(xn[k]), f"wv_w{k}"),
                         writes=(tkey(ps),), group=f"v{h0}_{tt}_{c0}")
                p.t_op("vector", "tensor_tensor", vbuf[:], ps[:, :],
                       bvb[:, c0:c0 + 512], ALU.add,
                       reads=(tkey(ps), tkey(bvb)), writes=("vbuf",))
                p.dma("gpsimd", v_d[h0 + tt * 128:h0 + (tt + 1) * 128,
                                    c0:c0 + 512], vbuf[:], reads=("vbuf",))
    p.finish()
    return p.finalize()


def host_l2aa_kv(state):
    inp = state["inp"]
    maps = []
    for c in range(8):
        maps.append({
            "x": as_bf(xshard(state["x"], c)),
            "wq": as_bf(inp["L2_attn_a_W_qkv"][0:3072:3].reshape(-1)[0:0] if False else qkv_slice(inp, "q").T),
            "wk": as_bf(qkv_slice(inp, "k").T),
            "wv": as_bf(qkv_slice(inp, "v").T),
            "bq": as_f32(qkv_bias(inp, "q").reshape(-1, 1)),
            "bk": as_f32(qkv_bias(inp, "k").reshape(-1, 1)),
            "bv": as_f32(qkv_bias(inp, "v").reshape(1, -1)),
        })
    return maps


def qkv_slice(inp, which):
    W = inp["L2_attn_a_W_qkv"]
    off = {"q": 0, "k": 64, "v": 128}[which]
    rows = np.concatenate([np.arange(h * 192 + off, h * 192 + off + 64)
                           for h in range(16)])
    return W[rows, :]


def qkv_bias(inp, which):
    b = inp["L2_attn_a_b_qkv"]
    off = {"q": 0, "k": 64, "v": 128}[which]
    rows = np.concatenate([np.arange(h * 192 + off, h * 192 + off + 64)
                           for h in range(16)])
    return b[rows]


def post_l2aa_kv(state, results):
    state["l2aa_q"] = [results[c]["qo"] for c in range(8)]
    state["l2aa_k"] = [results[c]["ko"] for c in range(8)]
    state["l2aa_v"] = [results[c]["vo"] for c in range(8)]


def build_l2aa_at():
    p = Prog()
    com = Common(p)
    NH, HD = 16, 64
    q_d = p.dram("q", [D, TH], BF16)
    k_d = p.dram("kf", [D, TFULL], BF16)
    v_d = p.dram("vf", [TFULL, NH * 65], BF16)
    x_d = p.dram("x", [D, TH], F32)
    wo = load_w(p, "wo", D, D)
    bo = load_rows(p, "bo", D)
    xt = load_dmajor(p, "x", TH)
    qt = [p.sbuf(f"qt{k}", [128, TH], BF16) for k in range(KT)]
    for k in range(KT):
        p.dma("sync", qt[k][:], q_d[k * 128:(k + 1) * 128, :], writes=(f"qt{k}",))
    attn = [p.sbuf(f"attn{k}", [128, TH], BF16) for k in range(KT)]
    kg = [p.sbuf(f"kg{j}", [128, TFULL], BF16) for j in range(2)]
    vg = [p.sbuf(f"vg{t}", [128, 4 * 65], BF16) for t in range(TFULL // 128)]
    psA = [p.psum(f"psA{i}", [128, 512], F32) for i in range(2)]
    psO = p.psum("psO", [65, 512], F32)
    eP = [p.sbuf(f"eP{i}", [128, 512], BF16) for i in range(2)]
    dnr = p.sbuf("dnr", [1, 512], F32)
    onb = p.sbuf("onb", [128, 512], F32)
    for g in range(4):
        for j in range(2):
            p.dma("sync", kg[j][:], k_d[g * 256 + j * 128:g * 256 + (j + 1) * 128, :],
                  writes=(f"kg{j}",))
        for t in range(TFULL // 128):
            p.dma("sync", vg[t][:], v_d[t * 128:(t + 1) * 128,
                                        g * 4 * 65:(g + 1) * 4 * 65],
                  writes=(f"vg{t}",))
        for hh in range(4):
            h = g * 4 + hh
            krow = hh * 64
            ktile = kg[hh // 2]
            kbase = 64 * (hh % 2)
            qtile = qt[h // 2]
            qbase = 64 * (h % 2)
            for qc in range(TH // 512):
                for kt in range(TFULL // 128):
                    pa = psA[kt % 2]
                    ep = eP[kt % 2]
                    p.mm(pa[0:128, :],
                         ktile[kbase:kbase + 64, kt * 128:(kt + 1) * 128],
                         qtile[qbase:qbase + 64, qc * 512:(qc + 1) * 512],
                         start=True, stop=True,
                         reads=(f"kg{hh // 2}", f"qt{h // 2}"),
                         writes=(tkey(pa),))
                    p.t_op("scalar", "activation", ep[:], pa[0:128, :], AF.Exp,
                           scale=0.125, reads=(tkey(pa),), writes=(tkey(ep),))
                    p.mm(psO[:, :], vg[kt][:, hh * 65:(hh + 1) * 65], ep[:],
                         start=(kt == 0), stop=(kt == TFULL // 128 - 1),
                         reads=(f"vg{kt}", tkey(ep)), writes=("psO",),
                         group=f"o{h}_{qc}")
                p.t_op("vector", "reciprocal", dnr[:], psO[64:65, :],
                       reads=("psO",), writes=("dnr",))
                psB = com.pool.get()
                p.mm(psB[:, :], com.ones_f[:], dnr[:], start=True, stop=True,
                     reads=("c_ones_f", "dnr"), writes=(tkey(psB),))
                p.t_op("vector", "tensor_copy", onb[:], psB[:, :],
                       reads=(tkey(psB),), writes=("onb",))
                p.t_op("vector", "tensor_tensor",
                       attn[h // 2][qbase:qbase + 64, qc * 512:(qc + 1) * 512],
                       psO[0:64, :], onb[0:64, :], ALU.mult,
                       reads=("psO", "onb"), writes=(f"attn{h // 2}",))
    xkeys = [f"x_t{k}" for k in range(KT)]
    proj(p, "out", attn, wo, D, TH, com,
         out_aps=[t[:] for t in xt], out_keys=xkeys,
         bias=[t[:] for t in bo], bias_key="bo",
         accum_into=xt, accum_keys=xkeys)
    d_out = p.dram("xo", [D, TH], F32, kind="ExternalOutput")
    for k in range(KT):
        p.dma("gpsimd", d_out[k * 128:(k + 1) * 128, :], xt[k][:], reads=(xkeys[k],))
    p.finish()
    return p.finalize()


def host_l2aa_at(state):
    inp = state["inp"]
    maps = []
    for c in range(8):
        b, hf = c // 2, c % 2
        c0, c1 = 2 * b, 2 * b + 1
        kf = np.concatenate([state["l2aa_k"][c0], state["l2aa_k"][c1]], axis=1)
        v = np.concatenate([state["l2aa_v"][c0], state["l2aa_v"][c1]], axis=0)
        vE = np.ones((TFULL, 16 * 65), BF)
        for h in range(16):
            vE[:, h * 65:h * 65 + 64] = v[:, h * 64:(h + 1) * 64]
        maps.append({
            "q": state["l2aa_q"][c].astype(BF),
            "kf": kf.astype(BF), "vf": vE,
            "x": xshard(state["x"], c),
            "wo": as_bf(inp["L2_attn_a_W_out"].T),
            "bo": as_f32(inp["L2_attn_a_b_out"].reshape(-1, 1)),
        })
    return maps


def post_l2aa_at(state, results):
    state["x"] = unshard_x(results)


def lerp_mix(p, tag, h, hkeys, tm, tmkey, out_tiles, TQ, com, fm_key=None):
    for k in range(len(h)):
        e = ["vector", "gpsimd"][k % 2]
        ok = tkey(out_tiles[k])
        cur = h[k][:, 1:TQ + 1]
        prev = h[k][:, 0:TQ]
        p.t_op(e, "tensor_sub", out_tiles[k][:, 0:TQ], cur, prev,
               reads=(hkeys[k],), writes=(ok,))
        p.t_op(e, "tensor_scalar", out_tiles[k][:, 0:TQ], out_tiles[k][:, 0:TQ],
               tm[k][:], None, ALU.mult, reads=(ok, tmkey), writes=(ok,))
        p.t_op(e, "tensor_add", out_tiles[k][:, 0:TQ], out_tiles[k][:, 0:TQ],
               prev, reads=(ok, hkeys[k]), writes=(ok,))


def build_l0a_proj():
    p = Prog()
    com = Common(p)
    TQ = 512
    x_d = p.dram("x", [D, TH + 1], BF16)
    wk = load_w(p, "wk", D, D)
    wv = load_w(p, "wv", D, D)
    wr = load_w(p, "wr", D, D)
    l1g = load_rows(p, "l1g", D)
    l1b = load_rows(p, "l1b", D)
    tmk = load_rows(p, "tmk", D)
    tmv = load_rows(p, "tmv", D)
    tmr = load_rows(p, "tmr", D)
    fm = p.sbuf("fmv", [128, 1], F32)
    p.dma("sync", fm[:], p.dram("fm", [128, 1], F32)[:, :], writes=("fmv",))
    k_d = p.dram("ko", [D, TH], BF16, kind="ExternalOutput")
    v_d = p.dram("vo", [D, TH], BF16, kind="ExternalOutput")
    s_d = p.dram("so", [D, TH], BF16, kind="ExternalOutput")
    xt = [p.sbuf(f"xq{k}", [128, TQ + 1], BF16) for k in range(KT)]
    mix = [p.sbuf(f"mx{k}", [128, TQ], BF16) for k in range(KT)]
    for q0 in range(0, TH, TQ):
        for k in range(KT):
            p.dma("sync", xt[k][:], x_d[k * 128:(k + 1) * 128, q0:q0 + TQ + 1],
                  writes=(f"xq{k}",))
        h0 = ln_d(p, "h0", xt, TQ + 1, com, out_dtype=BF16, in_bf=True,
                  x_keys=[f"xq{k}" for k in range(KT)])
        h1 = ln_d(p, "h1", h0, TQ + 1, com, g=[t[:] for t in l1g],
                  b=[t[:] for t in l1b], gb_key="l1g", out_dtype=BF16,
                  in_bf=True, scratch="lnw2")
        if q0 == 0:
            for k in range(KT):
                p.t_op("vector", "tensor_scalar", h1[k][:, 0:1], h1[k][:, 0:1],
                       fm[:], None, ALU.mult, reads=(tkey(h1[k]), "fmv"),
                       writes=(tkey(h1[k]),))
        for (tm, tmn, w, od, ef) in ((tmk, "tmk", wk, k_d, None),
                                     (tmv, "tmv", wv, v_d, None),
                                     (tmr, "tmr", wr, s_d, AF.Sigmoid)):
            lerp_mix(p, tmn, h1, keys(h1), tm, tmn, mix, TQ, com)
            ot = proj(p, f"p{tmn}", mix, w, D, TQ, com, evac_func=ef)
            for m in range(KT):
                p.dma("gpsimd", od[m * 128:(m + 1) * 128, q0:q0 + TQ],
                      ot[m][:], reads=(tkey(ot[m]),))
    p.finish()
    return p.finalize()


def host_l0a_proj(state, pref="L0_attn_a"):
    inp = state["inp"]
    maps = []
    for c in range(8):
        xs = xshard(state["x"], c, halo=1)
        maps.append({
            "x": as_bf(xs),
            "wk": as_bf(inp[f"{pref}_Wk"].T), "wv": as_bf(inp[f"{pref}_Wv"].T),
            "wr": as_bf(inp[f"{pref}_Wr"].T),
            "l1g": as_f32(inp[f"{pref}_ln1_g"].reshape(-1, 1)),
            "l1b": as_f32(inp[f"{pref}_ln1_b"].reshape(-1, 1)),
            "tmk": as_f32(inp[f"{pref}_tm_k"].reshape(-1, 1)),
            "tmv": as_f32(inp[f"{pref}_tm_v"].reshape(-1, 1)),
            "tmr": as_f32(inp[f"{pref}_tm_r"].reshape(-1, 1)),
            "fm": as_f32(np.full((128, 1), 0.0 if c % 2 == 0 else 1.0, np.float32)),
        })
    return maps


def post_l0a_proj(state, results):
    for nm, key in (("ko", "l0a_k"), ("vo", "l0a_v"), ("so", "l0a_sr")):
        full = np.zeros((NB, D, TFULL), BF)
        for c in range(8):
            b, h = c // 2, c % 2
            full[b, :, h * TH:(h + 1) * TH] = results[c][nm]
        state[key] = full


def build_l0a_wkv():
    p = Prog()
    com = Common(p)
    NL = 512
    k_d = p.dram("k", [NL, TFULL], BF16)
    v_d = p.dram("v", [NL, TFULL], BF16)
    s_d = p.dram("sr", [NL, TFULL], BF16)
    td = load_rows(p, "td", NL)
    tf = load_rows(p, "tf", NL)
    o_d = p.dram("yo", [NL, TFULL], F32, kind="ExternalOutput")
    kt = p.sbuf("ktl", [128, TFULL], BF16)
    vt = p.sbuf("vtl", [128, TFULL], BF16)
    st = p.sbuf("stl", [128, TFULL], BF16)
    EK = p.sbuf("EK", [128, TFULL], F32)
    EKV = p.sbuf("EKV", [128, TFULL], F32)
    num = p.sbuf("num", [128, TFULL], F32)
    den = p.sbuf("den", [128, TFULL], F32)
    ab = p.sbuf("ab", [128, TFULL], F32)
    for lt in range(NL // 128):
        r = slice(lt * 128, (lt + 1) * 128)
        p.dma("sync", kt[:], k_d[r, :], writes=("ktl",))
        p.dma("sync", vt[:], v_d[r, :], writes=("vtl",))
        p.dma("sync", st[:], s_d[r, :], writes=("stl",))
        w1 = p.sbuf("w1", [128, 1], F32)
        eu = p.sbuf("eu", [128, 1], F32)
        p.t_op("scalar", "activation", w1[:], td[lt][:], AF.Exp,
               reads=("td",), writes=("w1",))
        p.t_op("scalar", "activation", w1[:], w1[:], AF.Exp, scale=-1.0,
               reads=("w1",), writes=("w1",))
        p.t_op("scalar", "activation", eu[:], tf[lt][:], AF.Exp,
               reads=("tf",), writes=("eu",))
        p.t_op("vector", "memset", ab[:], 1.0, writes=("ab",))
        p.t_op("vector", "tensor_scalar", ab[:], ab[:], w1[:], None, ALU.mult,
               reads=("ab", "w1"), writes=("ab",))
        p.t_op("scalar", "activation", EK[:], kt[:], AF.Exp,
               reads=("ktl",), writes=("EK",))
        p.t_op("vector", "tensor_tensor", EKV[:], EK[:], vt[:], ALU.mult,
               reads=("EK", "vtl"), writes=("EKV",))
        p.t_op("vector", "tensor_tensor_scan", num[:], ab[:], EKV[:], 0.0,
               ALU.mult, ALU.add, reads=("ab", "EKV"), writes=("num",))
        p.t_op("vector", "tensor_tensor_scan", den[:], ab[:], EK[:], 0.0,
               ALU.mult, ALU.add, reads=("ab", "EK"), writes=("den",))
        p.t_op("vector", "tensor_scalar", EKV[:], EKV[:], eu[:], None, ALU.mult,
               reads=("EKV", "eu"), writes=("EKV",))
        p.t_op("vector", "tensor_tensor", EKV[:, 1:TFULL], EKV[:, 1:TFULL],
               num[:, 0:TFULL - 1], ALU.add, reads=("EKV", "num"), writes=("EKV",))
        p.t_op("vector", "tensor_scalar", EK[:], EK[:], eu[:], None, ALU.mult,
               reads=("EK", "eu"), writes=("EK",))
        p.t_op("vector", "tensor_tensor", EK[:, 1:TFULL], EK[:, 1:TFULL],
               den[:, 0:TFULL - 1], ALU.add, reads=("EK", "den"), writes=("EK",))
        p.t_op("vector", "reciprocal", num[:], EK[:], reads=("EK",), writes=("num",))
        p.t_op("vector", "tensor_tensor", EKV[:], EKV[:], num[:], ALU.mult,
               reads=("EKV", "num"), writes=("EKV",))
        p.t_op("vector", "tensor_tensor", EKV[:], EKV[:], st[:], ALU.mult,
               reads=("EKV", "stl"), writes=("EKV",))
        p.dma("gpsimd", o_d[r, :], EKV[:], reads=("EKV",))
    p.finish()
    return p.finalize()


def host_l0a_wkv(state, pref="L0_attn_a"):
    inp = state["inp"]
    maps = []
    for c in range(8):
        b, h = c // 2, c % 2
        ch = slice(h * 512, (h + 1) * 512)
        maps.append({
            "k": np.ascontiguousarray(state["l0a_k"][b][ch]),
            "v": np.ascontiguousarray(state["l0a_v"][b][ch]),
            "sr": np.ascontiguousarray(state["l0a_sr"][b][ch]),
            "td": as_f32(inp[f"{pref}_time_decay"][ch].reshape(-1, 1)),
            "tf": as_f32(inp[f"{pref}_time_first"][ch].reshape(-1, 1)),
        })
    return maps


def post_l0a_wkv(state, results):
    full = np.zeros((NB, D, TFULL), np.float32)
    for c in range(8):
        b, h = c // 2, c % 2
        full[b, h * 512:(h + 1) * 512, :] = results[c]["yo"]
    state["l0a_rwkv"] = full


def build_l0a_cm1():
    p = Prog()
    com = Common(p)
    TQ = 512
    x_d = p.dram("x", [D, TH + 1], F32)
    r_d = p.dram("rw", [D, TH + 1], BF16)
    wo = load_w(p, "wo", D, D)
    wcr = load_w(p, "wcr", D, D)
    wck = load_w(p, "wck", D, 4096)
    l2g = load_rows(p, "l2g", D)
    l2b = load_rows(p, "l2b", D)
    cmk = load_rows(p, "cmk", D)
    cmr = load_rows(p, "cmr", D)
    fm = p.sbuf("fmv", [128, 1], F32)
    p.dma("sync", fm[:], p.dram("fm", [128, 1], F32)[:, :], writes=("fmv",))
    xs_d = p.dram("xs", [D, TH], F32, kind="ExternalOutput")
    kk_d = p.dram("kk", [4096, TH], BF16, kind="ExternalOutput")
    sr_d = p.dram("sr2", [D, TH], BF16, kind="ExternalOutput")
    xt = [p.sbuf(f"xq{k}", [128, TQ + 1], F32) for k in range(KT)]
    rt = [p.sbuf(f"rq{k}", [128, TQ + 1], BF16) for k in range(KT)]
    mix = [p.sbuf(f"mx{k}", [128, TQ], BF16) for k in range(KT)]
    for q0 in range(0, TH, TQ):
        xkeys = [f"xq{k}" for k in range(KT)]
        for k in range(KT):
            p.dma("sync", xt[k][:], x_d[k * 128:(k + 1) * 128, q0:q0 + TQ + 1],
                  writes=(xkeys[k],))
            p.dma("sync", rt[k][:], r_d[k * 128:(k + 1) * 128, q0:q0 + TQ + 1],
                  writes=(f"rq{k}",))
        h0 = ln_d(p, "h0", xt, TQ + 1, com, out_dtype=F32, x_keys=xkeys)
        proj(p, "xl", rt, wo, D, TQ + 1, com,
             out_aps=[t[:] for t in h0], out_keys=keys(h0),
             accum_into=h0, accum_keys=keys(h0))
        for k in range(KT):
            p.t_op("vector", "tensor_add", xt[k][:, 1:TQ + 1], xt[k][:, 1:TQ + 1],
                   h0[k][:, 1:TQ + 1], reads=(xkeys[k], tkey(h0[k])),
                   writes=(xkeys[k],))
            p.dma("gpsimd", xs_d[k * 128:(k + 1) * 128, q0:q0 + TQ],
                  xt[k][:, 1:TQ + 1], reads=(xkeys[k],))
        h2 = ln_d(p, "h2", h0, TQ + 1, com, g=[t[:] for t in l2g],
                  b=[t[:] for t in l2b], gb_key="l2g", out_dtype=BF16)
        if q0 == 0:
            for k in range(KT):
                p.t_op("vector", "tensor_scalar", h2[k][:, 0:1], h2[k][:, 0:1],
                       fm[:], None, ALU.mult, reads=(tkey(h2[k]), "fmv"),
                       writes=(tkey(h2[k]),))
        lerp_mix(p, "cmk", h2, keys(h2), cmk, "cmk", mix, TQ, com)
        kbuf = [p.sbuf(f"kbuf{i}", [128, TQ], BF16) for i in range(2)]
        for m in range(32):
            ps = com.pool.get()
            for k in range(KT):
                p.mm(ps[:, :], wck[k][:, m * 128:(m + 1) * 128], mix[k][:],
                     start=(k == 0), stop=(k == KT - 1),
                     reads=(f"wck_w{k}", tkey(mix[k])), writes=(tkey(ps),),
                     group=f"kk{q0}_{m}")
            kb = kbuf[m % 2]
            p.t_op("scalar", "activation", kb[:], ps[:, :], AF.Relu,
                   reads=(tkey(ps),), writes=(tkey(kb),))
            p.t_op("vector", "tensor_mul", kb[:], kb[:], kb[:],
                   reads=(tkey(kb),), writes=(tkey(kb),))
            p.dma("gpsimd", kk_d[m * 128:(m + 1) * 128, q0:q0 + TQ], kb[:],
                  reads=(tkey(kb),))
        lerp_mix(p, "cmr", h2, keys(h2), cmr, "cmr", mix, TQ, com)
        srt = proj(p, "sr", mix, wcr, D, TQ, com, evac_func=AF.Sigmoid)
        for m in range(KT):
            p.dma("gpsimd", sr_d[m * 128:(m + 1) * 128, q0:q0 + TQ],
                  srt[m][:], reads=(tkey(srt[m]),))
    p.finish()
    return p.finalize()


def host_l0a_cm1(state, pref="L0_attn_a"):
    inp = state["inp"]
    maps = []
    for c in range(8):
        b, h = c // 2, c % 2
        t0 = h * TH - 1
        rw = np.zeros((D, TH + 1), np.float32)
        lo = max(t0, 0)
        rw[:, (1 if t0 < 0 else 0):] = state["l0a_rwkv"][b][:, lo:(h + 1) * TH]
        maps.append({
            "x": xshard(state["x"], c, halo=1),
            "rw": as_bf(rw),
            "wo": as_bf(inp[f"{pref}_Wo"].T),
            "wcr": as_bf(inp[f"{pref}_Wcr"].T),
            "wck": as_bf(inp[f"{pref}_Wck"].T),
            "l2g": as_f32(inp[f"{pref}_ln2_g"].reshape(-1, 1)),
            "l2b": as_f32(inp[f"{pref}_ln2_b"].reshape(-1, 1)),
            "cmk": as_f32(inp[f"{pref}_cm_k"].reshape(-1, 1)),
            "cmr": as_f32(inp[f"{pref}_cm_r"].reshape(-1, 1)),
            "fm": as_f32(np.full((128, 1), 0.0 if h == 0 else 1.0, np.float32)),
        })
    return maps


def post_l0a_cm1(state, results):
    state["l0a_xs"] = [results[c]["xs"] for c in range(8)]
    state["l0a_kk"] = [results[c]["kk"] for c in range(8)]
    state["l0a_sr2"] = [results[c]["sr2"] for c in range(8)]


def build_l0a_cm2():
    p = Prog()
    com = Common(p)
    TQ = 512
    xs_d = p.dram("xs", [D, TH], F32)
    kk_d = p.dram("kk", [4096, TH], BF16)
    sr_d = p.dram("sr2", [D, TH], BF16)
    wcv = load_w(p, "wcv", 4096, D)
    d_out = p.dram("xo", [D, TH], F32, kind="ExternalOutput")
    kkt = [p.sbuf(f"kq{j}", [128, TQ], BF16) for j in range(32)]
    srt = [p.sbuf(f"sq{m}", [128, TQ], BF16) for m in range(KT)]
    xt = [p.sbuf(f"xq{m}", [128, TQ], F32) for m in range(KT)]
    for q0 in range(0, TH, TQ):
        for j in range(32):
            p.dma("sync", kkt[j][:], kk_d[j * 128:(j + 1) * 128, q0:q0 + TQ],
                  writes=(f"kq{j}",))
        for m in range(KT):
            p.dma("sync", srt[m][:], sr_d[m * 128:(m + 1) * 128, q0:q0 + TQ],
                  writes=(f"sq{m}",))
            p.dma("sync", xt[m][:], xs_d[m * 128:(m + 1) * 128, q0:q0 + TQ],
                  writes=(f"xq{m}",))
        for m in range(KT):
            ps = com.pool.get()
            for j in range(32):
                p.mm(ps[:, :], wcv[j][:, m * 128:(m + 1) * 128], kkt[j][:],
                     start=(j == 0), stop=(j == 31),
                     reads=(f"wcv_w{j}", f"kq{j}"), writes=(tkey(ps),),
                     group=f"kv{q0}_{m}")
            p.t_op("vector", "tensor_tensor", srt[m][:], srt[m][:], ps[:, :],
                   ALU.mult, reads=(f"sq{m}", tkey(ps)), writes=(f"sq{m}",))
            p.t_op("vector", "tensor_tensor", xt[m][:], xt[m][:], srt[m][:],
                   ALU.add, reads=(f"xq{m}", f"sq{m}"), writes=(f"xq{m}",))
            p.dma("gpsimd", d_out[m * 128:(m + 1) * 128, q0:q0 + TQ],
                  xt[m][:], reads=(f"xq{m}",))
    p.finish()
    return p.finalize()


def host_l0a_cm2(state, pref="L0_attn_a"):
    inp = state["inp"]
    return [{"xs": as_f32(state["l0a_xs"][c]),
             "kk": state["l0a_kk"][c].astype(BF),
             "sr2": state["l0a_sr2"][c].astype(BF),
             "wcv": as_bf(inp[f"{pref}_Wcv"].T)} for c in range(8)]


def post_l0a_cm2(state, results):
    x = np.zeros((NB, TFULL, D), np.float32)
    for c in range(8):
        b, h = c // 2, c % 2
        x[b, h * TH:(h + 1) * TH, :] = results[c]["xo"].T
    state["x"] = x


def build_l0f():
    p = Prog()
    com = Common(p)
    TQ = 512
    x_d = p.dram("x", [D, TH], F32)
    hms = load_w(p, "hm", D, D)
    w_d = p.dram("w", [D, D], F32)
    ident = load_ident(p)
    d_out = p.dram("xo", [D, TH], F32, kind="ExternalOutput")
    wt = []
    for k in range(KT):
        t = p.sbuf(f"wrow{k}", [128, D], F32)
        p.dma("sync", t[:], w_d[k * 128:(k + 1) * 128, :], writes=(f"wrow{k}",))
        wt.append(t)
    wq = [p.sbuf(f"wq{k}", [128, D], BF16) for k in range(KT)]
    it = p.sbuf("i32w", [128, D], I32)
    mt2 = p.sbuf("mt2w", [128, D], F32)
    wsc = p.sbuf("wsc", [128, KT], F32)
    for k in range(KT):
        am = p.sbuf("wam", [128, 1], F32)
        p.t_op("vector", "tensor_reduce", am[:], wt[k][:], AX.X, ALU.add,
               apply_absolute_value=True, reads=(f"wrow{k}",), writes=("wam",))
        p.t_op("vector", "tensor_scalar", wsc[:, k:k + 1], am[:], 1.0 / D, None,
               ALU.mult, reads=("wam",), writes=("wsc",))
        rs = p.sbuf("wrs", [128, 1], F32)
        p.t_op("vector", "reciprocal", rs[:], wsc[:, k:k + 1], reads=("wsc",),
               writes=("wrs",))
        p.t_op("vector", "tensor_scalar", wt[k][:], wt[k][:], rs[:], None,
               ALU.mult, reads=(f"wrow{k}", "wrs"), writes=(f"wrow{k}",))
        p.t_op("vector", "tensor_scalar", wt[k][:], wt[k][:], 1.0, -1.0,
               ALU.min, ALU.max, reads=(f"wrow{k}",), writes=(f"wrow{k}",))
        round_ints(p, f"wr{k}", wt[k][:], f"wrow{k}", wt[k][:], f"wrow{k}",
                   it[:], mt2[:])
        wb = p.sbuf("wbf", [128, D], BF16)
        p.t_op("vector", "tensor_copy", wb[:], wt[k][:], reads=(f"wrow{k}",),
               writes=("wbf",))
        for ct in range(KT):
            transpose_128(p, "wtr", wb[:, ct * 128:(ct + 1) * 128], "wbf",
                          wq[ct][:, k * 128:(k + 1) * 128], f"wq{ct}", ident, com)
    xt = [p.sbuf(f"xq{k}", [128, TQ], F32) for k in range(KT)]
    for q0 in range(0, TH, TQ):
        xkeys = [f"xq{k}" for k in range(KT)]
        for k in range(KT):
            p.dma("sync", xt[k][:], x_d[k * 128:(k + 1) * 128, q0:q0 + TQ],
                  writes=(xkeys[k],))
        xn = ln_d(p, "ln", xt, TQ, com, out_dtype=F32, precise=True,
                  x_keys=xkeys)
        xhi = [p.sbuf(f"xhi{k}", [128, TQ], BF16) for k in range(KT)]
        xlo = [p.sbuf(f"xlo{k}", [128, TQ], BF16) for k in range(KT)]
        for k in range(KT):
            e = ["vector", "gpsimd"][k % 2]
            p.t_op(e, "tensor_copy", xhi[k][:], xn[k][:], reads=(tkey(xn[k]),),
                   writes=(tkey(xhi[k]),))
            p.t_op(e, "tensor_sub", xlo[k][:], xn[k][:], xhi[k][:],
                   reads=(tkey(xn[k]), tkey(xhi[k])), writes=(tkey(xlo[k]),))
        xqiT = [p.sbuf(f"xqiT{k}", [128, TQ], BF16) for k in range(KT)]
        sxr = p.sbuf("sxr", [1, TQ], F32)
        for tt in range(TQ // 128):
            pst = com.pool.get()
            xhbuf = p.sbuf("xhbuf", [128, D], F32)
            for c0 in range(0, D, 512):
                for k in range(KT):
                    p.mm(pst[:, :], xhi[k][:, tt * 128:(tt + 1) * 128],
                         hms[k][:, c0:c0 + 512], start=(k == 0), stop=False,
                         reads=(tkey(xhi[k]), f"hm_w{k}"), writes=(tkey(pst),),
                         group=f"xh{q0}_{tt}_{c0}")
                for k in range(KT):
                    p.mm(pst[:, :], xlo[k][:, tt * 128:(tt + 1) * 128],
                         hms[k][:, c0:c0 + 512], start=False, stop=(k == KT - 1),
                         reads=(tkey(xlo[k]), f"hm_w{k}"), writes=(tkey(pst),),
                         group=f"xh{q0}_{tt}_{c0}")
                p.t_op("vector", "tensor_copy", xhbuf[:, c0:c0 + 512],
                       pst[:, :], reads=(tkey(pst),), writes=("xhbuf",))
                pst = com.pool.get()
            amx = p.sbuf("amx", [128, 1], F32)
            p.t_op("vector", "reduce_max", amx[:], xhbuf[:], AX.X,
                   apply_absolute_value=True, reads=("xhbuf",), writes=("amx",))
            p.t_op("vector", "tensor_scalar", amx[:], amx[:], 1e-5, 1.0 / 7.0,
                   ALU.max, ALU.mult, reads=("amx",), writes=("amx",))
            rs = p.sbuf("qrs", [128, 1], F32)
            p.t_op("vector", "reciprocal", rs[:], amx[:], reads=("amx",),
                   writes=("qrs",))
            p.t_op("vector", "tensor_scalar", xhbuf[:], xhbuf[:], rs[:], None,
                   ALU.mult, reads=("xhbuf", "qrs"), writes=("xhbuf",))
            it2 = p.sbuf("i32q", [128, D], I32)
            mtq = p.sbuf("mtq2", [128, D], F32)
            round_ints(p, "qr", xhbuf[:], "xhbuf", xhbuf[:], "xhbuf", it2[:],
                       mtq[:])
            xqb = p.sbuf("xqb", [128, D], BF16)
            p.t_op("vector", "tensor_copy", xqb[:], xhbuf[:], reads=("xhbuf",),
                   writes=("xqb",))
            dscr = p.dram("sx_scr", [128, 1], F32, kind="Internal")
            p.dma("gpsimd", dscr[:, :], amx[:], reads=("amx",), writes=("sx_scr",))
            p.dma("gpsimd", sxr[:, tt * 128:(tt + 1) * 128],
                  dscr[:, :].rearrange("p one -> one p"),
                  reads=("sx_scr",), writes=("sxr",))
            for ct in range(KT):
                transpose_128(p, "xtr", xqb[:, ct * 128:(ct + 1) * 128], "xqb",
                              xqiT[ct][:, tt * 128:(tt + 1) * 128],
                              f"xqiT{ct}", ident, com)
        sxb = bcast_row(p, "sxb", sxr[:], "sxr", TQ, com)
        for m in range(KT):
            ps = com.pool.get()
            for k in range(KT):
                p.mm(ps[:, :], wq[k][:, m * 128:(m + 1) * 128], xqiT[k][:],
                     start=(k == 0), stop=(k == KT - 1),
                     reads=(f"wq{k}", tkey(xqiT[k])), writes=(tkey(ps),),
                     group=f"y{q0}_{m}")
            yb = p.sbuf("ybuf", [128, TQ], F32)
            p.t_op("vector", "tensor_scalar", yb[:], ps[:, :], wsc[:, m:m + 1],
                   None, ALU.mult, reads=(tkey(ps), "wsc"), writes=("ybuf",))
            p.t_op("vector", "tensor_mul", yb[:], yb[:], sxb[:],
                   reads=("ybuf", tkey(sxb)), writes=("ybuf",))
            p.t_op("vector", "tensor_add", xt[m][:], xt[m][:], yb[:],
                   reads=(xkeys[m], "ybuf"), writes=(xkeys[m],))
            p.dma("gpsimd", d_out[m * 128:(m + 1) * 128, q0:q0 + TQ],
                  xt[m][:], reads=(xkeys[m],))
    p.finish()
    return p.finalize()


def _hadamard1024():
    h = np.array([[1.0]], dtype=np.float64)
    while h.shape[0] < 1024:
        h = np.block([[h, h], [h, -h]])
    return (h / 32.0).astype(np.float32)


def host_l0f(state):
    inp = state["inp"]
    H = _hadamard1024()
    return [{"x": xshard(state["x"], c), "hm": as_bf(H),
             "w": as_f32(inp["L0_ffn_weight"]), "ident": get_ident()}
            for c in range(8)]


def post_l0f(state, results):
    state["x"] = unshard_x(results)


def build_resproj(din=D, with_bias=False):
    p = Prog()
    com = Common(p)
    x_d = p.dram("x", [D, TH], F32)
    y_d = p.dram("y", [din, TH], BF16)
    wo = load_w(p, "wo", din, D)
    bias = load_rows(p, "bo", D) if with_bias else None
    xt = load_dmajor(p, "x", TH)
    yt = []
    for k in range(din // 128):
        t = p.sbuf(f"yy{k}", [128, TH], BF16)
        p.dma("sync", t[:], y_d[k * 128:(k + 1) * 128, :], writes=(f"yy{k}",))
        yt.append(t)
    xkeys = [f"x_t{k}" for k in range(KT)]
    proj(p, "o", yt, wo, D, TH, com, out_aps=[t[:] for t in xt],
         out_keys=xkeys, accum_into=xt, accum_keys=xkeys,
         bias=[t[:] for t in bias] if with_bias else None,
         bias_key="bo" if with_bias else None)
    d_out = p.dram("xo", [D, TH], F32, kind="ExternalOutput")
    for k in range(KT):
        p.dma("gpsimd", d_out[k * 128:(k + 1) * 128, :], xt[k][:], reads=(xkeys[k],))
    p.finish()
    return p.finalize()


def host_resproj(state, ykey, wkey, bkey=None, wT=True):
    inp = state["inp"]
    maps = []
    for c in range(8):
        b, h = c // 2, c % 2
        y = state[ykey][b][:, h * TH:(h + 1) * TH]
        m = {"x": xshard(state["x"], c), "y": y.astype(BF),
             "wo": as_bf(inp[wkey].T if wT else inp[wkey])}
        if bkey:
            m["bo"] = as_f32(inp[bkey].reshape(-1, 1))
        maps.append(m)
    return maps


def build_addres():
    p = Prog()
    x_d = p.dram("x", [D, TH], F32)
    y_d = p.dram("y", [D, TH], F32)
    xt = load_dmajor(p, "x", TH)
    yt = load_dmajor(p, "y", TH)
    d_out = p.dram("xo", [D, TH], F32, kind="ExternalOutput")
    for k in range(KT):
        p.t_op("vector", "tensor_add", xt[k][:], xt[k][:], yt[k][:],
               reads=(f"x_t{k}", f"y_t{k}"), writes=(f"x_t{k}",))
        p.dma("gpsimd", d_out[k * 128:(k + 1) * 128, :], xt[k][:],
              reads=(f"x_t{k}",))
    p.finish()
    return p.finalize()


def host_addres(state, ykey):
    maps = []
    for c in range(8):
        b, h = c // 2, c % 2
        y = state[ykey][b][:, h * TH:(h + 1) * TH]
        maps.append({"x": xshard(state["x"], c), "y": as_f32(y)})
    return maps


def build_l2f_a():
    p = Prog()
    com = Common(p)
    TQ = 2048
    x_d = p.dram("x", [D, TH], BF16)
    w1_d = p.dram("w1", [128, 32 * 128], BF16)
    w2_d = p.dram("w2", [128, 32 * 512], BF16)
    h_d = p.dram("h", [16384, TH], BF16, kind="ExternalOutput")
    xt = [p.sbuf(f"xq{k}", [128, TQ], BF16) for k in range(KT)]
    for k in range(KT):
        p.dma("sync", xt[k][:], x_d[k * 128:(k + 1) * 128, :], writes=(f"xq{k}",))
    xn = ln_d(p, "ln", xt, TQ, com, out_dtype=BF16, in_bf=True,
              x_keys=[f"xq{k}" for k in range(KT)])
    w1b = p.sbuf("w1b", [128, 128], BF16)
    w2b = p.sbuf("w2b", [128, 512], BF16)
    bd = [p.sbuf(f"bd{i}", [128, 512], BF16) for i in range(32)]
    for rg in range(8):
        for h in range(4):
            i = rg * 4 + h
            p.dma("sync", w1b[:], w1_d[:, i * 128:(i + 1) * 128], writes=("w1b",))
            p.dma("sync", w2b[:], w2_d[:, i * 512:(i + 1) * 512], writes=("w2b",))
            ps = com.pool.get()
            p.mm(ps[:, :], w1b[:], w2b[:], start=True, stop=True,
                 reads=("w1b", "w2b"), writes=(tkey(ps),))
            p.t_op("vector", "tensor_copy", bd[i][:], ps[:, :],
                   reads=(tkey(ps),), writes=(f"bd{i}",))
    hbuf = p.sbuf("hbuf", [128, 512], BF16)
    for rg in range(8):
        for h in range(4):
            i = rg * 4 + h
            for m in range(4):
                for c0 in range(0, TQ, 512):
                    ps = com.pool.get()
                    p.mm(ps[:, :], bd[i][:, m * 128:(m + 1) * 128],
                         xn[rg][:, c0:c0 + 512], start=True, stop=True,
                         reads=(f"bd{i}", tkey(xn[rg])), writes=(tkey(ps),))
                    p.t_op("scalar", "activation", hbuf[:], ps[:, :], AF.Gelu,
                           reads=(tkey(ps),), writes=("hbuf",))
                    row = rg * 2048 + h * 512 + m * 128
                    p.dma("gpsimd", h_d[row:row + 128, c0:c0 + 512], hbuf[:],
                          reads=("hbuf",))
    p.finish()
    return p.finalize()


def host_l2f_a(state):
    inp = state["inp"]
    K = inp["L2_ffn_rail_maker"]
    U = inp["L2_ffn_rail_up_proj"]
    W1 = np.zeros((128, 32 * 128), np.float32)
    W2 = np.zeros((128, 32 * 512), np.float32)
    for rg in range(8):
        for h in range(4):
            i = rg * 4 + h
            for rl in range(8):
                r = rg * 8 + rl
                W1[rl * 16:(rl + 1) * 16, i * 128 + rl * 16:i * 128 + (rl + 1) * 16] = K[h, r].T
                W2[rl * 16:(rl + 1) * 16, i * 512 + rl * 64:i * 512 + (rl + 1) * 64] = U[h, r].T
    maps = []
    for c in range(8):
        maps.append({"x": as_bf(xshard(state["x"], c)),
                     "w1": as_bf(W1), "w2": as_bf(W2)})
    return maps


def post_l2f_a(state, results):
    state["l2f_h"] = [results[c]["h"] for c in range(8)]


def build_l2f_b():
    p = Prog()
    com = Common(p, npool=8)
    HI = 16384
    h_d = p.dram("h", [HI, TH], BF16)
    w_d = p.dram("w", [HI, D], BF16)
    x_d = p.dram("x", [D, TH], F32)
    xt = load_dmajor(p, "x", TH)
    xkeys = [f"x_t{k}" for k in range(KT)]
    wb = [p.sbuf(f"wb{i}", [128, D], BF16) for i in range(2)]
    hb = [p.sbuf(f"hb{i}", [128, 512], BF16) for i in range(2)]
    xr = [p.sbuf(f"xr{i}", [128, 512], F32) for i in range(2)]
    d_out = p.dram("xo", [D, TH], F32, kind="ExternalOutput")
    for c0 in range(0, TH, 512):
        pss = [com.pool.get() for _ in range(KT)]
        for j in range(HI // 128):
            jb = j % 2
            p.dma("sync", wb[jb][:], w_d[j * 128:(j + 1) * 128, :],
                  writes=(f"wb{jb}",))
            p.dma("scalar", hb[jb][:], h_d[j * 128:(j + 1) * 128, c0:c0 + 512],
                  writes=(f"hb{jb}",))
            for m in range(KT):
                p.mm(pss[m][:, :], wb[jb][:, m * 128:(m + 1) * 128], hb[jb][:],
                     start=(j == 0), stop=(j == HI // 128 - 1),
                     reads=(f"wb{jb}", f"hb{jb}"), writes=(tkey(pss[m]),),
                     group=f"cp{c0}_{m}")
        for m in range(KT):
            xm = xr[m % 2]
            p.dma("sync", xm[:], x_d[m * 128:(m + 1) * 128, c0:c0 + 512],
                  writes=(f"xr{m % 2}",))
            p.t_op("vector", "tensor_tensor", xm[:], xm[:], pss[m][:, :],
                   ALU.add, reads=(f"xr{m % 2}", tkey(pss[m])),
                   writes=(f"xr{m % 2}",))
            p.dma("gpsimd", d_out[m * 128:(m + 1) * 128, c0:c0 + 512],
                  xm[:], reads=(f"xr{m % 2}",))
    p.finish()
    return p.finalize()


def host_l2f_b(state):
    inp = state["inp"]
    if "l2f_WT" not in state:
        W = inp["L2_ffn_w_cproj"]
        perm = np.zeros(16384, np.int64)
        pos = 0
        for rg in range(8):
            for h in range(4):
                for rl in range(8):
                    r = rg * 8 + rl
                    perm[pos:pos + 64] = h * 4096 + r * 64 + np.arange(64)
                    pos += 64
        state["l2f_WT"] = as_bf(W[:, perm].T)
    WT = state["l2f_WT"]
    return [{"h": state["l2f_h"][c], "w": WT,
             "x": xshard(state["x"], c)} for c in range(8)]


def post_l2f_b(state, results):
    state["x"] = unshard_x(results)


def build_l1ab_a():
    p = Prog()
    com = Common(p)
    x_d = p.dram("x", [D, TH], BF16)
    xt = [p.sbuf(f"xq{k}", [128, TH], BF16) for k in range(KT)]
    for k in range(KT):
        p.dma("sync", xt[k][:], x_d[k * 128:(k + 1) * 128, :], writes=(f"xq{k}",))
    xn = ln_d(p, "ln", xt, TH, com, out_dtype=F32, in_bf=True,
              x_keys=[f"xq{k}" for k in range(KT)])
    P = [p.sbuf(f"P{k}", [128, 33], F32) for k in range(KT)]
    for l in range(33):
        lo = max(64 * l - 63, 0)
        hi = min(64 * l, TH - 1)
        for k in range(KT):
            p.t_op("vector", "tensor_reduce",
                   P[k][:, l:l + 1], xn[k][:, lo:hi + 1], AX.X, ALU.add,
                   reads=(tkey(xn[k]),), writes=(f"P{k}",))
    o_d = p.dram("po", [D, 33], F32, kind="ExternalOutput")
    for k in range(KT):
        p.dma("gpsimd", o_d[k * 128:(k + 1) * 128, :], P[k][:], reads=(f"P{k}",))
    p.finish()
    return p.finalize()


def host_l1ab_a(state):
    return [{"x": as_bf(xshard(state["x"], c))} for c in range(8)]


def post_l1ab_a(state, results):
    state["l1ab_p"] = [results[c]["po"] for c in range(8)]


def build_l1ab_b():
    p = Prog()
    com = Common(p)
    NH = 16
    TQ = 1024
    x_d = p.dram("x", [D, TH], F32)
    A_d = p.dram("pa", [D, 64], F32)
    B_d = p.dram("pb", [D, 64], F32)
    il_d = p.dram("ilen", [1, 64], F32)
    msk_d = p.dram("msk", [TH, 64], BF16)
    wq = load_w(p, "wq", D, D)
    wo = load_w(p, "wo", D, D)
    bq = load_rows(p, "bq", D)
    bk = load_rows(p, "bk", D)
    bo = load_rows(p, "bo", D)
    bvr = p.sbuf("bvr", [1, D], F32)
    p.dma("sync", bvr[:], p.dram("bv", [1, D], F32)[:, :], writes=("bvr",))
    wbuf = [p.sbuf(f"wb{k}", [128, D], BF16) for k in range(KT)]
    At = [p.sbuf(f"At{k}", [128, 64], F32) for k in range(KT)]
    Bt = [p.sbuf(f"Bt{k}", [128, 64], F32) for k in range(KT)]
    for k in range(KT):
        p.dma("sync", At[k][:], A_d[k * 128:(k + 1) * 128, :], writes=(f"At{k}",))
        p.dma("sync", Bt[k][:], B_d[k * 128:(k + 1) * 128, :], writes=(f"Bt{k}",))
    ilr = p.sbuf("ilr", [1, 64], F32)
    p.dma("sync", ilr[:], il_d[:, :], writes=("ilr",))
    ilb = bcast_row(p, "ilb", ilr[:], "ilr", 64, com)
    summ = [p.sbuf(f"sm{k}", [128, 64], BF16) for k in range(KT)]
    for k in range(KT):
        p.t_op("vector", "tensor_add", At[k][:], At[k][:], Bt[k][:],
               reads=(f"At{k}", f"Bt{k}"), writes=(f"At{k}",))
        p.t_op("vector", "tensor_mul", summ[k][:], At[k][:], ilb[:, 0:64],
               reads=(f"At{k}", tkey(ilb)), writes=(tkey(summ[k]),))
    wk_d = p.dram("wk", [D, D], BF16)
    wv_d = p.dram("wv", [D, D], BF16)
    for k in range(KT):
        p.dma("sync", wbuf[k][:], wk_d[k * 128:(k + 1) * 128, :], writes=(f"wb{k}",))
    kS = proj(p, "k", summ, wbuf, D, 64, com, bias=[t[:] for t in bk],
              bias_key="bk", evac_func="bias_add", chunk=64)
    vS = p.sbuf("vS", [64, D], BF16)
    bvb = bcast_row(p, "bvb", bvr[:], "bvr", D, com)
    for k in range(KT):
        p.dma("sync", wbuf[k][:], wv_d[k * 128:(k + 1) * 128, :], writes=(f"wb{k}",))
    for c0 in range(0, D, 512):
        ps = com.pool.get()
        for k in range(KT):
            p.mm(ps[0:64, :], summ[k][:, 0:64], wbuf[k][:, c0:c0 + 512],
                 start=(k == 0), stop=(k == KT - 1),
                 reads=(tkey(summ[k]), f"wb{k}"), writes=(tkey(ps),),
                 group=f"vs{c0}")
        p.t_op("vector", "tensor_tensor", vS[:, c0:c0 + 512], ps[0:64, :],
               bvb[0:64, c0:c0 + 512], ALU.add,
               reads=(tkey(ps), tkey(bvb)), writes=("vS",))
    ident = load_ident(p)
    xt = [p.sbuf(f"xh{k}", [128, TQ], F32) for k in range(KT)]
    mskt = [p.sbuf(f"msk{t}", [128, 64], BF16) for t in range(TQ // 128)]
    sc = p.sbuf("scb", [128, 64], F32)
    eP = p.sbuf("ePb", [128, 64], BF16)
    rmx = p.sbuf("rmx", [128, 1], F32)
    dnn = p.sbuf("dnn", [128, 1], F32)
    psT = p.psum("psT", [64, 128], BF16)
    PT = p.sbuf("PTb", [64, 128], BF16)
    d_out = p.dram("xo", [D, TH], F32, kind="ExternalOutput")
    xkeys = [f"xh{k}" for k in range(KT)]
    for h0 in (0, TQ):
        for k in range(KT):
            p.dma("sync", xt[k][:], x_d[k * 128:(k + 1) * 128, h0:h0 + TQ],
                  writes=(xkeys[k],))
        for t in range(TQ // 128):
            p.dma("scalar", mskt[t][:], msk_d[h0 + t * 128:h0 + (t + 1) * 128, :],
                  writes=(f"msk{t}",))
        xn = ln_d(p, "ln", xt, TQ, com, out_dtype=BF16, x_keys=xkeys)
        q = proj(p, "q", xn, wq, D, TQ, com, bias=[t[:] for t in bq],
                 bias_key="bq", evac_func="bias_add")
        attn = xn
        for h in range(NH):
            qtile = q[h // 2]
            base = 64 * (h % 2)
            for qt in range(TQ // 128):
                ps = com.pool.get()
                p.mm(ps[:, 0:64], qtile[base:base + 64, qt * 128:(qt + 1) * 128],
                     kS[h // 2][base:base + 64, 0:64], start=True, stop=True,
                     reads=(tkey(qtile), tkey(kS[h // 2])), writes=(tkey(ps),))
                p.t_op("vector", "scalar_tensor_tensor", sc[:], ps[:, 0:64],
                       0.125, mskt[qt][:], ALU.mult, ALU.add,
                       reads=(tkey(ps), f"msk{qt}"), writes=("scb",))
                p.t_op("vector", "reduce_max", rmx[:], sc[:], AX.X,
                       reads=("scb",), writes=("rmx",))
                p.t_op("vector", "tensor_scalar", rmx[:], rmx[:], -1.0, None,
                       ALU.mult, reads=("rmx",), writes=("rmx",))
                p.t_op("scalar", "activation", eP[:], sc[:], AF.Exp,
                       bias=rmx[:], accum_out=dnn[:],
                       reads=("scb", "rmx"), writes=("ePb", "dnn"))
                p.t_op("vector", "reciprocal", dnn[:], dnn[:], reads=("dnn",),
                       writes=("dnn",))
                p.t_op("vector", "tensor_scalar", eP[:], eP[:], dnn[:], None,
                       ALU.mult, reads=("ePb", "dnn"), writes=("ePb",))
                p.transp(psT[:], eP[:], ident[:], reads=("ePb", "ident"),
                         writes=("psT",))
                p.t_op("vector", "tensor_copy", PT[:], psT[:], reads=("psT",),
                       writes=("PTb",))
                ps2 = com.pool.get()
                p.mm(ps2[0:64, 0:128], vS[:, h * 64:(h + 1) * 64], PT[:],
                     start=True, stop=True, reads=("vS", "PTb"),
                     writes=(tkey(ps2),))
                p.t_op("vector", "tensor_copy",
                       attn[h // 2][base:base + 64, qt * 128:(qt + 1) * 128],
                       ps2[0:64, 0:128], reads=(tkey(ps2),),
                       writes=(tkey(attn[h // 2]),))
        proj(p, "out", attn, wo, D, TQ, com, out_aps=[t[:] for t in xt],
             out_keys=xkeys, bias=[t[:] for t in bo], bias_key="bo",
             accum_into=xt, accum_keys=xkeys)
        for k in range(KT):
            p.dma("gpsimd", d_out[k * 128:(k + 1) * 128, h0:h0 + TQ],
                  xt[k][:], reads=(xkeys[k],))
    p.finish()
    return p.finalize()


def host_l1ab_b(state):
    inp = state["inp"]
    maps = []
    ilen = np.ones((1, 64), np.float32) / 64.0
    ilen[0, 0] = 1.0
    for c in range(8):
        b, hf = c // 2, c % 2
        p0 = state["l1ab_p"][2 * b]
        p1 = state["l1ab_p"][2 * b + 1]
        A = np.zeros((D, 64), np.float32)
        Bb = np.zeros((D, 64), np.float32)
        A[:, 0:33] = p0
        A[:, 33:64] = p1[:, 1:32]
        Bb[:, 32] = p1[:, 0]
        tglob = hf * TH + np.arange(TH)
        jmax = tglob // 64
        msk = np.where(np.arange(64)[None, :] <= jmax[:, None], 0.0, -30000.0)
        maps.append({
            "x": xshard(state["x"], c),
            "pa": A, "pb": Bb, "ilen": ilen,
            "msk": msk.astype(BF),
            "wq": as_bf(inp["L1_attn_b_Wq"]), "bq": as_f32(inp["L1_attn_b_bq"].reshape(-1, 1)),
            "wk": as_bf(inp["L1_attn_b_Wk"]), "bk": as_f32(inp["L1_attn_b_bk"].reshape(-1, 1)),
            "wv": as_bf(inp["L1_attn_b_Wv"]), "bv": as_f32(inp["L1_attn_b_bv"].reshape(1, -1)),
            "wo": as_bf(inp["L1_attn_b_Wo"]), "bo": as_f32(inp["L1_attn_b_bo"].reshape(-1, 1)),
            "ident": get_ident(),
        })
    return maps


def post_l1ab_b(state, results):
    state["x"] = unshard_x(results)


def build_l1aa_a():
    p = Prog()
    com = Common(p)
    TQ = 2048
    x_d = p.dram("x", [D, TH], BF16)
    xt = [p.sbuf(f"xq{k}", [128, TQ], BF16) for k in range(KT)]
    for k in range(KT):
        p.dma("sync", xt[k][:], x_d[k * 128:(k + 1) * 128, :], writes=(f"xq{k}",))
    wxdt = load_w(p, "wxdt", D, 64)
    bxdt = load_rows(p, "bxdt", 64)
    wdt = load_w(p, "wdt", 64, D)
    bdt = load_rows(p, "bdt", D)
    wB = load_w(p, "wB", D, 16)
    bB = load_rows(p, "bB", 16)
    wC = load_w(p, "wC", D, 16)
    bC = load_rows(p, "bC", 16)
    Dv = load_rows(p, "Dv", D)
    un = ln_d(p, "ln", xt, TQ, com, out_dtype=BF16, in_bf=True,
              x_keys=[f"xq{k}" for k in range(KT)])
    xdt = proj(p, "xdt", un, wxdt, 64, TQ, com, bias=[bxdt[0][0:64, :]],
               bias_key="bxdt", evac_func="bias_add")
    dtp = proj(p, "dtp", xdt, wdt, D, TQ, com, bias=[t[:] for t in bdt],
               bias_key="bdt", evac_func="bias_add", out_dtype=BF16,
               x_keys=[tkey(xdt[0])])
    dt_d = p.dram("dto", [D, TQ], BF16, kind="ExternalOutput")
    du_d = p.dram("dtu", [D, TQ], BF16, kind="ExternalOutput")
    Du_d = p.dram("Du", [D, TQ], F32, kind="ExternalOutput")
    t1 = p.sbuf("t1b", [128, TQ], F32)
    t2 = p.sbuf("t2b", [128, TQ], F32)
    dtb = p.sbuf("dtb", [128, TQ], BF16)
    for k in range(KT):
        dk = tkey(dtp[k])
        p.t_op("scalar", "activation", t1[:], dtp[k][:], AF.Abs,
               reads=(dk,), writes=("t1b",))
        p.t_op("scalar", "activation", t1[:], t1[:], AF.Exp, scale=-1.0,
               reads=("t1b",), writes=("t1b",))
        one, onek = com.const(1.0)
        p.t_op("scalar", "activation", t1[:], t1[:], AF.Ln, bias=one[:, 0:1],
               reads=("t1b", onek), writes=("t1b",))
        p.t_op("scalar", "activation", t2[:], dtp[k][:], AF.Relu,
               reads=(dk,), writes=("t2b",))
        p.t_op("vector", "tensor_tensor", dtb[:], t1[:], t2[:], ALU.add,
               reads=("t1b", "t2b"), writes=("dtbk",))
        p.dma("gpsimd", dt_d[k * 128:(k + 1) * 128, :], dtb[:], reads=("dtbk",))
        dub = p.sbuf("dub", [128, TQ], BF16)
        p.t_op("vector", "tensor_tensor", dub[:], dtb[:], un[k][:], ALU.mult,
               reads=("dtbk", tkey(un[k])), writes=("dub",))
        p.dma("sync", du_d[k * 128:(k + 1) * 128, :], dub[:], reads=("dub",))
        p.t_op("vector", "tensor_scalar", t2[:], un[k][:], Dv[k][:], None,
               ALU.mult, reads=(tkey(un[k]), "Dv"), writes=("t2b",))
        p.dma("scalar", Du_d[k * 128:(k + 1) * 128, :], t2[:], reads=("t2b",))
    Bo = proj(p, "Bo", un, wB, 16, TQ, com, bias=[bB[0][0:16, :]],
              bias_key="bB", evac_func="bias_add", out_dtype=F32, chunk=512)
    Co = proj(p, "Co", un, wC, 16, TQ, com, bias=[bC[0][0:16, :]],
              bias_key="bC", evac_func="bias_add", out_dtype=F32, chunk=512)
    B_d = p.dram("Bo_d", [16, TQ], F32, kind="ExternalOutput")
    C_d = p.dram("Co_d", [16, TQ], F32, kind="ExternalOutput")
    p.dma("gpsimd", B_d[:, :], Bo[0][0:16, :], reads=(tkey(Bo[0]),))
    p.dma("gpsimd", C_d[:, :], Co[0][0:16, :], reads=(tkey(Co[0]),))
    p.finish()
    return p.finalize()


def host_l1aa_a(state):
    inp = state["inp"]
    maps = []
    for c in range(8):
        maps.append({
            "x": as_bf(xshard(state["x"], c)),
            "wxdt": as_bf(inp["L1_attn_a_W_xdt"].T),
            "bxdt": as_f32(inp["L1_attn_a_b_xdt"].reshape(-1, 1)),
            "wdt": as_bf(inp["L1_attn_a_W_dt"].T),
            "bdt": as_f32(inp["L1_attn_a_b_dt"].reshape(-1, 1)),
            "wB": as_bf(inp["L1_attn_a_W_B"].T),
            "bB": as_f32(inp["L1_attn_a_b_B"].reshape(-1, 1)),
            "wC": as_bf(inp["L1_attn_a_W_C"].T),
            "bC": as_f32(inp["L1_attn_a_b_C"].reshape(-1, 1)),
            "Dv": as_f32(inp["L1_attn_a_D"].reshape(-1, 1)),
        })
    return maps


def _pad128(v):
    out = np.zeros((128, 1), np.float32)
    out[0:v.shape[0], 0] = v
    return out


def post_l1aa_a(state, results):
    for nm, key, dt_ in (("dto", "l1aa_dt", BF), ("dtu", "l1aa_dtu", BF),
                         ("Du", "l1aa_Du", np.float32), ("Bo_d", "l1aa_B", np.float32),
                         ("Co_d", "l1aa_C", np.float32)):
        rows = results[0][nm].shape[0]
        full = np.zeros((NB, rows, TFULL), dt_)
        for c in range(8):
            b, h = c // 2, c % 2
            full[b, :, h * TH:(h + 1) * TH] = results[c][nm]
        state[key] = full


def build_l1aa_b():
    p = Prog()
    com = Common(p)
    NL = 512
    TQ = 2048
    A_d = p.dram("Al", [NL, 16], F32)
    dt_d = p.dram("dt", [NL, TFULL], BF16)
    du_d = p.dram("dtu", [NL, TFULL], BF16)
    Du_d = p.dram("Du", [NL, TFULL], F32)
    B_d = p.dram("B", [16, TFULL], F32)
    C_d = p.dram("C", [16, TFULL], F32)
    y_d = p.dram("yo", [NL, TFULL], F32, kind="ExternalOutput")
    At = p.sbuf("At", [128, 4 * 16], F32)
    for lt in range(4):
        p.dma("sync", At[:, lt * 16:(lt + 1) * 16],
              A_d[lt * 128:(lt + 1) * 128, :], writes=("At",))
    p.t_op("scalar", "activation", At[:], At[:], AF.Exp, reads=("At",),
           writes=("At",))
    p.t_op("vector", "tensor_scalar", At[:], At[:], -1.0, None, ALU.mult,
           reads=("At",), writes=("At",))
    Br = p.sbuf("Br", [1, TQ], F32)
    Cr = p.sbuf("Cr", [1, TQ], F32)
    dt = [p.sbuf(f"dt{lt}", [128, TQ], BF16) for lt in range(4)]
    du = [p.sbuf(f"du{lt}", [128, TQ], BF16) for lt in range(4)]
    y = [p.sbuf(f"y{lt}", [128, TQ], F32) for lt in range(4)]
    ab = p.sbuf("abuf", [128, TQ], F32)
    hb = p.sbuf("hbuf", [128, TQ], F32)
    xb = p.sbuf("xbuf", [128, TQ], BF16)
    Bb = p.sbuf("Bb", [128, TQ], BF16)
    Cb = p.sbuf("Cb", [128, TQ], BF16)
    carry = [p.sbuf(f"cr{lt}", [128, 16], F32) for lt in range(4)]
    for lt in range(4):
        p.t_op("vector", "memset", carry[lt][:], 0.0, writes=(f"cr{lt}",))
    for h0 in (0, TQ):
        for lt in range(4):
            r = slice(lt * 128, (lt + 1) * 128)
            p.dma("sync", dt[lt][:], dt_d[r, h0:h0 + TQ], writes=(f"dt{lt}",))
            p.dma("scalar", du[lt][:], du_d[r, h0:h0 + TQ], writes=(f"du{lt}",))
            p.dma("gpsimd", y[lt][:], Du_d[r, h0:h0 + TQ], writes=(f"y{lt}",))
        for n in range(16):
            p.dma("sync", Br[:], B_d[n:n + 1, h0:h0 + TQ], writes=("Br",))
            p.dma("sync", Cr[:], C_d[n:n + 1, h0:h0 + TQ], writes=("Cr",))
            for c0 in range(0, TQ, 512):
                ps = com.pool.get()
                p.mm(ps[:, :], com.ones_f[:], Br[:, c0:c0 + 512], start=True,
                     stop=True, reads=("c_ones_f", "Br"), writes=(tkey(ps),))
                p.t_op("vector", "tensor_copy", Bb[:, c0:c0 + 512], ps[:, :],
                       reads=(tkey(ps),), writes=("Bb",))
                ps2 = com.pool.get()
                p.mm(ps2[:, :], com.ones_f[:], Cr[:, c0:c0 + 512], start=True,
                     stop=True, reads=("c_ones_f", "Cr"), writes=(tkey(ps2),))
                p.t_op("vector", "tensor_copy", Cb[:, c0:c0 + 512], ps2[:, :],
                       reads=(tkey(ps2),), writes=("Cb",))
            for lt in range(4):
                p.t_op("scalar", "activation", ab[:], dt[lt][:], AF.Exp,
                       scale=At[:, lt * 16 + n:lt * 16 + n + 1],
                       reads=(f"dt{lt}", "At"), writes=("abuf",))
                p.t_op("gpsimd", "tensor_tensor", xb[:], du[lt][:], Bb[:],
                       ALU.mult, reads=(f"du{lt}", "Bb"), writes=("xbuf",))
                eng = "vector" if lt % 2 == 0 else "gpsimd"
                p.t_op(eng, "tensor_tensor_scan", hb[:], ab[:], xb[:],
                       carry[lt][:, n:n + 1], ALU.mult, ALU.add,
                       reads=("abuf", "xbuf", f"cr{lt}"), writes=("hbuf",))
                p.t_op("vector", "tensor_copy", carry[lt][:, n:n + 1],
                       hb[:, TQ - 1:TQ], reads=("hbuf",), writes=(f"cr{lt}",))
                p.t_op("gpsimd", "tensor_tensor", hb[:], hb[:], Cb[:], ALU.mult,
                       reads=("hbuf", "Cb"), writes=("hbuf",))
                p.t_op("vector", "tensor_tensor", y[lt][:], y[lt][:], hb[:],
                       ALU.add, reads=(f"y{lt}", "hbuf"), writes=(f"y{lt}",))
        for lt in range(4):
            p.dma("gpsimd", y_d[lt * 128:(lt + 1) * 128, h0:h0 + TQ],
                  y[lt][:], reads=(f"y{lt}",))
    p.finish()
    return p.finalize()


def host_l1aa_b(state):
    inp = state["inp"]
    maps = []
    for c in range(8):
        b, h = c // 2, c % 2
        ch = slice(h * 512, (h + 1) * 512)
        maps.append({
            "Al": as_f32(inp["L1_attn_a_A_log"][ch]),
            "dt": state["l1aa_dt"][b][ch],
            "dtu": np.ascontiguousarray(state["l1aa_dtu"][b][ch]),
            "Du": as_f32(state["l1aa_Du"][b][ch]),
            "B": as_f32(state["l1aa_B"][b]),
            "C": as_f32(state["l1aa_C"][b]),
        })
    return maps


def post_l1aa_b(state, results):
    full = np.zeros((NB, D, TFULL), np.float32)
    for c in range(8):
        b, h = c // 2, c % 2
        full[b, h * 512:(h + 1) * 512, :] = results[c]["yo"]
    state["l1aa_y"] = full


def cheb_slot(bufs3, m):
    return bufs3[(m - 1) % 3]


def build_l0ab_a():
    p = Prog()
    com = Common(p)
    TQ = 512
    SC = 0.125
    x_d = p.dram("x", [D, TH], BF16)
    wq_d = p.dram("wq", [D, D], BF16)
    wk_d = p.dram("wk", [D, D], BF16)
    wv_d = p.dram("wv", [D, D], BF16)
    wbuf = [p.sbuf(f"wb{k}", [128, D], BF16) for k in range(KT)]
    wq = wk = wv = wbuf
    qo_d = p.dram("qo", [D, TH], BF16, kind="ExternalOutput")
    kvp_d = p.dram("kvp", [11 * 64, D], F32, kind="ExternalOutput")
    xt = [p.sbuf(f"xq{k}", [128, TQ], BF16) for k in range(KT)]
    ktm = [p.sbuf(f"ktm{t}", [128, D], BF16) for t in range(TQ // 128)]
    vtm = [p.sbuf(f"vtm{t}", [128, D], BF16) for t in range(TQ // 128)]
    T0 = p.sbuf("T0", [128, D], BF16)
    p.t_op("vector", "memset", T0[:], 1.0, writes=("T0",))
    Tb = [[p.sbuf(f"Tb{i}_{t}", [128, D], BF16) for t in range(TQ // 128)]
          for i in range(3)]
    kv_acc = [p.sbuf(f"kva{j}", [128, D], F32) for j in range(6)]
    kvps = p.psum("kvps", [64, D], F32)
    qb = p.sbuf("qb", [128, TQ], BF16)
    qf = p.sbuf("qf", [128, TQ], F32)
    for h0 in (0, 512, 1024, 1536):
        for k in range(KT):
            p.dma("sync", xt[k][:], x_d[k * 128:(k + 1) * 128, h0:h0 + TQ],
                  writes=(f"xq{k}",))
        xn = ln_d(p, "ln", xt, TQ, com, out_dtype=BF16, in_bf=True,
                  x_keys=[f"xq{k}" for k in range(KT)])
        for k in range(KT):
            p.dma("sync", wbuf[k][:], wq_d[k * 128:(k + 1) * 128, :],
                  writes=(f"wb{k}",))
        for m in range(KT):
            for c0 in range(0, TQ, 512):
                ps = com.pool.get()
                for kk in range(KT):
                    p.mm(ps[:, :], wq[kk][:, m * 128:(m + 1) * 128],
                         xn[kk][:, c0:c0 + 512], start=(kk == 0),
                         stop=(kk == KT - 1), reads=(f"wb{kk}", tkey(xn[kk])),
                         writes=(tkey(ps),), group=f"q{h0}_{m}_{c0}")
                p.t_op("vector", "tensor_scalar", qf[:, c0:c0 + 512], ps[:, :],
                       SC, 1.0, ALU.mult, ALU.min, reads=(tkey(ps),),
                       writes=("qf",))
            p.t_op("vector", "tensor_scalar", qb[:], qf[:], -1.0, None,
                   ALU.max, reads=("qf",), writes=("qb",))
            p.dma("gpsimd", qo_d[m * 128:(m + 1) * 128, h0:h0 + TQ], qb[:],
                  reads=("qb",))
        for (wd_, wn, dst, clip) in ((wk_d, "wk", ktm, True), (wv_d, "wv", vtm, False)):
            for k in range(KT):
                p.dma("sync", wbuf[k][:], wd_[k * 128:(k + 1) * 128, :],
                      writes=(f"wb{k}",))
            for tt in range(TQ // 128):
                for c0 in range(0, D, 512):
                    ps = com.pool.get()
                    for kk in range(KT):
                        p.mm(ps[:, :], xn[kk][:, tt * 128:(tt + 1) * 128],
                             wbuf[kk][:, c0:c0 + 512], start=(kk == 0),
                             stop=(kk == KT - 1),
                             reads=(tkey(xn[kk]), f"wb{kk}"),
                             writes=(tkey(ps),), group=f"{wn}{h0}_{tt}_{c0}")
                    if clip:
                        p.t_op("vector", "tensor_scalar",
                               dst[tt][:, c0:c0 + 512], ps[:, :], SC, 1.0,
                               ALU.mult, ALU.min, reads=(tkey(ps),),
                               writes=(tkey(dst[tt]),))
                        p.t_op("vector", "tensor_scalar",
                               dst[tt][:, c0:c0 + 512], dst[tt][:, c0:c0 + 512],
                               -1.0, None, ALU.max, reads=(tkey(dst[tt]),),
                               writes=(tkey(dst[tt]),))
                    else:
                        p.t_op("vector", "tensor_copy",
                               dst[tt][:, c0:c0 + 512], ps[:, :],
                               reads=(tkey(ps),), writes=(tkey(dst[tt]),))
        for m in range(11):
            if m == 1:
                for t in range(TQ // 128):
                    p.t_op("vector", "tensor_copy", cheb_slot(Tb, 1)[t][:],
                           ktm[t][:], reads=(tkey(ktm[t]),),
                           writes=(tkey(cheb_slot(Tb, 1)[t]),))
            elif m >= 2:
                for t in range(TQ // 128):
                    e = ["vector", "gpsimd"][t % 2]
                    dstt = cheb_slot(Tb, m)[t]
                    p1t = cheb_slot(Tb, m - 1)[t]
                    p2t = T0 if m == 2 else cheb_slot(Tb, m - 2)[t]
                    tbk = p.sbuf(f"chb{t % 2}", [128, D], F32)
                    p.t_op(e, "tensor_tensor", tbk[:], ktm[t][:], p1t[:],
                           ALU.mult, reads=(tkey(ktm[t]), tkey(p1t)),
                           writes=(tkey(tbk),))
                    p.t_op(e, "scalar_tensor_tensor", dstt[:], tbk[:], 2.0,
                           p2t[:], ALU.mult, ALU.subtract,
                           reads=(tkey(tbk), tkey(p2t)), writes=(tkey(dstt),))
            for h in range(16):
                for tt in range(TQ // 128):
                    src_t = T0 if m == 0 else cheb_slot(Tb, m)[tt]
                    p.mm(kvps[:, h * 64:(h + 1) * 64],
                         src_t[:, h * 64:(h + 1) * 64],
                         vtm[tt][:, h * 64:(h + 1) * 64],
                         start=(tt == 0), stop=(tt == TQ // 128 - 1),
                         reads=(tkey(src_t), tkey(vtm[tt])), writes=("kvps",),
                         group=f"kv{h0}_{m}_{h}")
            jt, jo = (m * 64) // 128, (m * 64) % 128
            if h0 == 0:
                p.t_op("vector", "tensor_copy", kv_acc[jt][jo:jo + 64, :],
                       kvps[:, :], reads=("kvps",), writes=(f"kva{jt}",))
            else:
                p.t_op("vector", "tensor_tensor", kv_acc[jt][jo:jo + 64, :],
                       kv_acc[jt][jo:jo + 64, :], kvps[:, :], ALU.add,
                       reads=(f"kva{jt}", "kvps"), writes=(f"kva{jt}",))
    for j in range(6):
        r = 128 if j < 5 else 64
        p.dma("gpsimd", kvp_d[j * 128:j * 128 + r, :], kv_acc[j][0:r, :],
              reads=(f"kva{j}",))
    p.finish()
    return p.finalize()


def host_l0ab_a(state):
    inp = state["inp"]
    Wi = inp["L0_attn_b_w_in"]
    maps = []
    for c in range(8):
        maps.append({
            "x": as_bf(xshard(state["x"], c)),
            "wq": as_bf(Wi[0:1024].T), "wk": as_bf(Wi[1024:2048].T),
            "wv": as_bf(Wi[2048:3072].T),
        })
    return maps


def post_l0ab_a(state, results):
    state["l0ab_q"] = [results[c]["qo"] for c in range(8)]
    state["l0ab_kv"] = [results[c]["kvp"] for c in range(8)]


def build_l0ab_b():
    p = Prog()
    com = Common(p)
    TQ = 1024
    x_d = p.dram("x", [D, TH], F32)
    q_d = p.dram("q", [D, TH], BF16)
    kv0_d = p.dram("kv0", [11 * 64, D], F32)
    kv1_d = p.dram("kv1", [11 * 64, D], F32)
    bp_d = p.dram("bpat", [11 * 64, D], BF16)
    wo = load_w(p, "wo", D, D)
    kvt = [p.sbuf(f"kvt{m}", [128, D], BF16) for m in range(11)]
    a_b = p.sbuf("kva", [128, D], F32)
    b_b = p.sbuf("kvb", [128, D], F32)
    bp_b = p.sbuf("bpb", [128, D], BF16)
    for m in range(11):
        for half in (0, 64):
            p.dma("sync", a_b[half:half + 64, :], kv0_d[m * 64:(m + 1) * 64, :],
                  writes=("kva",))
            p.dma("sync", b_b[half:half + 64, :], kv1_d[m * 64:(m + 1) * 64, :],
                  writes=("kvb",))
            p.dma("sync", bp_b[half:half + 64, :], bp_d[m * 64:(m + 1) * 64, :],
                  writes=("bpb",))
        p.t_op("vector", "tensor_add", a_b[:, :], a_b[:, :], b_b[:, :],
               reads=("kva", "kvb"), writes=("kva",))
        p.t_op("vector", "tensor_mul", kvt[m][:, :], a_b[:, :],
               bp_b[:, :], reads=("kva", "bpb"), writes=(tkey(kvt[m]),))
    qt = [p.sbuf(f"qt{k}", [128, TQ], BF16) for k in range(KT)]
    xt = [p.sbuf(f"xh{k}", [128, TQ], F32) for k in range(KT)]
    xkeys = [f"xh{k}" for k in range(KT)]
    T0 = p.sbuf("T0", [128, TQ], BF16)
    p.t_op("vector", "memset", T0[:], 1.0, writes=("T0",))
    Tb = [[p.sbuf(f"Tq{i}_{k}", [128, TQ], BF16) for k in range(KT)]
          for i in range(3)]
    tb2 = [p.sbuf(f"tb2_{i}", [128, TQ], F32) for i in range(2)]
    attf = [p.sbuf(f"af{k}", [128, TQ], F32) for k in range(KT)]
    attn = [p.sbuf(f"at{k}", [128, TQ], BF16) for k in range(KT)]
    d_out = p.dram("xo", [D, TH], F32, kind="ExternalOutput")
    for h0 in (0, TQ):
        for k in range(KT):
            p.dma("sync", qt[k][:], q_d[k * 128:(k + 1) * 128, h0:h0 + TQ],
                  writes=(f"qt{k}",))
            p.dma("scalar", xt[k][:], x_d[k * 128:(k + 1) * 128, h0:h0 + TQ],
                  writes=(xkeys[k],))
        for m in range(11):
            if m == 1:
                for k in range(KT):
                    p.t_op("vector", "tensor_copy", cheb_slot(Tb, 1)[k][:],
                           qt[k][:], reads=(f"qt{k}",),
                           writes=(tkey(cheb_slot(Tb, 1)[k]),))
            elif m >= 2:
                for k in range(KT):
                    e = ["vector", "gpsimd"][k % 2]
                    dstt = cheb_slot(Tb, m)[k]
                    p1t = cheb_slot(Tb, m - 1)[k]
                    tbk = tb2[k % 2]
                    p.t_op(e, "tensor_tensor", tbk[:], qt[k][:], p1t[:],
                           ALU.mult, reads=(f"qt{k}", tkey(p1t)),
                           writes=(tkey(tbk),))
                    if m == 2:
                        p.t_op(e, "tensor_scalar", dstt[:], tbk[:], 2.0, -1.0,
                               ALU.mult, ALU.add, reads=(tkey(tbk),),
                               writes=(tkey(dstt),))
                    else:
                        p2t = cheb_slot(Tb, m - 2)[k]
                        p.t_op(e, "scalar_tensor_tensor", dstt[:], tbk[:], 2.0,
                               p2t[:], ALU.mult, ALU.subtract,
                               reads=(tkey(tbk), tkey(p2t)), writes=(tkey(dstt),))
            for h in range(16):
                if m == 0:
                    src_t, base, sk = T0, 64 * (h % 2), "T0"
                else:
                    src_t = cheb_slot(Tb, m)[h // 2]
                    base = 64 * (h % 2)
                    sk = tkey(src_t)
                for qc in range(TQ // 512):
                    ps = com.pool.get()
                    p.mm(ps[base:base + 64, :],
                         kvt[m][base:base + 64, h * 64:(h + 1) * 64],
                         src_t[base:base + 64, qc * 512:(qc + 1) * 512],
                         start=True, stop=True,
                         reads=(tkey(kvt[m]), sk), writes=(tkey(ps),))
                    hb = 64 * (h % 2)
                    if m == 0:
                        p.t_op("vector", "tensor_copy",
                               attf[h // 2][hb:hb + 64, qc * 512:(qc + 1) * 512],
                               ps[base:base + 64, :], reads=(tkey(ps),),
                               writes=(tkey(attf[h // 2]),))
                    else:
                        p.t_op("vector", "tensor_tensor",
                               attf[h // 2][hb:hb + 64, qc * 512:(qc + 1) * 512],
                               attf[h // 2][hb:hb + 64, qc * 512:(qc + 1) * 512],
                               ps[base:base + 64, :], ALU.add,
                               reads=(tkey(attf[h // 2]), tkey(ps)),
                               writes=(tkey(attf[h // 2]),))
        for k in range(KT):
            p.t_op("vector", "tensor_copy", attn[k][:], attf[k][:],
                   reads=(tkey(attf[k]),), writes=(tkey(attn[k]),))
        proj(p, "out", attn, wo, D, TQ, com, out_aps=[t[:] for t in xt],
             out_keys=xkeys, accum_into=xt, accum_keys=xkeys)
        for k in range(KT):
            p.dma("gpsimd", d_out[k * 128:(k + 1) * 128, h0:h0 + TQ],
                  xt[k][:], reads=(xkeys[k],))
    p.finish()
    return p.finalize()


def host_l0ab_b(state):
    inp = state["inp"]
    beta = inp["L0_attn_b_beta"]
    bpat = np.zeros((704, D), np.float32)
    for m in range(11):
        for h in range(16):
            bpat[m * 64:(m + 1) * 64, h * 64:(h + 1) * 64] = beta[h, m]
    maps = []
    for c in range(8):
        b, hf = c // 2, c % 2
        maps.append({
            "x": xshard(state["x"], c),
            "q": state["l0ab_q"][c],
            "kv0": state["l0ab_kv"][2 * b], "kv1": state["l0ab_kv"][2 * b + 1],
            "bpat": bpat.astype(BF),
            "wo": as_bf(inp["L0_attn_b_w_out"].T),
        })
    return maps


def post_l0ab_b(state, results):
    state["x"] = unshard_x(results)


def build_l2ab_a():
    p = Prog()
    com = Common(p)
    TQ = 1024
    x_d = p.dram("x", [D, TH], BF16)
    wq = load_w(p, "wq", D, D)
    wk = load_w(p, "wk", D, D)
    wg = load_w(p, "wg", D, D)
    wv_d = p.dram("wv", [D, D], BF16)
    n1g = load_rows(p, "n1g", D)
    n1b = load_rows(p, "n1b", D)
    bgr = load_rows(p, "bg", D)
    q_o = p.dram("qo", [D, TH], BF16, kind="ExternalOutput")
    k_o = p.dram("ko", [D, TH], BF16, kind="ExternalOutput")
    v_o = p.dram("vo", [TH, D], BF16, kind="ExternalOutput")
    g_o = p.dram("go", [D, TH], BF16, kind="ExternalOutput")
    xt = [p.sbuf(f"xq{k}", [128, TQ], BF16) for k in range(KT)]
    wvb = [p.sbuf(f"wvb{k}", [128, D], BF16) for k in range(KT)]
    for k in range(KT):
        p.dma("scalar", wvb[k][:], wv_d[k * 128:(k + 1) * 128, :],
              writes=(f"wvb{k}",))
    vbuf = p.sbuf("vbuf", [128, 512], BF16)
    for h0 in (0, TQ):
        for k in range(KT):
            p.dma("sync", xt[k][:], x_d[k * 128:(k + 1) * 128, h0:h0 + TQ],
                  writes=(f"xq{k}",))
        h00 = ln_d(p, "h0", xt, TQ, com, out_dtype=BF16, in_bf=True,
                   x_keys=[f"xq{k}" for k in range(KT)])
        xn = ln_d(p, "xn", h00, TQ, com, g=[t[:] for t in n1g],
                  b=[t[:] for t in n1b], gb_key="n1g", out_dtype=BF16,
                  in_bf=True)
        for (w, nm, od, ef, bias, bk_) in (
                (wq, "q", q_o, None, None, None),
                (wk, "k", k_o, None, None, None),
                (wg, "g", g_o, AF.Silu, [t[:] for t in bgr], "bg")):
            ot = proj(p, f"p{nm}", xn, w, D, TQ, com, evac_func=ef,
                      bias=bias, bias_key=bk_)
            for m in range(KT):
                p.dma("gpsimd", od[m * 128:(m + 1) * 128, h0:h0 + TQ],
                      ot[m][:], reads=(tkey(ot[m]),))
        for tt in range(TQ // 128):
            for c0 in range(0, D, 512):
                ps = com.pool.get()
                for k in range(KT):
                    p.mm(ps[:, :], xn[k][:, tt * 128:(tt + 1) * 128],
                         wvb[k][:, c0:c0 + 512], start=(k == 0),
                         stop=(k == KT - 1), reads=(tkey(xn[k]), f"wvb{k}"),
                         writes=(tkey(ps),), group=f"v{h0}_{tt}_{c0}")
                p.t_op("vector", "tensor_copy", vbuf[:], ps[:, :],
                       reads=(tkey(ps),), writes=("vbuf",))
                p.dma("gpsimd", v_o[h0 + tt * 128:h0 + (tt + 1) * 128,
                                    c0:c0 + 512], vbuf[:], reads=("vbuf",))
    p.finish()
    return p.finalize()


def host_l2ab_a(state):
    inp = state["inp"]
    maps = []
    for c in range(8):
        maps.append({
            "x": as_bf(xshard(state["x"], c)),
            "wq": as_bf(inp["L2_attn_b_wq"]), "wk": as_bf(inp["L2_attn_b_wk"]),
            "wv": as_bf(inp["L2_attn_b_wv"]), "wg": as_bf(inp["L2_attn_b_wg"]),
            "n1g": as_f32(inp["L2_attn_b_norm1_w"].reshape(-1, 1)),
            "n1b": as_f32(inp["L2_attn_b_norm1_b"].reshape(-1, 1)),
            "bg": as_f32(inp["L2_attn_b_bg"].reshape(-1, 1)),
        })
    return maps


def post_l2ab_a(state, results):
    for nm, key in (("qo", "l2ab_q"), ("ko", "l2ab_k"), ("go", "l2ab_g")):
        full = np.zeros((NB, D, TFULL), BF)
        for c in range(8):
            b, h = c // 2, c % 2
            full[b, :, h * TH:(h + 1) * TH] = results[c][nm]
        state[key] = full
    fullv = np.zeros((NB, TFULL, D), BF)
    for c in range(8):
        b, h = c // 2, c % 2
        fullv[b, h * TH:(h + 1) * TH, :] = results[c]["vo"]
    state["l2ab_v"] = fullv


def build_l2ab_b():
    p = Prog()
    com = Common(p, npool=2)
    q_d = p.dram("q", [512, TFULL], BF16)
    kd_d = p.dram("kd", [512, TFULL], BF16)
    k_d = p.dram("kt", [TFULL, 512], BF16)
    v_d = p.dram("vt", [TFULL, 512], BF16)
    dm_d = p.dram("dmask", [128, 4 * 256], BF16)
    xi_d = p.dram("xi", [512, 128], BF16)
    xip_d = p.dram("xip", [128, 512], BF16)
    gc_d = p.dram("gc", [512, 1], F32)
    r_o = p.dram("ro", [512, TFULL], BF16, kind="ExternalOutput")
    qt = [p.sbuf(f"qt{j}", [128, TFULL], BF16) for j in range(4)]
    kdm = [p.sbuf(f"kdm{j}", [128, TFULL], BF16) for j in range(4)]
    for j in range(4):
        p.dma("sync", qt[j][:], q_d[j * 128:(j + 1) * 128, :], writes=(f"qt{j}",))
        p.dma("sync", kdm[j][:], kd_d[j * 128:(j + 1) * 128, :], writes=(f"kdm{j}",))
    ktm = [p.sbuf(f"ktm{t}", [128, 512], BF16) for t in range(32)]
    vtm = [p.sbuf(f"vtm{t}", [128, 512], BF16) for t in range(32)]
    for t in range(32):
        p.dma("scalar", ktm[t][:], k_d[t * 128:(t + 1) * 128, :], writes=(f"ktm{t}",))
        p.dma("scalar", vtm[t][:], v_d[t * 128:(t + 1) * 128, :], writes=(f"vtm{t}",))
    dmt = [p.sbuf(f"dmt{pr}", [128, 256], BF16) for pr in range(4)]
    xit = [p.sbuf(f"xit{j}", [128, 128], BF16) for j in range(4)]
    xipt = p.sbuf("xipt", [128, 512], BF16)
    gct = [p.sbuf(f"gct{j}", [128, 1], F32) for j in range(4)]
    p.dma("sync", xipt[:], xip_d[:, :], writes=("xipt",))
    for pr in range(4):
        p.dma("sync", dmt[pr][:], dm_d[:, pr * 256:(pr + 1) * 256],
              writes=(f"dmt{pr}",))
    for j in range(4):
        p.dma("sync", xit[j][:], xi_d[j * 128:(j + 1) * 128, :], writes=(f"xit{j}",))
        p.dma("sync", gct[j][:], gc_d[j * 128:(j + 1) * 128, :], writes=(f"gct{j}",))
    qx = p.sbuf("qx", [128, 128], BF16)
    S = [p.sbuf(f"S{pr}", [128, 64], F32) for pr in range(4)]
    Sb = [p.sbuf(f"Sb{pr}", [128, 64], BF16) for pr in range(4)]
    for pr in range(4):
        p.t_op("vector", "memset", S[pr][:], 0.0, writes=(f"S{pr}",))
        p.t_op("vector", "memset", Sb[pr][:], 0.0, writes=(f"Sb{pr}",))
    psS = p.psum("psS", [128, 256], F32)
    Ab = p.sbuf("Abu", [128, 256], BF16)
    kbb = p.sbuf("kbb", [128, 512], BF16)
    psO = [p.psum(f"psO{i}", [64, 128], F32) for i in range(2)]
    psKV = [p.psum(f"psKV{i}", [64, 64], F32) for i in range(2)]
    ret = [p.sbuf(f"ret{j}", [128, TFULL], BF16) for j in range(4)]
    for c in range(32):
        p.t_op("vector", "tensor_mul", kbb[:], ktm[c][:], xipt[:],
               reads=(f"ktm{c}", "xipt"), writes=("kbb",))
        for pr in range(4):
            p.t_op("gpsimd", "tensor_mul", qx[:],
                   qt[pr][:, c * 128:(c + 1) * 128], xit[pr][:],
                   reads=(f"qt{pr}", f"xit{pr}"), writes=("qx",))
            for hi, (base, off) in enumerate(((0, 0), (64, 128))):
                hh = 2 * pr + hi
                p.mm(psS[:, off:off + 128],
                     kdm[pr][base:base + 64, c * 128:(c + 1) * 128],
                     qt[pr][base:base + 64, c * 128:(c + 1) * 128],
                     start=True, stop=True,
                     reads=(f"kdm{pr}", f"qt{pr}"), writes=("psS",))
            p.t_op("vector", "tensor_tensor", Ab[:], psS[:, :], dmt[pr][:],
                   ALU.mult, reads=("psS", f"dmt{pr}"), writes=("Abu",))
            for hi, (base, off) in enumerate(((0, 0), (64, 128))):
                hh = 2 * pr + hi
                po = psO[hi]
                p.mm(po[:, :], vtm[c][:, hh * 64:(hh + 1) * 64],
                     Ab[:, off:off + 128], start=True, stop=False,
                     reads=(f"vtm{c}", "Abu"), writes=(f"psO{hi}",),
                     group=f"o{c}_{pr}_{hi}")
                p.mm(po[:, :], Sb[pr][base:base + 64, :],
                     qx[base:base + 64, :],
                     start=False, stop=True,
                     reads=(f"Sb{pr}", "qx"), writes=(f"psO{hi}",),
                     group=f"o{c}_{pr}_{hi}")
                p.t_op("vector", "tensor_copy",
                       ret[pr][base:base + 64, c * 128:(c + 1) * 128],
                       po[:, :], reads=(f"psO{hi}",), writes=(f"ret{pr}",))
                pk = psKV[hi]
                p.mm(pk[:, :], kbb[:, hh * 64:(hh + 1) * 64],
                     vtm[c][:, hh * 64:(hh + 1) * 64], start=True, stop=True,
                     reads=("kbb", f"vtm{c}"), writes=(f"psKV{hi}",))
            p.t_op("vector", "tensor_scalar", S[pr][:], S[pr][:], gct[pr][:],
                   None, ALU.mult, reads=(f"S{pr}", f"gct{pr}"),
                   writes=(f"S{pr}",))
            for hi, base in ((0, 0), (1, 64)):
                p.t_op("vector", "tensor_tensor", S[pr][base:base + 64, :],
                       S[pr][base:base + 64, :], psKV[hi][:, :], ALU.add,
                       reads=(f"S{pr}", f"psKV{hi}"), writes=(f"S{pr}",))
            p.t_op("vector", "tensor_copy", Sb[pr][:], S[pr][:],
                   reads=(f"S{pr}",), writes=(f"Sb{pr}",))
    for j in range(4):
        p.dma("gpsimd", r_o[j * 128:(j + 1) * 128, :], ret[j][:],
              reads=(f"ret{j}",))
    p.finish()
    return p.finalize()


def host_l2ab_b(state):
    maps = []
    for c in range(8):
        b, hf = c // 2, c % 2
        hsl = slice(hf * 512, (hf + 1) * 512)
        heads = range(hf * 8, (hf + 1) * 8)
        gam = 1.0 - 2.0 ** (-5.0 - np.arange(16))
        dm = np.zeros((128, 4 * 256), np.float32)
        xi = np.zeros((512, 128), np.float32)
        xip = np.zeros((128, 512), np.float32)
        gc = np.zeros((512, 1), np.float32)
        ml = np.arange(128)
        for i, h in enumerate(heads):
            g = gam[h]
            pr, hi = i // 2, i % 2
            mask = np.where(ml[None, :] >= ml[:, None],
                            0.125 * g ** (ml[None, :] - ml[:, None]), 0.0)
            dm[:, pr * 256 + hi * 128: pr * 256 + (hi + 1) * 128] = mask
            xi[i * 64:(i + 1) * 64, :] = 0.125 * (g ** (ml + 1.0))[None, :]
            xip[:, i * 64:(i + 1) * 64] = (g ** (127.0 - ml))[:, None]
            gc[i * 64:(i + 1) * 64, 0] = g ** 128.0
        maps.append({
            "q": np.ascontiguousarray(state["l2ab_q"][b][hsl]),
            "kd": np.ascontiguousarray(state["l2ab_k"][b][hsl]),
            "kt": np.ascontiguousarray(state["l2ab_k"][b].T[:, hsl]),
            "vt": np.ascontiguousarray(state["l2ab_v"][b][:, hsl]),
            "dmask": dm.astype(BF), "xi": xi.astype(BF),
            "xip": xip.astype(BF), "gc": gc,
        })
    return maps


def post_l2ab_b(state, results):
    full = np.zeros((NB, D, TFULL), BF)
    for c in range(8):
        b, hf = c // 2, c % 2
        full[b, hf * 512:(hf + 1) * 512, :] = results[c]["ro"]
    state["l2ab_ret"] = full


def build_l2ab_c():
    p = Prog()
    com = Common(p, npool=3)
    TQ = 512
    HID = 2730
    MT = (HID + 127) // 128
    x_d = p.dram("x", [D, TH], F32)
    r_d = p.dram("ret", [D, TH], BF16)
    g_d = p.dram("gate", [D, TH], BF16)
    wo = load_w(p, "wo", D, D)
    bo = load_rows(p, "bo", D)
    gnw = load_rows(p, "gnw", D)
    gnb = load_rows(p, "gnb", D)
    n2g = load_rows(p, "n2g", D)
    n2b = load_rows(p, "n2b", D)
    wu_d = p.dram("wu", [D, HID], BF16)
    wg2_d = p.dram("wg2", [D, HID], BF16)
    bu = load_rows(p, "bu", HID)
    bg2 = load_rows(p, "bg2", HID)
    wo2 = load_w(p, "wo2", HID, D)
    bo2 = load_rows(p, "bo2", D)
    ones2 = p.sbuf("ones2", [128, 2], BF16)
    i2 = p.sbuf("i2", [2, 128], F32)
    p.dma("sync", ones2[:], p.dram("ones2_d", [128, 2], BF16)[:, :], writes=("ones2",))
    p.dma("sync", i2[:], p.dram("i2_d", [2, 128], F32)[:, :], writes=("i2",))
    wus = [[p.sbuf(f"wus{i}_{k}", [128, 128], BF16) for k in range(KT)]
           for i in range(2)]
    wgs = [[p.sbuf(f"wgs{i}_{k}", [128, 128], BF16) for k in range(KT)]
           for i in range(2)]
    d_out = p.dram("xo", [D, TH], F32, kind="ExternalOutput")
    xt = [p.sbuf(f"xq{k}", [128, TQ], F32) for k in range(KT)]
    rt = [p.sbuf(f"rq{k}", [128, TQ], BF16) for k in range(KT)]
    gt = [p.sbuf(f"gq{k}", [128, TQ], BF16) for k in range(KT)]
    xkeys = [f"xq{k}" for k in range(KT)]
    sums = p.sbuf("gsum", [2, TQ], F32)
    sumq = p.sbuf("gsq", [2, TQ], F32)
    sqb = p.sbuf("sqb", [128, 512], BF16)
    ubuf = p.sbuf("ubuf", [128, TQ], BF16)
    gbuf = p.sbuf("gbuf", [128, TQ], BF16)
    mbuf = [p.sbuf(f"mbuf{j}", [128, TQ], BF16) for j in range(MT)]
    for q0 in range(0, TH, TQ):
        for k in range(KT):
            p.dma("sync", xt[k][:], x_d[k * 128:(k + 1) * 128, q0:q0 + TQ],
                  writes=(xkeys[k],))
            p.dma("sync", rt[k][:], r_d[k * 128:(k + 1) * 128, q0:q0 + TQ],
                  writes=(f"rq{k}",))
            p.dma("sync", gt[k][:], g_d[k * 128:(k + 1) * 128, q0:q0 + TQ],
                  writes=(f"gq{k}",))
        h00 = ln_d(p, "h0", xt, TQ, com, out_dtype=F32, x_keys=xkeys)
        for k in range(KT):
            p.t_op("vector", "tensor_add", xt[k][:], xt[k][:], h00[k][:],
                   reads=(xkeys[k], tkey(h00[k])), writes=(xkeys[k],))
        for k in range(KT):
            ps = com.ps_s if False else None
            pss = p.psum("gn_ps", [2, 512], F32)
            psq = p.psum("gn_psq", [2, 512], F32)
            p.mm(pss[:, :], ones2[:], rt[k][:], start=True, stop=True,
                 reads=("ones2", f"rq{k}"), writes=("gn_ps",))
            p.t_op("scalar", "activation", sqb[:], rt[k][:], AF.Square,
                   reads=(f"rq{k}",), writes=("sqb",))
            p.mm(psq[:, :], ones2[:], sqb[:], start=True, stop=True,
                 reads=("ones2", "sqb"), writes=("gn_psq",))
            p.t_op("vector", "tensor_scalar", sums[:], pss[:, :], 1.0 / 64, None,
                   ALU.mult, reads=("gn_ps",), writes=("gsum",))
            p.t_op("vector", "tensor_scalar", sumq[:], psq[:, :], 1.0 / 64, None,
                   ALU.mult, reads=("gn_psq",), writes=("gsq",))
            p.t_op("vector", "tensor_tensor", sumq[:], sumq[:], sums[:],
                   ALU.subtract, reads=("gsq", "gsum"), writes=("gsq",))                 if False else None
            tmp2 = p.sbuf("gtmp", [2, TQ], F32)
            p.t_op("scalar", "activation", tmp2[:], sums[:], AF.Square,
                   reads=("gsum",), writes=("gtmp",))
            p.t_op("vector", "tensor_tensor", sumq[:], sumq[:], tmp2[:],
                   ALU.subtract, reads=("gsq", "gtmp"), writes=("gsq",))
            epst, epsk = com.const(1e-5)
            p.t_op("scalar", "activation", tmp2[:], sumq[:], AF.Ln,
                   bias=epst[0:2, :], reads=("gsq", epsk), writes=("gtmp",))
            p.t_op("scalar", "activation", sumq[:], tmp2[:], AF.Exp, scale=-0.5,
                   reads=("gtmp",), writes=("gsq",))
            psb = com.pool.get()
            p.mm(psb[:, :], i2[:], sums[:], start=True, stop=True,
                 reads=("i2", "gsum"), writes=(tkey(psb),))
            mb2 = p.sbuf("gmb", [128, TQ], F32)
            p.t_op("vector", "tensor_copy", mb2[:], psb[:, :],
                   reads=(tkey(psb),), writes=("gmb",))
            psb2 = com.pool.get()
            p.mm(psb2[:, :], i2[:], sumq[:], start=True, stop=True,
                 reads=("i2", "gsq"), writes=(tkey(psb2),))
            rb2 = p.sbuf("grb", [128, TQ], F32)
            p.t_op("vector", "tensor_copy", rb2[:], psb2[:, :],
                   reads=(tkey(psb2),), writes=("grb",))
            p.t_op("vector", "tensor_sub", rt[k][:], rt[k][:], mb2[:],
                   reads=(f"rq{k}", "gmb"), writes=(f"rq{k}",))
            p.t_op("vector", "tensor_mul", rt[k][:], rt[k][:], rb2[:],
                   reads=(f"rq{k}", "grb"), writes=(f"rq{k}",))
            p.t_op("vector", "tensor_scalar", rt[k][:], rt[k][:], gnw[k][:],
                   gnb[k][:], ALU.mult, ALU.add, reads=(f"rq{k}", "gnw", "gnb"),
                   writes=(f"rq{k}",))
            p.t_op("vector", "tensor_mul", rt[k][:], rt[k][:], gt[k][:],
                   reads=(f"rq{k}", f"gq{k}"), writes=(f"rq{k}",))
        proj(p, "xs", rt, wo, D, TQ, com, out_aps=[t[:] for t in xt],
             out_keys=xkeys, bias=[t[:] for t in bo], bias_key="bo",
             accum_into=xt, accum_keys=xkeys)
        x2 = ln_d(p, "x2", xt, TQ, com, g=[t[:] for t in n2g],
                  b=[t[:] for t in n2b], gb_key="n2g", out_dtype=BF16,
                  x_keys=xkeys)
        for j in range(MT):
            mr = min(128, HID - j * 128)
            jb = j % 2
            for k in range(KT):
                p.dma("scalar", wus[jb][k][:, 0:mr],
                      wu_d[k * 128:(k + 1) * 128, j * 128:j * 128 + mr],
                      writes=(f"wus{jb}_{k}",))
                p.dma("sync", wgs[jb][k][:, 0:mr],
                      wg2_d[k * 128:(k + 1) * 128, j * 128:j * 128 + mr],
                      writes=(f"wgs{jb}_{k}",))
            psu = com.pool.get()
            psg = com.pool.get()
            for k in range(KT):
                p.mm(psu[0:mr, :], wus[jb][k][:, 0:mr],
                     x2[k][:], start=(k == 0), stop=(k == KT - 1),
                     reads=(f"wus{jb}_{k}", tkey(x2[k])), writes=(tkey(psu),),
                     group=f"u{q0}_{j}")
                p.mm(psg[0:mr, :], wgs[jb][k][:, 0:mr],
                     x2[k][:], start=(k == 0), stop=(k == KT - 1),
                     reads=(f"wgs{jb}_{k}", tkey(x2[k])), writes=(tkey(psg),),
                     group=f"g{q0}_{j}")
            p.t_op("scalar", "activation", ubuf[0:mr, :], psu[0:mr, :], AF.Silu,
                   bias=bu[j][0:mr, :], reads=(tkey(psu), "bu"), writes=("ubuf",))
            p.t_op("vector", "tensor_scalar", gbuf[0:mr, :], psg[0:mr, :],
                   bg2[j][0:mr, :], None, ALU.add, reads=(tkey(psg), "bg2"),
                   writes=("gbuf",))
            p.t_op("vector", "tensor_mul", mbuf[j][0:mr, :], ubuf[0:mr, :],
                   gbuf[0:mr, :], reads=("ubuf", "gbuf"), writes=(tkey(mbuf[j]),))
        proj(p, "fo", mbuf, wo2, D, TQ, com, out_aps=[t[:] for t in xt],
             out_keys=xkeys, bias=[t[:] for t in bo2], bias_key="bo2",
             accum_into=xt, accum_keys=xkeys)
        for k in range(KT):
            p.dma("gpsimd", d_out[k * 128:(k + 1) * 128, q0:q0 + TQ],
                  xt[k][:], reads=(xkeys[k],))
    p.finish()
    return p.finalize()


def host_l2ab_c(state):
    inp = state["inp"]
    ones2 = np.zeros((128, 2), np.float32)
    ones2[0:64, 0] = 1.0
    ones2[64:128, 1] = 1.0
    i2 = np.zeros((2, 128), np.float32)
    i2[0, 0:64] = 1.0
    i2[1, 64:128] = 1.0
    w_in = inp["L2_attn_b_w_in"]
    b_in = inp["L2_attn_b_b_in"]
    maps = []
    for c in range(8):
        b, hf = c // 2, c % 2
        maps.append({
            "x": xshard(state["x"], c),
            "ret": np.ascontiguousarray(state["l2ab_ret"][b][:, hf * TH:(hf + 1) * TH]),
            "gate": np.ascontiguousarray(state["l2ab_g"][b][:, hf * TH:(hf + 1) * TH]),
            "wo": as_bf(inp["L2_attn_b_wo"]),
            "bo": as_f32(inp["L2_attn_b_bo"].reshape(-1, 1)),
            "gnw": as_f32(inp["L2_attn_b_gn_w"].reshape(-1, 1)),
            "gnb": as_f32(inp["L2_attn_b_gn_b"].reshape(-1, 1)),
            "n2g": as_f32(inp["L2_attn_b_norm2_w"].reshape(-1, 1)),
            "n2b": as_f32(inp["L2_attn_b_norm2_b"].reshape(-1, 1)),
            "wu": as_bf(w_in[:, 0:2730]), "wg2": as_bf(w_in[:, 2730:5460]),
            "bu": as_f32(b_in[0:2730].reshape(-1, 1)),
            "bg2": as_f32(b_in[2730:5460].reshape(-1, 1)),
            "wo2": as_bf(inp["L2_attn_b_w_out"]),
            "bo2": as_f32(inp["L2_attn_b_b_out"].reshape(-1, 1)),
            "ones2_d": ones2.astype(BF), "i2_d": as_f32(i2),
        })
    return maps


def post_l2ab_c(state, results):
    state["x"] = unshard_x(results)


def build_l3ab_a():
    p = Prog()
    com = Common(p)
    TQ = 1024
    x_d = p.dram("x", [D, TH], BF16)
    wq = load_w(p, "wq", D, D)
    wk = load_w(p, "wk", D, D)
    wg = load_w(p, "wg", D, D)
    wv_d = p.dram("wv", [D, D], BF16)
    bgr = load_rows(p, "bg", D)
    q_o = p.dram("qo", [D, TH], BF16, kind="ExternalOutput")
    k_o = p.dram("ko", [D, TH], BF16, kind="ExternalOutput")
    v_o = p.dram("vo", [TH, D], BF16, kind="ExternalOutput")
    lg_o = p.dram("lgo", [D, TH], F32, kind="ExternalOutput")
    xt = [p.sbuf(f"xq{k}", [128, TQ], BF16) for k in range(KT)]
    wvb = [p.sbuf(f"wvb{k}", [128, D], BF16) for k in range(KT)]
    for k in range(KT):
        p.dma("scalar", wvb[k][:], wv_d[k * 128:(k + 1) * 128, :],
              writes=(f"wvb{k}",))
    vbuf = p.sbuf("vbuf", [128, 512], BF16)
    t1 = p.sbuf("t1g", [128, TQ], F32)
    t2 = p.sbuf("t2g", [128, TQ], F32)
    for h0 in (0, TQ):
        for k in range(KT):
            p.dma("sync", xt[k][:], x_d[k * 128:(k + 1) * 128, h0:h0 + TQ],
                  writes=(f"xq{k}",))
        xn = ln_d(p, "ln", xt, TQ, com, out_dtype=BF16, in_bf=True,
                  x_keys=[f"xq{k}" for k in range(KT)])
        for (w, nm, od) in ((wq, "q", q_o), (wk, "k", k_o)):
            ot = proj(p, f"p{nm}", xn, w, D, TQ, com)
            for m in range(KT):
                p.dma("gpsimd", od[m * 128:(m + 1) * 128, h0:h0 + TQ],
                      ot[m][:], reads=(tkey(ot[m]),))
        gp = proj(p, "pg", xn, wg, D, TQ, com, bias=[t[:] for t in bgr],
                  bias_key="bg", evac_func="bias_add", out_dtype=F32)
        one, onek = com.const(1.0)
        for m in range(KT):
            gk = tkey(gp[m])
            p.t_op("scalar", "activation", t1[:], gp[m][:], AF.Abs,
                   reads=(gk,), writes=("t1g",))
            p.t_op("scalar", "activation", t1[:], t1[:], AF.Exp, scale=-1.0,
                   reads=("t1g",), writes=("t1g",))
            p.t_op("scalar", "activation", t1[:], t1[:], AF.Ln, bias=one[:, 0:1],
                   reads=("t1g", onek), writes=("t1g",))
            p.t_op("vector", "tensor_scalar", t2[:], gp[m][:], 0.0, None,
                   ALU.min, reads=(gk,), writes=("t2g",))
            p.t_op("vector", "tensor_sub", t2[:], t2[:], t1[:],
                   reads=("t2g", "t1g"), writes=("t2g",))
            p.dma("gpsimd", lg_o[m * 128:(m + 1) * 128, h0:h0 + TQ], t2[:],
                  reads=("t2g",))
        for tt in range(TQ // 128):
            for c0 in range(0, D, 512):
                ps = com.pool.get()
                for k in range(KT):
                    p.mm(ps[:, :], xn[k][:, tt * 128:(tt + 1) * 128],
                         wvb[k][:, c0:c0 + 512], start=(k == 0),
                         stop=(k == KT - 1), reads=(tkey(xn[k]), f"wvb{k}"),
                         writes=(tkey(ps),), group=f"v{h0}_{tt}_{c0}")
                p.t_op("vector", "tensor_copy", vbuf[:], ps[:, :],
                       reads=(tkey(ps),), writes=("vbuf",))
                p.dma("gpsimd", v_o[h0 + tt * 128:h0 + (tt + 1) * 128,
                                    c0:c0 + 512], vbuf[:], reads=("vbuf",))
    p.finish()
    return p.finalize()


def host_l3ab_a(state):
    inp = state["inp"]
    maps = []
    for c in range(8):
        maps.append({
            "x": as_bf(xshard(state["x"], c)),
            "wq": as_bf(inp["L3_attn_b_Wq"]), "wk": as_bf(inp["L3_attn_b_Wk"]),
            "wv": as_bf(inp["L3_attn_b_Wv"]), "wg": as_bf(inp["L3_attn_b_Wg"]),
            "bg": as_f32(inp["L3_attn_b_bg"].reshape(-1, 1)),
        })
    return maps


def post_l3ab_a(state, results):
    for nm, key in (("qo", "l3ab_q"), ("ko", "l3ab_k")):
        full = np.zeros((NB, D, TFULL), BF)
        for c in range(8):
            b, h = c // 2, c % 2
            full[b, :, h * TH:(h + 1) * TH] = results[c][nm]
        state[key] = full
    full = np.zeros((NB, D, TFULL), np.float32)
    for c in range(8):
        b, h = c // 2, c % 2
        full[b, :, h * TH:(h + 1) * TH] = results[c]["lgo"]
    state["l3ab_lg"] = full
    fullv = np.zeros((NB, TFULL, D), BF)
    for c in range(8):
        b, h = c // 2, c % 2
        fullv[b, h * TH:(h + 1) * TH, :] = results[c]["vo"]
    state["l3ab_v"] = fullv


def build_l3ab_b():
    p = Prog()
    com = Common(p, npool=2)
    CW = 32
    NC_ = TFULL // CW
    q_d = p.dram("q", [512, TFULL], BF16)
    k_d = p.dram("k", [512, TFULL], BF16)
    lg_d = p.dram("lg", [512, TFULL], F32)
    v_d = p.dram("vch", [CW, NC_ * 512], BF16)
    mk_d = p.dram("shmask", [128, 5 * TFULL], BF16)
    tril_d = p.dram("trilT", [CW, CW], BF16)
    ident = load_ident(p)
    r_o = p.dram("ro", [512, TFULL], BF16, kind="ExternalOutput")
    qt = [p.sbuf(f"qt{j}", [128, TFULL], BF16) for j in range(4)]
    kt = [p.sbuf(f"kt{j}", [128, TFULL], BF16) for j in range(4)]
    lgb = p.sbuf("lgb", [128, TFULL], F32)
    for j in range(4):
        p.dma("sync", qt[j][:], q_d[j * 128:(j + 1) * 128, :], writes=(f"qt{j}",))
        p.dma("sync", kt[j][:], k_d[j * 128:(j + 1) * 128, :], writes=(f"kt{j}",))
    vg = [p.sbuf(f"vg{i}", [CW, 16 * 512], BF16) for i in range(2)]
    mskb = p.sbuf("mskb", [128, TFULL], BF16)
    trilT = p.sbuf("trilTs", [CW, CW], BF16)
    p.dma("sync", trilT[:], tril_d[:, :], writes=("trilTs",))
    tshb = p.sbuf("tshb", [128, TFULL], F32)
    gamt = [p.sbuf(f"gam{j}", [128, NC_], F32) for j in range(4)]
    enl = p.sbuf("enl", [128, TFULL], F32)
    for j in range(4):
        p.dma("scalar", lgb[:], lg_d[j * 128:(j + 1) * 128, :], writes=("lgb",))
        for si, s in enumerate((1, 2, 4, 8, 16)):
            p.dma("sync", mskb[:], mk_d[:, si * TFULL:(si + 1) * TFULL],
                  writes=("mskb",))
            p.t_op("vector", "tensor_mul", tshb[:, s:TFULL],
                   lgb[:, 0:TFULL - s], mskb[:, s:TFULL],
                   reads=("lgb", "mskb"), writes=("tshb",))
            p.t_op("vector", "tensor_tensor", lgb[:, s:TFULL],
                   lgb[:, s:TFULL], tshb[:, s:TFULL], ALU.add,
                   reads=("lgb", "tshb"), writes=("lgb",))
        p.t_op("vector", "tensor_scalar", lgb[:], lgb[:], -75.0, None,
               ALU.max, reads=("lgb",), writes=("lgb",))
        p.t_op("scalar", "activation", tshb[:], lgb[:], AF.Exp,
               reads=("lgb",), writes=("tshb",))
        p.t_op("vector", "tensor_mul", qt[j][:], qt[j][:], tshb[:],
               reads=(f"qt{j}", "tshb"), writes=(f"qt{j}",))
        p.t_op("vector", "tensor_copy", gamt[j][:],
               tshb[:, CW - 1:TFULL:CW], reads=("tshb",), writes=(f"gam{j}",))
        p.t_op("scalar", "activation", enl[:], lgb[:], AF.Exp, scale=-1.0,
               reads=("lgb",), writes=("enl",))
        p.t_op("vector", "tensor_mul", kt[j][:], kt[j][:], enl[:],
               reads=(f"kt{j}", "enl"), writes=(f"kt{j}",))
    S = [p.sbuf(f"S{pr}", [128, 64], F32) for pr in range(4)]
    Sb = [p.sbuf(f"Sb{pr}", [128, 64], BF16) for pr in range(4)]
    for pr in range(4):
        p.t_op("vector", "memset", S[pr][:], 0.0, writes=(f"S{pr}",))
        p.t_op("vector", "memset", Sb[pr][:], 0.0, writes=(f"Sb{pr}",))
    psS = p.psum("psS", [CW, 64], F32)
    Ab = p.sbuf("Abu", [CW, 64], BF16)
    psT = p.psum("psTg", [CW, 128], BF16)
    kbT = p.sbuf("kbT", [CW, 128], BF16)
    psO = [p.psum(f"psO{i}", [64, CW], F32) for i in range(2)]
    psKV = [p.psum(f"psKV{i}", [64, 64], F32) for i in range(2)]
    ret = [p.sbuf(f"ret{j}", [128, TFULL], BF16) for j in range(4)]
    for c in range(NC_):
        cs = slice(c * CW, (c + 1) * CW)
        gi = (c // 16) % 2
        if c % 16 == 0:
            p.dma("sync", vg[gi][:], v_d[:, c * 512:(c + 16) * 512],
                  writes=(f"vg{gi}",))
        vcol = (c % 16) * 512
        for pr in range(4):
            kbs = p.sbuf("kbs", [128, CW], BF16)
            p.t_op("gpsimd", "tensor_scalar", kbs[:], kt[pr][:, cs],
                   gamt[pr][:, c:c + 1], None, ALU.mult,
                   reads=(f"kt{pr}", f"gam{pr}"), writes=("kbs",))
            p.transp(psT[:, 0:128], kbs[:], ident[:],
                     reads=("kbs", "ident"), writes=("psTg",))
            p.t_op("vector", "tensor_copy", kbT[:], psT[:, 0:128],
                   reads=("psTg",), writes=("kbT",))
            for hi, base in ((0, 0), (1, 64)):
                hh = 2 * pr + hi
                p.mm(psS[:, hi * 32:(hi + 1) * 32],
                     kt[pr][base:base + 64, cs], qt[pr][base:base + 64, cs],
                     start=True, stop=True, reads=(f"kt{pr}", f"qt{pr}"),
                     writes=("psS",))
            p.t_op("vector", "tensor_tensor", Ab[:], psS[:, :],
                   trilT[:].rearrange("m t -> m t") if False else psS[:, :],
                   ALU.mult, reads=("psS",), writes=("Abu",)) \
                if False else None
            p.t_op("vector", "tensor_mul", Ab[:, 0:32], psS[:, 0:32],
                   trilT[:], reads=("psS", "trilTs"), writes=("Abu",))
            p.t_op("vector", "tensor_mul", Ab[:, 32:64], psS[:, 32:64],
                   trilT[:], reads=("psS", "trilTs"), writes=("Abu",))
            for hi, base in ((0, 0), (1, 64)):
                hh = 2 * pr + hi
                po = psO[hi]
                p.mm(po[:, :], vg[gi][:, vcol + hh * 64:vcol + (hh + 1) * 64],
                     Ab[:, hi * 32:(hi + 1) * 32], start=True, stop=False,
                     reads=(f"vg{gi}", "Abu"), writes=(f"psO{hi}",),
                     group=f"o{c}_{pr}_{hi}")
                p.mm(po[:, :], Sb[pr][base:base + 64, :],
                     qt[pr][base:base + 64, cs], start=False, stop=True,
                     reads=(f"Sb{pr}", f"qt{pr}"), writes=(f"psO{hi}",),
                     group=f"o{c}_{pr}_{hi}")
                p.t_op("vector", "tensor_copy", ret[pr][base:base + 64, cs],
                       po[:, :], reads=(f"psO{hi}",), writes=(f"ret{pr}",))
                pk = psKV[hi]
                p.mm(pk[:, :], kbT[:, base:base + 64],
                     vg[gi][:, vcol + hh * 64:vcol + (hh + 1) * 64],
                     start=True, stop=True, reads=("kbT", f"vg{gi}"),
                     writes=(f"psKV{hi}",))
            p.t_op("vector", "tensor_scalar", S[pr][:], S[pr][:],
                   gamt[pr][:, c:c + 1], None, ALU.mult,
                   reads=(f"S{pr}", f"gam{pr}"), writes=(f"S{pr}",))
            for hi, base in ((0, 0), (1, 64)):
                p.t_op("vector", "tensor_tensor", S[pr][base:base + 64, :],
                       S[pr][base:base + 64, :], psKV[hi][:, :], ALU.add,
                       reads=(f"S{pr}", f"psKV{hi}"), writes=(f"S{pr}",))
            p.t_op("vector", "tensor_copy", Sb[pr][:], S[pr][:],
                   reads=(f"S{pr}",), writes=(f"Sb{pr}",))
    for j in range(4):
        p.dma("gpsimd", r_o[j * 128:(j + 1) * 128, :], ret[j][:],
              reads=(f"ret{j}",))
    p.finish()
    return p.finalize()


def host_l3ab_b(state):
    maps = []
    CW = 32
    shmask = np.zeros((128, 5 * TFULL), np.float32)
    for si, s in enumerate((1, 2, 4, 8, 16)):
        m = ((np.arange(TFULL) % CW) >= s).astype(np.float32)
        shmask[:, si * TFULL:(si + 1) * TFULL] = m[None, :]
    tril = np.tril(np.ones((CW, CW), np.float32)).T
    for c in range(8):
        b, hf = c // 2, c % 2
        hsl = slice(hf * 512, (hf + 1) * 512)
        v = state["l3ab_v"][b][:, hsl].astype(BF)
        vch = np.ascontiguousarray(
            v.reshape(TFULL // CW, CW, 512).transpose(1, 0, 2).reshape(CW, -1))
        maps.append({
            "q": np.ascontiguousarray(state["l3ab_q"][b][hsl]),
            "k": np.ascontiguousarray(state["l3ab_k"][b][hsl]),
            "lg": as_f32(state["l3ab_lg"][b][hsl]),
            "vch": vch,
            "shmask": shmask.astype(BF),
            "trilT": tril.astype(BF),
            "ident": get_ident(),
        })
    return maps


def post_l3ab_b(state, results):
    full = np.zeros((NB, D, TFULL), BF)
    for c in range(8):
        b, hf = c // 2, c % 2
        full[b, hf * 512:(hf + 1) * 512, :] = results[c]["ro"]
    state["l3ab_ret"] = full


def build_l3aa_a():
    p = Prog()
    com = Common(p)
    TQ = 512
    x_d = p.dram("x", [D, TH], BF16)
    win = load_w(p, "win", D, D)
    bin_ = load_rows(p, "bin", D)
    wgt = load_w(p, "wgt", D, D)
    bgt = load_rows(p, "bgt", D)
    wB = load_w(p, "wB", D, D)
    wA = load_w(p, "wA", D, D)
    g_o = p.dram("go", [D, TH], BF16, kind="ExternalOutput")
    u2_o = p.dram("u2o", [D, TH], F32, kind="ExternalOutput")
    xt = [p.sbuf(f"xq{k}", [128, TQ], BF16) for k in range(KT)]
    for h0 in (0, 512, 1024, 1536):
        for k in range(KT):
            p.dma("sync", xt[k][:], x_d[k * 128:(k + 1) * 128, h0:h0 + TQ],
                  writes=(f"xq{k}",))
        xn = ln_d(p, "ln", xt, TQ, com, out_dtype=BF16, in_bf=True,
                  x_keys=[f"xq{k}" for k in range(KT)])
        inp = proj(p, "inp", xn, win, D, TQ, com, bias=[t[:] for t in bin_],
                   bias_key="bin", evac_func="bias_add", out_dtype=F32)
        inb = [p.sbuf(f"inb{m}", [128, TQ], BF16) for m in range(KT)]
        for m in range(KT):
            p.t_op("vector", "tensor_copy", inb[m][:], inp[m][:],
                   reads=(tkey(inp[m]),), writes=(tkey(inb[m]),))
        gt = proj(p, "g", inb, wgt, D, TQ, com, bias=[t[:] for t in bgt],
                  bias_key="bgt", evac_func=AF.Sigmoid, out_dtype=BF16)
        for m in range(KT):
            p.dma("gpsimd", g_o[m * 128:(m + 1) * 128, h0:h0 + TQ], gt[m][:],
                  reads=(tkey(gt[m]),))
        cc = [p.sbuf(f"cc{m}", [128, TQ], BF16) for m in range(KT)]
        for m in range(KT):
            e = ["vector", "gpsimd"][m % 2]
            p.t_op(e, "tensor_scalar", cc[m][:], gt[m][:], -1.0, 1.0,
                   ALU.mult, ALU.add, reads=(tkey(gt[m]),), writes=(tkey(cc[m]),))
            p.t_op(e, "tensor_mul", cc[m][:], cc[m][:], inb[m][:],
                   reads=(tkey(cc[m]), tkey(inb[m])), writes=(tkey(cc[m]),))
        u2 = proj(p, "u2", inb, wB, D, TQ, com, out_dtype=F32)
        proj(p, "u2b", cc, wA, D, TQ, com,
             out_aps=[t[:] for t in u2], out_keys=keys(u2),
             accum_into=u2, accum_keys=keys(u2))
        for m in range(KT):
            p.dma("gpsimd", u2_o[m * 128:(m + 1) * 128, h0:h0 + TQ], u2[m][:],
                  reads=(tkey(u2[m]),))
    p.finish()
    return p.finalize()


def host_l3aa_a(state):
    inp = state["inp"]
    maps = []
    for c in range(8):
        maps.append({
            "x": as_bf(xshard(state["x"], c)),
            "win": as_bf(inp["L3_attn_a_W_in"].T),
            "bin": as_f32(inp["L3_attn_a_b_in"].reshape(-1, 1)),
            "wgt": as_bf(inp["L3_attn_a_W_gate"].T),
            "bgt": as_f32(inp["L3_attn_a_b_gate"].reshape(-1, 1)),
            "wB": as_bf(inp["L3_attn_a_B"].T),
            "wA": as_bf(inp["L3_attn_a_A"].T),
        })
    return maps


def post_l3aa_a(state, results):
    for nm, key, dt_ in (("go", "l3aa_g", BF), ("u2o", "l3aa_u2", np.float32)):
        full = np.zeros((NB, D, TFULL), dt_)
        for c in range(8):
            b, h = c // 2, c % 2
            full[b, :, h * TH:(h + 1) * TH] = results[c][nm]
        state[key] = full


def build_l3aa_b(nsteps=1024):
    p = Prog()
    W = 8
    g_d = p.dram("gL", [128, nsteps * W], BF16)
    u_d = p.dram("u2L", [128, nsteps * W], BF16)
    a_d = p.dram("at", [D, D], BF16)
    s0_d = p.dram("s0", [128, W], F32)
    so_d = p.dram("so", [128, W], F32, kind="ExternalOutput")
    st_d = p.dram("states", [128, nsteps * W], BF16, kind="ExternalOutput")
    at = [p.sbuf(f"at{k}", [128, D], BF16) for k in range(KT)]
    for k in range(KT):
        p.dma("sync", at[k][:], a_d[k * 128:(k + 1) * 128, :], writes=(f"at{k}",))
    CH = min(128, nsteps)
    gb = [p.sbuf(f"gb{i}", [128, CH * W], BF16) for i in range(2)]
    ub = [p.sbuf(f"ub{i}", [128, CH * W], BF16) for i in range(2)]
    sb = [p.sbuf(f"sb{i}", [128, CH * W], BF16) for i in range(2)]
    sig = p.sbuf("sig", [128, W], F32)
    p.dma("sync", sig[:], s0_d[:, :], writes=("sig",))
    blend = p.sbuf("blend", [128, W], BF16)
    zbuf = p.sbuf("zbuf", [128, W], F32)
    zps = [p.psum(f"zps{i}", [128, W], F32) for i in range(2)]
    for c0 in range(0, nsteps, CH):
        ci = (c0 // CH) % 2
        p.dma("sync", gb[ci][:], g_d[:, c0 * W:(c0 + CH) * W],
              writes=(f"gb{ci}",))
        p.dma("scalar", ub[ci][:], u_d[:, c0 * W:(c0 + CH) * W],
              writes=(f"ub{ci}",))
        for tl in range(CH):
            off = tl * W
            zp = zps[tl % 2]
            p.t_op("vector", "tensor_tensor", blend[:],
                   gb[ci][:, off:off + W], sig[:], ALU.mult,
                   reads=(f"gb{ci}", "sig"), writes=("blend",))
            for m in range(KT):
                for k in range(KT):
                    p.mm(zp[:, m:m + 1],
                         at[k][:, m * 128:(m + 1) * 128],
                         blend[:, k:k + 1],
                         start=(k == 0), stop=(k == KT - 1),
                         reads=(f"at{k}", "blend"), writes=(tkey(zp),),
                         group=f"z{c0}_{tl}_{m}")
            p.t_op("vector", "tensor_tensor", zbuf[:], zp[:, :],
                   ub[ci][:, off:off + W], ALU.add,
                   reads=(tkey(zp), f"ub{ci}"), writes=("zbuf",))
            p.t_op("scalar", "activation", sig[:], zbuf[:], AF.Tanh,
                   reads=("zbuf",), writes=("sig",))
            p.t_op("gpsimd", "tensor_copy", sb[ci][:, off:off + W], sig[:],
                   reads=("sig",), writes=(f"sb{ci}",))
        p.dma("gpsimd", st_d[:, c0 * W:(c0 + CH) * W], sb[ci][:],
              reads=(f"sb{ci}",))
    p.dma("gpsimd", so_d[:, :], sig[:], reads=("sig",))
    p.finish()
    return p.finalize()


def host_l3aa_b(state, seg, nsteps=1024):
    g = state["l3aa_g"]
    u2 = state["l3aa_u2"]
    t0 = seg * nsteps
    atw = as_bf(state["inp"]["L3_attn_a_A"].T)
    maps = []
    for c in range(8):
        b = c % 4
        gs = np.ascontiguousarray(g[b, :, t0:t0 + nsteps]).astype(np.float32)
        us = np.ascontiguousarray(u2[b, :, t0:t0 + nsteps])
        def lay(a):
            a2 = a.reshape(8, 128, nsteps)
            a2 = a2.transpose(1, 2, 0)
            return np.ascontiguousarray(a2.reshape(128, nsteps * 8))
        s0 = state.get("l3aa_s0")
        if s0 is None:
            s0z = np.zeros((128, 8), np.float32)
        else:
            s0z = s0[c]
        maps.append({"gL": lay(gs).astype(BF), "u2L": lay(us).astype(BF),
                     "at": atw, "s0": s0z})
    return maps


def post_l3aa_b(state, results, seg, nsteps=1024):
    state["l3aa_s0"] = [results[c]["so"] for c in range(8)]
    if "l3aa_states" not in state:
        state["l3aa_states"] = np.zeros((4, 1024, TFULL), np.float32)
    t0 = seg * nsteps
    for b in range(4):
        st = results[b]["states"].astype(np.float32)
        a2 = st.reshape(128, nsteps, 8)
        a2 = a2.transpose(2, 0, 1)
        state["l3aa_states"][b, :, t0:t0 + nsteps] = a2.reshape(1024, nsteps)


def build_l2f():
    p = Prog()
    com = Common(p, npool=8)
    TQ = 2048
    HI = 16384
    x_d = p.dram("x", [D, TH], F32)
    w1_d = p.dram("w1", [128, 32 * 128], BF16)
    w2_d = p.dram("w2", [128, 32 * 512], BF16)
    w_d = p.dram("w", [HI, D], BF16)
    h_d = p.dram("h", [HI, TH], BF16, kind="Internal")
    xbf = [p.sbuf(f"xbf{k}", [128, TQ], BF16) for k in range(KT)]
    for k in range(KT):
        p.dma("gpsimd", xbf[k][:], x_d[k * 128:(k + 1) * 128, :],
              writes=(f"xbf{k}",))
    xn = ln_d(p, "ln", xbf, TQ, com, out_dtype=BF16, in_bf=True,
              x_keys=[f"xbf{k}" for k in range(KT)])
    w1b = p.sbuf("w1b", [128, 128], BF16)
    w2b = p.sbuf("w2b", [128, 512], BF16)
    bd = [p.sbuf(f"bd{i}", [128, 512], BF16) for i in range(32)]
    for rg in range(8):
        for h in range(4):
            i = rg * 4 + h
            p.dma("sync", w1b[:], w1_d[:, i * 128:(i + 1) * 128], writes=("w1b",))
            p.dma("sync", w2b[:], w2_d[:, i * 512:(i + 1) * 512], writes=("w2b",))
            ps = com.pool.get()
            p.mm(ps[:, :], w1b[:], w2b[:], start=True, stop=True,
                 reads=("w1b", "w2b"), writes=(tkey(ps),))
            p.t_op("vector", "tensor_copy", bd[i][:], ps[:, :],
                   reads=(tkey(ps),), writes=(f"bd{i}",))
    hbuf = p.sbuf("hbuf", [128, 512], BF16)
    for rg in range(8):
        for h in range(4):
            i = rg * 4 + h
            for m in range(4):
                for c0 in range(0, TQ, 512):
                    ps = com.pool.get()
                    p.mm(ps[:, :], bd[i][:, m * 128:(m + 1) * 128],
                         xn[rg][:, c0:c0 + 512], start=True, stop=True,
                         reads=(f"bd{i}", tkey(xn[rg])), writes=(tkey(ps),))
                    p.t_op("scalar", "activation", hbuf[:], ps[:, :], AF.Gelu,
                           reads=(tkey(ps),), writes=("hbuf",))
                    row = rg * 2048 + h * 512 + m * 128
                    p.dma("gpsimd", h_d[row:row + 128, c0:c0 + 512], hbuf[:],
                          reads=("hbuf",))
    wb = [p.sbuf(f"wb{i}", [128, D], BF16) for i in range(2)]
    hb = [p.sbuf(f"hb{i}", [128, 512], BF16) for i in range(2)]
    xr = [p.sbuf(f"xr{i}", [128, 512], F32) for i in range(2)]
    d_out = p.dram("xo", [D, TH], F32, kind="ExternalOutput")
    for c0 in range(0, TH, 512):
        pss = [com.pool.get() for _ in range(KT)]
        for j in range(HI // 128):
            jb = j % 2
            p.dma("sync", wb[jb][:], w_d[j * 128:(j + 1) * 128, :],
                  writes=(f"wb{jb}",))
            p.dma("scalar", hb[jb][:], h_d[j * 128:(j + 1) * 128, c0:c0 + 512],
                  writes=(f"hb{jb}",))
            for m in range(KT):
                p.mm(pss[m][:, :], wb[jb][:, m * 128:(m + 1) * 128], hb[jb][:],
                     start=(j == 0), stop=(j == HI // 128 - 1),
                     reads=(f"wb{jb}", f"hb{jb}"), writes=(tkey(pss[m]),),
                     group=f"cp{c0}_{m}")
        for m in range(KT):
            xm = xr[m % 2]
            p.dma("sync", xm[:], x_d[m * 128:(m + 1) * 128, c0:c0 + 512],
                  writes=(f"xr{m % 2}",))
            p.t_op("vector", "tensor_tensor", xm[:], xm[:], pss[m][:, :],
                   ALU.add, reads=(f"xr{m % 2}", tkey(pss[m])),
                   writes=(f"xr{m % 2}",))
            p.dma("gpsimd", d_out[m * 128:(m + 1) * 128, c0:c0 + 512],
                  xm[:], reads=(f"xr{m % 2}",))
    p.finish()
    return p.finalize()


def host_l2f(state):
    inp = state["inp"]
    if "l2f_W1" not in state:
        K_ = inp["L2_ffn_rail_maker"]
        U = inp["L2_ffn_rail_up_proj"]
        W1 = np.zeros((128, 32 * 128), np.float32)
        W2 = np.zeros((128, 32 * 512), np.float32)
        for rg in range(8):
            for h in range(4):
                i = rg * 4 + h
                for rl in range(8):
                    r = rg * 8 + rl
                    W1[rl * 16:(rl + 1) * 16,
                       i * 128 + rl * 16:i * 128 + (rl + 1) * 16] = K_[h, r].T
                    W2[rl * 16:(rl + 1) * 16,
                       i * 512 + rl * 64:i * 512 + (rl + 1) * 64] = U[h, r].T
        state["l2f_W1"] = W1.astype(BF)
        state["l2f_W2"] = W2.astype(BF)
        W = inp["L2_ffn_w_cproj"]
        perm = np.zeros(16384, np.int64)
        pos = 0
        for rg in range(8):
            for h in range(4):
                for rl in range(8):
                    r = rg * 8 + rl
                    perm[pos:pos + 64] = h * 4096 + r * 64 + np.arange(64)
                    pos += 64
        state["l2f_WT"] = np.ascontiguousarray(W[:, perm].T.astype(BF))
    return [{"x": xshard(state["x"], c), "w1": state["l2f_W1"],
             "w2": state["l2f_W2"], "w": state["l2f_WT"]} for c in range(8)]


def post_l2f(state, results):
    state["x"] = unshard_x(results)



_NC_CACHE = {}


def _get_nc(name, builder, *args):
    if name not in _NC_CACHE:
        _NC_CACHE[name] = builder(*args)
    return _NC_CACHE[name]


def _run(name, builder, host_fn, post_fn, state, *bargs):
    nc = _get_nc(name, builder, *bargs)
    maps = host_fn(state)
    res = run_bass_kernel_spmd(nc, maps, core_ids=list(range(8)))
    if post_fn is not None:
        post_fn(state, res.results)
    return res.results


def kernel(**inputs):
    state = {"inp": inputs, "x": np.asarray(inputs["L0_attn_a_x"],
                                            dtype=np.float32)}
    idx = np.arange(1024)
    Fm = np.exp(-2j * np.pi * np.outer(idx, idx) / 1024)
    state["consts"] = {"F": Fm, "G": np.conj(Fm) / 1024}

    _run("l0a_proj", build_l0a_proj, host_l0a_proj, post_l0a_proj, state)
    _run("l0a_wkv", build_l0a_wkv, host_l0a_wkv, post_l0a_wkv, state)
    _run("l0a_cm1", build_l0a_cm1, host_l0a_cm1, post_l0a_cm1, state)
    _run("l0a_cm2", build_l0a_cm2, host_l0a_cm2, post_l0a_cm2, state)
    _run("l0f", build_l0f, host_l0f, post_l0f, state)
    _run("l0ab_a", build_l0ab_a, host_l0ab_a, post_l0ab_a, state)
    _run("l0ab_b", build_l0ab_b, host_l0ab_b, post_l0ab_b, state)
    _run("l0f", build_l0f, host_l0f, post_l0f, state)

    _run("l1aa_a", build_l1aa_a, host_l1aa_a, post_l1aa_a, state)
    _run("l1aa_b", build_l1aa_b, host_l1aa_b, post_l1aa_b, state)
    res = _run("addres", build_addres, lambda s: host_addres(s, "l1aa_y"),
               None, state)
    state["x"] = unshard_x(res)
    _run("l1f_m", build_l1f_m, host_l1f_m, post_l1f_m, state)
    _run("l1f", build_l1f, host_l1f, post_l1f, state)
    _run("l1ab_a", build_l1ab_a, host_l1ab_a, post_l1ab_a, state)
    _run("l1ab_b", build_l1ab_b, host_l1ab_b, post_l1ab_b, state)
    _run("l1f", build_l1f, host_l1f, post_l1f, state)

    _run("l2aa_kv", build_l2aa_kv, host_l2aa_kv, post_l2aa_kv, state)
    _run("l2aa_at", build_l2aa_at, host_l2aa_at, post_l2aa_at, state)
    _run("l2f", build_l2f, host_l2f, post_l2f, state)
    _run("l2ab_a", build_l2ab_a, host_l2ab_a, post_l2ab_a, state)
    _run("l2ab_b", build_l2ab_b, host_l2ab_b, post_l2ab_b, state)
    _run("l2ab_c", build_l2ab_c, host_l2ab_c, post_l2ab_c, state)
    _run("l2f", build_l2f, host_l2f, post_l2f, state)

    _run("l3aa_a", build_l3aa_a, host_l3aa_a, post_l3aa_a, state)
    state.pop("l3aa_s0", None)
    nc = _get_nc("l3aa_b", build_l3aa_b)
    for seg in range(4):
        maps = host_l3aa_b(state, seg)
        res = run_bass_kernel_spmd(nc, maps, core_ids=list(range(8)))
        post_l3aa_b(state, res.results, seg)
    state["l3aa_y"] = [None]
    stb = state["l3aa_states"]
    state["l3aa_yst"] = stb
    nc = _get_nc("resproj_b", build_resproj, 1024, True)
    maps = host_resproj(state, "l3aa_yst", "L3_attn_a_W_out",
                        bkey="L3_attn_a_b_out", wT=True)
    res = run_bass_kernel_spmd(nc, maps, core_ids=list(range(8)))
    state["x"] = unshard_x(res.results)

    _run("l3f_w", build_l3f_w, host_l3f_w, post_l3f_w, state)
    _run("l3f_a", build_l3f_a, host_l3f_a, post_l3f_a, state)
    _run("l3f_b", build_l3f_b, lambda s: host_l3f_b(s, 0), post_l3f_b, state)
    _run("l3f_c", build_l3f_c, host_l3f_c, post_l3f_c, state)

    _run("l3ab_a", build_l3ab_a, host_l3ab_a, post_l3ab_a, state)
    _run("l3ab_b", build_l3ab_b, host_l3ab_b, post_l3ab_b, state)
    nc = _get_nc("resproj_nb", build_resproj, 1024, False)
    maps = host_resproj(state, "l3ab_ret", "L3_attn_b_Wo", wT=False)
    res = run_bass_kernel_spmd(nc, maps, core_ids=list(range(8)))
    state["x"] = unshard_x(res.results)

    _run("l3f_a", build_l3f_a, host_l3f_a, post_l3f_a, state)
    _run("l3f_b", build_l3f_b, lambda s: host_l3f_b(s, 1), post_l3f_b, state)
    _run("l3f_c", build_l3f_c, host_l3f_c, post_l3f_c, state)

    return np.ascontiguousarray(state["x"].astype(np.float32))
